# version 25
# baseline (speedup 1.0000x reference)
import contextlib
import numpy as np
import ml_dtypes
import concourse.bass as bass
import concourse.mybir as mybir
from concourse.bass_utils import run_bass_kernel_spmd
from concourse.aot_env import allow_ici_all_to_all

F32 = mybir.dt.float32
BF16 = mybir.dt.bfloat16
FP8 = mybir.dt.float8e4
AF = mybir.ActivationFunctionType
ALU = mybir.AluOpType
AX = mybir.AxisListType

NCORES = 8
D = 1024
SEQ = 4096
HEADS = 16
HD = 64
DHID = 4096
LANES = SEQ // NCORES
EPS = 1e-5

ENGINES = ("sync", "gpsimd", "tensor", "vector", "scalar")



def host_consts():
    c = {}
    inv = (1.0 / (10000.0 ** (np.arange(0, HD, 2, dtype=np.float32) / np.float32(HD)))).astype(np.float32)
    pos = np.arange(SEQ, dtype=np.float32)
    ang = pos[:, None] * inv[None, :]
    cos = np.cos(ang).astype(np.float32)
    sin = np.sin(ang).astype(np.float32)
    cos64 = np.concatenate([cos, cos], axis=1).T
    sin64 = np.concatenate([sin, sin], axis=1).T
    c["cosd"] = np.ascontiguousarray(np.vstack([cos64, cos64])).astype(ml_dtypes.bfloat16)
    c["sind"] = np.ascontiguousarray(np.vstack([sin64, sin64])).astype(ml_dtypes.bfloat16)
    P = np.zeros((128, 128), dtype=np.float32)
    for b in (0, 64):
        for m in range(32):
            P[b + m + 32, b + m] = -1.0
            P[b + m, b + m + 32] = 1.0
    c["pshift"] = P.astype(ml_dtypes.bfloat16)
    tri = (np.arange(128)[None, :] >= np.arange(128)[:, None]).astype(np.float32)
    c["tri"] = tri
    c["id_bf"] = np.eye(128, dtype=np.float32).astype(ml_dtypes.bfloat16)
    c["id_f32"] = np.eye(128, dtype=np.float32)
    c["ones_row"] = np.ones((1, 128), dtype=np.float32)
    c["ones_col_bf"] = np.ones((128, 1), dtype=np.float32).astype(ml_dtypes.bfloat16)
    fc = np.zeros((128, 4), dtype=np.float32)
    fc[:, 0] = 1e-5
    fc[:, 1] = 1e-6
    fc[:, 2] = 0.5
    fc[:, 3] = -0.5
    c["fconsts"] = fc
    selv = np.zeros((8, 128, 128), dtype=np.float32)
    selq = np.zeros((8, 128, 128), dtype=np.float32)
    negblk = np.zeros((128, 128), dtype=np.float32)
    for pp in range(128):
        for m in range(128):
            if pp // 8 == m // 8:
                negblk[pp, m] = -1.0
                selq[m % 8, pp, m] = 1.0
    for i in range(8):
        for m in range(128):
            selv[i, (m // 8) * 8 + i, m] = 1.0
    c["selv"] = selv.astype(ml_dtypes.bfloat16)
    c["selq"] = selq.astype(ml_dtypes.bfloat16)
    c["negblk"] = negblk.astype(ml_dtypes.bfloat16)
    c["posblk"] = (-negblk).astype(np.float32)
    sel4 = np.zeros((4, 4, 128), dtype=np.float32)
    for i in range(4):
        sel4[i, i, :] = 1.0
    c["sel4"] = sel4
    return c


def host_weights(inputs):
    w = {}
    t = lambda a: np.ascontiguousarray(np.asarray(a, dtype=np.float32).T)
    for n in ("L0_attn_Wq", "L0_attn_Wk", "L0_attn_Wv", "L0_attn_Wo", "L0_attn_Wg1", "L0_attn_Wg2"):
        w[n] = np.ascontiguousarray(np.asarray(inputs[n], dtype=np.float32))
    def pack(a):
        K, M = a.shape
        return np.ascontiguousarray(a.reshape(K // 128, 128, M // 128, 128).transpose(2, 0, 1, 3))
    w["w1P"] = pack(t(inputs["L0_ffn_w1_w"]))
    w["w2P"] = pack(t(inputs["L0_ffn_w2_w"]))
    w["w12P"] = np.ascontiguousarray(np.concatenate(
        [w["w1P"], w["w2P"]], axis=1))
    w["h1P"] = pack(t(inputs["L0_ffn_h1_w"]))
    w["h2P"] = pack(t(inputs["L0_ffn_h2_w"]))
    w["w3P"] = pack(t(inputs["L0_ffn_w3_w"]))
    pc = lambda a: np.ascontiguousarray(np.asarray(a, dtype=np.float32).reshape(-1, 128).T)
    w["s1"] = pc(inputs["L0_ffn_w1_s"])
    w["s2"] = pc(inputs["L0_ffn_w2_s"])
    w["hs1"] = pc(inputs["L0_ffn_h1_s"])
    w["hs2"] = pc(inputs["L0_ffn_h2_s"])
    w["s3"] = pc(inputs["L0_ffn_w3_s"])
    w["lnw"] = pc(inputs["L0_attn_ln_w"])
    w["lnb"] = pc(inputs["L0_attn_ln_b"])
    w["wkT"] = t(inputs["L1_attn_W_k"])
    w["wvT"] = t(inputs["L1_attn_W_v"])
    w["wqT"] = t(inputs["L1_attn_W_q"])
    w["wfT"] = t(inputs["L1_ffn_weight"])
    w["bias"] = pc(inputs["L1_ffn_bias"])
    return w



class Sem:
    __slots__ = ("h", "n", "name")

    def __init__(self, h, name):
        self.h = h
        self.n = 0
        self.name = name


class Prog:
    SEMS = ("sd", "gd", "rd", "pe", "ve", "se", "po", "od", "cc", "s2", "g2", "r2", "g3")

    def __init__(self, nc, ctx):
        self.nc = nc
        self.ctx = ctx
        self.st = {e: [] for e in ENGINES}
        for name in self.SEMS:
            setattr(self, name, Sem(ctx.enter_context(nc.semaphore(name)), name))

    def emit(self, eng, fn, sem=None, inc=1, waits=()):
        waits = tuple(waits)
        if sem is not None:
            sem.n += inc
        semh = sem.h if sem is not None else None

        def run(e, fn=fn, waits=waits, semh=semh, inc=inc):
            for s, v in waits:
                e.wait_ge(s.h, v)
            ins = fn(e)
            if semh is not None:
                ins.then_inc(semh, inc)
        self.st[eng].append(run)
        return sem.n if sem is not None else None

    def dma(self, q, out, in_, sem, waits=()):
        gate = (sem, sem.n)
        return self.emit(q, lambda e: e.dma_start(out=out, in_=in_), sem=sem, inc=16,
                         waits=tuple(waits) + (gate,))

    def wait_all(self):
        return [(getattr(self, n), getattr(self, n).n) for n in self.SEMS if getattr(self, n).n > 0]

    def barrier(self):
        vals = tuple(self.wait_all())
        for eng in ENGINES:
            def run(e, vals=vals):
                for s, v in vals:
                    e.wait_ge(s.h, v)
            self.st[eng].append(run)

    def realize(self, block):
        for eng in ENGINES:
            ops = self.st[eng]
            if ops:
                getattr(block, eng)(lambda e, ops=ops: [f(e) for f in ops])



def build_phase1(p, io, E):
    nc = p.nc
    xin, resid, kspill = io["xin"], io["resid"], io["kspill"]
    sd, gd, rd, pe, ve, se, po = p.sd, p.gd, p.rd, p.pe, p.ve, p.se, p.po

    wbuf = E(nc.sbuf_tensor("p1_wbuf", [128, 16, 1024], BF16))
    wg1 = E(nc.sbuf_tensor("p1_wg1", [128, 8, 64], BF16))
    wg2 = E(nc.sbuf_tensor("p1_wg2", [64, 1024], BF16))
    vbuf = E(nc.sbuf_tensor("p1_vbuf", [128, 3, 32, 128], BF16))
    vstage = E(nc.sbuf_tensor("p1_vstage", [128, 4, 512], BF16))
    kbuf = E(nc.sbuf_tensor("p1_kbuf", [128, 3, 4096], BF16))
    cosg = E(nc.sbuf_tensor("p1_cosg", [128, 2, 2, 512], BF16))
    xsub = E(nc.sbuf_tensor("p1_xsub", [128, 2, 1024], F32))
    xtf = E(nc.sbuf_tensor("p1_xtf", [128, 2, 8, 512], BF16))
    xn_fm = E(nc.sbuf_tensor("p1_xnfm", [128, 2, 8, 512], BF16))
    murow = E(nc.sbuf_tensor("p1_murow", [4, 128], F32))
    rsrow = E(nc.sbuf_tensor("p1_rsrow", [4, 128], F32))
    q_fm = E(nc.sbuf_tensor("p1_qfm", [128, 8, 512], BF16))
    g1sb = E(nc.sbuf_tensor("p1_g1", [64, 512], BF16))
    gate = E(nc.sbuf_tensor("p1_gate", [128, 8, 512], BF16))
    attn = E(nc.sbuf_tensor("p1_attn", [128, 8, 512], BF16))
    xo = E(nc.sbuf_tensor("p1_xo", [128, 8, 512], BF16))
    esb = E(nc.sbuf_tensor("p1_esb", [128, 6, 512], BF16))
    ksil = E(nc.sbuf_tensor("p1_ksil", [128, 3, 512], BF16))
    ropeA = E(nc.sbuf_tensor("p1_ropeA", [128, 2, 512], F32))
    ropeB = E(nc.sbuf_tensor("p1_ropeB", [128, 512], F32))
    lnsA = E(nc.sbuf_tensor("p1_lnsA", [128, 512], F32))
    lnsB = E(nc.sbuf_tensor("p1_lnsB", [128, 512], F32))
    sq = E(nc.sbuf_tensor("p1_sq", [128, 512], BF16))
    x1t = E(nc.sbuf_tensor("p1_x1t", [128, 2, 1024], F32))
    ssum = E(nc.sbuf_tensor("p1_ssum", [128, 32], F32))
    ssq = E(nc.sbuf_tensor("p1_ssq", [128, 32], F32))
    sqdump = xtf
    mu = E(nc.sbuf_tensor("p1_mu", [128, 32], F32))
    rstd = E(nc.sbuf_tensor("p1_rstd", [128, 32], F32))
    tmp32 = E(nc.sbuf_tensor("p1_tmp32", [128, 32], F32))

    srow = lambda i: x1t[0:1, i // 2, (i % 2) * 512:(i % 2) * 512 + 512]

    pqA0 = E(nc.psum_tensor("p1_pqA0", [128, 512], F32))
    pqA1 = E(nc.psum_tensor("p1_pqA1", [128, 512], F32))
    pqA2 = E(nc.psum_tensor("p1_pqA2", [128, 512], F32))
    pqB0 = E(nc.psum_tensor("p1_pqB0", [128, 512], F32))
    pqB1 = E(nc.psum_tensor("p1_pqB1", [128, 512], F32))
    pqB2 = E(nc.psum_tensor("p1_pqB2", [128, 512], F32))
    ov = E(nc.psum_tensor("p1_ov", [128, 512], F32))
    m1 = E(nc.psum_tensor("p1_m1", [128, 512], F32))

    sb = io["sb"]

    wk_ld = p.dma("gpsimd", wbuf[:, 0:8, :], io["Wk"].rearrange("(c p) o -> p c o", p=128), gd)
    wv_ld = p.dma("gpsimd", wbuf[:, 8:16, :], io["Wv"].rearrange("(c p) o -> p c o", p=128), gd)

    ve_tile, se_tile = [], []
    for i in range(32):
        slot = i % 2
        w = [(ve, ve_tile[i - 2]), (se, se_tile[i - 2])] if i >= 2 else []
        ld = p.dma("sync", xsub[:, slot, :], xin[i * 128:(i + 1) * 128, :], sd, waits=w)
        v = p.emit("vector", lambda e, s=slot, i=i: e.reduce_sum(ssum[:, i:i + 1], xsub[:, s, :], AX.X),
                   sem=ve, waits=[(sd, ld)])
        s = p.emit("scalar", lambda e, s=slot, i=i: e.activation(sqdump[:, 0, 0:2, :], xsub[:, s, :],
                                                                 AF.Square, accum_out=ssq[:, i:i + 1]),
                   sem=se, waits=[(sd, ld)])
        ve_tile.append(v)
        se_tile.append(s)
    v1 = p.emit("vector", lambda e: e.tensor_scalar_mul(mu[:], ssum[:], 1.0 / D), sem=ve,
                waits=[(ve, ve_tile[-1])])
    v2 = p.emit("vector", lambda e: e.tensor_scalar_mul(tmp32[:], ssq[:], 1.0 / D), sem=ve,
                waits=[(se, se_tile[-1]), (ve, v1)])
    v3 = p.emit("vector", lambda e: e.tensor_mul(rstd[:], mu[:], mu[:]), sem=ve, waits=[(ve, v2)])
    v4 = p.emit("vector", lambda e: e.tensor_sub(tmp32[:], tmp32[:], rstd[:]), sem=ve, waits=[(ve, v3)])
    s1 = p.emit("scalar", lambda e: e.activation(tmp32[:], tmp32[:], AF.Sqrt, bias=sb["fc"][:, 0:1]), sem=se,
                waits=[(ve, v4)])
    stats_done = p.emit("vector", lambda e: e.reciprocal(rstd[:], tmp32[:]), sem=ve, waits=[(se, s1)])

    xtf_rd = [0, 0]
    sqdump_se = [se.n]
    bc_rd = [[], []]
    murow_rd = [0]

    def xtf_load(g):
        xs = g % 2
        return p.dma("gpsimd", xtf[:, xs, :, :],
                     io["xinT"][:, g * 512:(g + 1) * 512].rearrange("(c q) n -> q c n", q=128),
                     gd, waits=[(ve, xtf_rd[xs]), (se, sqdump_se[0])])

    def build_xn(g, xld2):
        xs = g % 2
        tmu = p.emit("tensor", lambda e, g=g: e.transpose(
            m1[0:4, 0:128], mu[:, g * 4:(g + 1) * 4], sb["id_f32"][:, :]),
            sem=pe, waits=[(ve, stats_done), (se, proj_se_g[0]), (se, proj_se_g[1]),
                           (ve, ve.n)])
        cm = p.emit("vector", lambda e: e.tensor_copy(murow[:, :], m1[0:4, 0:128]),
                    sem=ve, waits=[(pe, tmu)])
        tru = p.emit("tensor", lambda e, g=g: e.transpose(
            m1[0:4, 128:256], rstd[:, g * 4:(g + 1) * 4], sb["id_f32"][:, :]),
            sem=pe, waits=[(ve, cm)])
        cr = p.emit("vector", lambda e: e.tensor_copy(rsrow[:, :], m1[0:4, 128:256]),
                    sem=ve, waits=[(pe, tru)])
        for si in range(4):
            p.emit("tensor", lambda e, si=si: e.matmul(
                pqA2[:, si * 128:(si + 1) * 128], sb["sel4"][:, si, :], murow[:, :],
                start=True, stop=True),
                sem=pe, waits=([(ve, cm), (ve, ve.n), (se, se.n)] + bc_rd[0]) if si == 0 else [])
            p.emit("tensor", lambda e, si=si: e.matmul(
                pqB2[:, si * 128:(si + 1) * 128], sb["sel4"][:, si, :], rsrow[:, :],
                start=True, stop=True),
                sem=pe, waits=([(ve, cr)] + bc_rd[1]) if si == 0 else [])
        bdone = pe.n
        lastv = None
        for c in range(8):
            t1 = p.emit("vector", lambda e, c=c, xs=xs: e.tensor_tensor(
                lnsA[:, :], xtf[:, xs, c, :], pqA2[:, :], ALU.subtract), sem=ve,
                waits=[(pe, bdone), (gd, xld2)] + ([(pe, pe.n)] if c == 0 else []))
            lastv = p.emit("vector", lambda e, c=c, xs=xs: e.tensor_tensor(
                xn_fm[:, xs, c, :], lnsA[:, :], pqB2[:, :], ALU.mult), sem=ve,
                waits=[(ve, t1)])
        xtf_rd[xs] = lastv
        bc_rd[0] = [(ve, lastv)]
        bc_rd[1] = [(ve, lastv)]
        murow_rd[0] = lastv
        return lastv

    def proj_rope_mm(c, cslot, xs, dst_ap, first_wait=()):
        kb = c % 2
        ks = c % 3
        mm_bank = (m1, pqB0)[kb]
        for kk in range(8):
            w = list(first_wait) if kk == 0 else []
            if kk == 0:
                w.append((se, proj_se_g[kb]))
            p.emit("tensor", lambda e, c=c, kk=kk, mm_bank=mm_bank, xs=xs: e.matmul(
                mm_bank[:, :], wbuf[:, kk, c * 128:(c + 1) * 128], xn_fm[:, xs, kk, :],
                start=(kk == 0), stop=(kk == 7)), sem=pe, waits=w)
        mmk = pe.n
        kw = kst_dma[ks]
        sv = p.emit("scalar", lambda e, ks=ks, mm_bank=mm_bank: e.activation(
            ksil[:, ks, :], mm_bank[:, :], AF.Silu),
            sem=se, waits=[(pe, mmk), (ve, ksil_ve_g[ks])] + ([kw] if kw else []))
        proj_se_g[kb] = sv
        return (c, cslot, dst_ap, kb, ks, sv)

    def proj_rope_tail(tok, cldw):
        (c, cslot, dst_ap, kb, ks, sv) = tok
        rp_bank = (pqA0, pqA1)[kb]
        rv = p.emit("tensor", lambda e, ks=ks, rp_bank=rp_bank: e.matmul(
            rp_bank[:, :], sb["pshift"][:, :], ksil[:, ks, :], start=True, stop=True),
            sem=pe, waits=[(se, sv), (ve, rope_ve_g[kb]), (se, rope_se_g[kb])])
        av = p.emit("vector", lambda e, ks=ks, kb=kb, cs=cslot: e.tensor_tensor(
            ropeA[:, kb, :], ksil[:, ks, :], cosg[:, cs, 0, :], ALU.mult),
            sem=ve, waits=[(se, sv)])
        tv = p.emit("vector", lambda e, cs=cslot, kb=kb, rp_bank=rp_bank: e.tensor_tensor(
            ropeB[:, :], rp_bank[:, :], cosg[:, cs, 1, :], ALU.mult),
            sem=ve, waits=[(pe, rv), (p.s2, cldw)])
        rope_ve_g[kb] = tv
        kv2 = p.emit("vector", lambda e, dst=dst_ap, kb=kb: e.tensor_add(
            dst, ropeA[:, kb, :], ropeB[:, :]), sem=ve, waits=[(ve, tv)])
        ksil_ve_g[ks] = kv2
        return ks, kv2

    proj_se_g = [0, 0]
    rope_se_g = [0, 0]
    ksil_ve_g = [0, 0, 0]
    rope_ve_g = [0, 0]
    kst_dma = [None, None, None]
    vst_dma = [0, 0]

    bank_vrd = {id(pqB1): [], id(ov): []}
    xld = [None, None]
    xld[0] = xtf_load(0)
    xn_ve_next = None
    for g in range(8):
        cslot = g % 2
        p.dma("sync", cosg[:, cslot, 0, :], io["cosd"][:, g * 512:(g + 1) * 512], p.s2,
              waits=[(ve, ve.n)])
        cld = p.dma("sync", cosg[:, cslot, 1, :], io["sind"][:, g * 512:(g + 1) * 512], p.s2)
        if g == 0:
            xn_ve = build_xn(0, xld[0])
        else:
            xn_ve = xn_ve_next
        if g < 7:
            xld[(g + 1) % 2] = xtf_load(g + 1)

        def flush_k(tok, g=g, cld=cld):
            ks, kv2 = proj_rope_tail(tok, cld)
            qsem = (rd, sd)[tok[0] % 2]
            kst_dma[ks] = (qsem, p.dma("sync", kspill[tok[0], :, g * 512:(g + 1) * 512],
                                       ksil[:, ks, :], qsem, waits=[(ve, kv2)]))

        pend = None
        for c in range(8):
            fw = [(ve, xn_ve), (p.s2, cld)]
            if g == 0 and c == 0:
                fw.append((gd, wk_ld))
            tok = proj_rope_mm(c, cslot, g % 2, ksil[:, c % 3, :], first_wait=fw)
            if pend is not None:
                flush_k(pend)
            pend = tok
        flush_k(pend)
        if g < 7:
            xn_ve_next = build_xn(g + 1, xld[(g + 1) % 2])
        cvs = []
        for nh in range(2):
            for m in range(4):
                vb = (pqB1, ov)[m % 2]
                for kk in range(8):
                    w = []
                    if kk == 0:
                        w = list(bank_vrd[id(vb)])
                        if g == 0 and m == 0 and nh == 0:
                            w.append((gd, wv_ld))
                    p.emit("tensor", lambda e, m=m, nh=nh, kk=kk, vb=vb, xs=g % 2: e.matmul(
                        vb[:, :], xn_fm[:, xs, kk, m * 128:(m + 1) * 128],
                        wbuf[:, 8 + kk, nh * 512:(nh + 1) * 512],
                        start=(kk == 0), stop=(kk == 7)), sem=pe, waits=w)
                mmv = pe.n
                cv = p.emit("scalar", lambda e, m=m, vb=vb: e.copy(
                    vstage[:, m, :], vb[:, :]), sem=se,
                    waits=[(pe, mmv)] + ([(p.g2, vst_dma[0]), (p.g3, vst_dma[1])] if m == 0 else []))
                bank_vrd[id(vb)] = [(se, cv)]
                cvs.append(cv)
            for pq4 in range(4):
                pr = nh * 4 + pq4
                vst_dma[pq4 % 2] = p.dma(
                    "gpsimd",
                    io["vspill"][pr, :, g * 4:(g + 1) * 4, :],
                    vstage[:, :, pq4 * 128:(pq4 + 1) * 128].rearrange("q m d -> q m d"),
                    (p.g2, p.g3)[pq4 % 2], waits=[(se, cvs[-1])])
            cvs = []

    wq_ld = p.dma("gpsimd", wbuf[:, 0:8, :], io["Wq"].rearrange("(c p) o -> p c o", p=128), gd,
                  waits=[(pe, pe.n)])
    wo_ld = p.dma("gpsimd", wbuf[:, 8:16, :], io["Wo"].rearrange("(c p) o -> p c o", p=128), gd)
    wg1_ld = p.dma("gpsimd", wg1[:, :, :], io["Wg1"].rearrange("(c p) o -> p c o", p=128), gd)
    wg2_ld = p.dma("gpsimd", wg2[:, :], io["Wg2"][:, :], gd)
    rd_spill_done = rd.n

    kbuf_dma = [None, None, None]
    vbuf_dma = [0, 0, 0]
    kbuf_rd = [0, 0, 0]
    vbuf_rd = [0, 0, 0]
    ov_guard = [[(se, se.n)]]
    vspill_done = p.g2.n

    xldq = [None, None]
    xldq[0] = xtf_load(0)
    xn_ve_next = build_xn(0, xldq[0])
    for qt in range(8):
        span = (qt + 1) * 512
        cslot = qt % 2
        p.dma("sync", cosg[:, cslot, 0, :], io["cosd"][:, qt * 512:(qt + 1) * 512], p.s2,
              waits=[(ve, ve.n)])
        cld = p.dma("sync", cosg[:, cslot, 1, :], io["sind"][:, qt * 512:(qt + 1) * 512], p.s2)
        xn_ve = xn_ve_next
        if qt < 7:
            xldq[(qt + 1) % 2] = xtf_load(qt + 1)
        rope_ve_g[0] = rope_ve_g[1] = ve.n
        rope_se_g[0] = rope_se_g[1] = se.n
        pend = None
        for c in range(8):
            fw = [(ve, xn_ve), (p.s2, cld)]
            if qt == 0 and c == 0:
                fw.append((gd, wq_ld))
            tok = proj_rope_mm(c, cslot, qt % 2, q_fm[:, c, :], first_wait=fw)
            if pend is not None:
                proj_rope_tail(pend, cld)
            pend = tok
        proj_rope_tail(pend, cld)
        q_done_ve = ve.n
        if qt < 7:
            xn_ve_next = build_xn(qt + 1, xldq[(qt + 1) % 2])
        for kk in range(8):
            w = [(ve, q_done_ve), (se, se.n)] if kk == 0 else []
            if kk == 0 and qt == 0:
                w.append((gd, wg1_ld))
            p.emit("tensor", lambda e, kk=kk, xs=qt % 2: e.matmul(
                m1[0:64, :], wg1[:, kk, :], xn_fm[:, xs, kk, :], start=(kk == 0), stop=(kk == 7)),
                sem=pe, waits=w)
        g1mm = pe.n
        g1v = p.emit("vector", lambda e: e.tensor_copy(g1sb[:, :], m1[0:64, :]), sem=ve, waits=[(pe, g1mm)])
        gate_se = [0] * 8
        m1_guard = [[(ve, g1v)]]

        def emit_gate(c, qt=qt, g1v=g1v):
            w = [(ve, g1v)] + m1_guard[0]
            if qt == 0 and c == 0:
                w.append((gd, wg2_ld))
            gm = p.emit("tensor", lambda e, c=c: e.matmul(
                m1[:, :], wg2[:, c * 128:(c + 1) * 128], g1sb[:, :], start=True, stop=True),
                sem=pe, waits=w)
            sg = p.emit("scalar", lambda e, c=c: e.activation(gate[:, c, :], m1[:, :], AF.Sigmoid),
                        sem=se, waits=[(pe, gm)])
            gate_se[c] = sg
            m1_guard[0] = [(se, sg)]

        for sl in range(2):
            kbuf_dma[sl] = p.dma("sync", kbuf[:, sl, 0:span], kspill[sl, :, 0:span], p.r2,
                                 waits=[(rd, rd_spill_done), (pe, kbuf_rd[sl])])
            vbuf_dma[sl] = p.dma("sync", vbuf[:, sl, 0:span // 128, :],
                                 io["vspill"][sl, :, 0:span // 128, :],
                                 (p.g2, p.g3)[sl % 2], waits=[(p.g2, vspill_done), (pe, vbuf_rd[sl])])

        esb_last = [[(pe, pe.n)]] * 6
        bank_rd = {}
        for b in (pqA0, pqA1, pqA2, pqB0, pqB1, pqB2):
            bank_rd[id(b)] = [(ve, ve.n), (se, se.n)]
        njt = (qt + 1) * 4
        gvs = []
        for pr in range(8):
            ksl = pr % 3
            emit_gate(pr)
            nxt = pr + 2
            if nxt < 8:
                dst = nxt % 3
                kbuf_dma[dst] = p.dma("sync", kbuf[:, dst, 0:span], kspill[nxt, :, 0:span],
                                      p.r2, waits=[(rd, rd_spill_done), (pe, kbuf_rd[dst])])
                vbuf_dma[dst] = p.dma("sync", vbuf[:, dst, 0:span // 128, :],
                                      io["vspill"][nxt, :, 0:span // 128, :],
                                      (p.g2, p.g3)[dst % 2],
                                      waits=[(p.g2, vspill_done), (pe, vbuf_rd[dst])])
            ev_q = []
            for j in range(njt + 1):
                if j < njt:
                    kj0 = j * 128
                    off = max(0, kj0 - qt * 512)
                    diag = kj0 >= qt * 512
                    r3 = j % 3
                    pa = (pqA0, pqA1, pqA2)[r3]
                    pb = (pqB0, pqB1, pqB2)[r3]
                    ea = 2 * r3
                    eb = ea + 1
                    wA = ([(p.r2, kbuf_dma[ksl])] if j == 0 else []) + bank_rd[id(pa)]
                    if j == 0:
                        wA.append((ve, q_done_ve))
                    mA = p.emit("tensor", lambda e, pa=pa, off=off, j=j, ks=ksl, pr=pr: e.matmul(
                        pa[:, off:512], kbuf[0:64, ks, j * 128:(j + 1) * 128], q_fm[0:64, pr, off:512],
                        start=True, stop=True, tile_position=(0, 0)), sem=pe, waits=wA)
                    mB = p.emit("tensor", lambda e, pb=pb, off=off, j=j, ks=ksl, pr=pr: e.matmul(
                        pb[:, off:512], kbuf[64:128, ks, j * 128:(j + 1) * 128], q_fm[64:128, pr, off:512],
                        start=True, stop=True, tile_position=(64, 0)), sem=pe, waits=bank_rd[id(pb)])
                if ev_q:
                    (jd, offd, ead, ebd, evw) = ev_q.pop(0)
                    evA = p.emit("tensor", lambda e, j=jd, off=offd, ea=ead, ksl=ksl, ovb=ov: e.matmul(
                        ovb[0:64, off:512], vbuf[:, ksl, j, 0:64], esb[:, ea, off:512],
                        start=(j == 0), stop=False, tile_position=(0, 0)), sem=pe, waits=evw)
                    evB = p.emit("tensor", lambda e, j=jd, off=offd, eb=ebd, ksl=ksl, ovb=ov, njt=njt: e.matmul(
                        ovb[64:128, off:512], vbuf[:, ksl, j, 64:128], esb[:, eb, off:512],
                        start=(j == 0), stop=(j == njt - 1), tile_position=(0, 64)), sem=pe, waits=evw)
                    esb_last[ead] = [(pe, evA)]
                    esb_last[ebd] = [(pe, evB)]
                if j < njt:
                    if diag:
                        va = p.emit("vector", lambda e, pa=pa, off=off, ea=ea: e.tensor_tensor(
                            esb[:, ea, off:off + 128], pa[:, off:off + 128], sb["tri"][:, :], ALU.mult),
                            sem=ve, waits=[(pe, mA)] + esb_last[ea])
                        if off + 128 < 512:
                            va = p.emit("vector", lambda e, pa=pa, off=off, ea=ea: e.tensor_copy(
                                esb[:, ea, off + 128:512], pa[:, off + 128:512]), sem=ve)
                        vb2 = p.emit("vector", lambda e, pb=pb, off=off, eb=eb: e.tensor_tensor(
                            esb[:, eb, off:off + 128], pb[:, off:off + 128], sb["tri"][:, :], ALU.mult),
                            sem=ve, waits=[(pe, mB)] + esb_last[eb])
                        evw = [(ve, va), (ve, vb2)]
                        if off + 128 < 512:
                            sc2 = p.emit("scalar", lambda e, pb=pb, off=off, eb=eb: e.copy(
                                esb[:, eb, off + 128:512], pb[:, off + 128:512]), sem=se, waits=[(pe, mB)])
                            evw.append((se, sc2))
                        bank_rd[id(pa)] = [(ve, va)]
                        bank_rd[id(pb)] = [(ve, vb2)] + ([(se, sc2)] if off + 128 < 512 else [])
                    else:
                        va = p.emit("vector", lambda e, pa=pa, ea=ea: e.tensor_copy(
                            esb[:, ea, :], pa[:, :]), sem=ve, waits=[(pe, mA)] + esb_last[ea])
                        sc2 = p.emit("scalar", lambda e, pb=pb, eb=eb: e.copy(
                            esb[:, eb, :], pb[:, :]), sem=se, waits=[(pe, mB)] + esb_last[eb])
                        evw = [(ve, va), (se, sc2)]
                        bank_rd[id(pa)] = [(ve, va)]
                        bank_rd[id(pb)] = [(se, sc2)]
                    if j == 0:
                        evw = evw + ov_guard[0] + [((p.g2, p.g3)[ksl % 2], vbuf_dma[ksl])]
                    ev_q.append((j, off, ea, eb, evw))
            gv = p.emit("vector", lambda e, pr=pr: e.tensor_tensor(
                attn[:, pr, :], ov[:, :], gate[:, pr, :], ALU.mult), sem=ve,
                waits=[(pe, pe.n), (se, gate_se[pr])])
            gvs.append(gv)
            ov_guard[0] = [(ve, gv)]
            kbuf_rd[ksl] = pe.n
            vbuf_rd[ksl] = pe.n
        attn_done_ve = ve.n

        for c in range(8):
            sqv = p.emit("scalar", lambda e, c=c: e.activation(sq[:, :], attn[:, c, :], AF.Square),
                         sem=se, waits=[(ve, gvs[c])] + ([] if c == 0 else [(pe, pe.n)]))
            p.emit("tensor", lambda e, c=c: e.matmul(
                m1[0:1, :], sb["ones_col_bf"][:, :], attn[:, c, :], start=(c == 0), stop=(c == 7)),
                sem=pe, waits=([(ve, gvs[0]), (se, se.n)] if c == 0 else [(ve, gvs[c])]))
            p.emit("tensor", lambda e, c=c: e.matmul(
                pqA0[0:1, :], sb["ones_col_bf"][:, :], sq[:, :], start=(c == 0), stop=(c == 7)),
                sem=pe, waits=[(se, sqv)] + (bank_rd[id(pqA0)] if c == 0 else []))
        sums_pe = pe.n
        r1 = p.emit("vector", lambda e: e.tensor_scalar_mul(srow(0), m1[0:1, :], 1.0 / D),
                    sem=ve, waits=[(pe, sums_pe), (rd, rd.n)])
        r2 = p.emit("vector", lambda e: e.tensor_scalar_mul(srow(1), pqA0[0:1, :], 1.0 / D),
                    sem=ve, waits=[(ve, r1)])
        r3 = p.emit("vector", lambda e: e.tensor_tensor(srow(2), srow(0), srow(0), ALU.mult),
                    sem=ve, waits=[(ve, r2)])
        r4 = p.emit("vector", lambda e: e.tensor_sub(srow(1), srow(1), srow(2)),
                    sem=ve, waits=[(ve, r3)])
        r5 = p.emit("scalar", lambda e: e.activation(srow(1), srow(1), AF.Sqrt, bias=sb["fc"][0:1, 0:1]),
                    sem=se, waits=[(ve, r4)])
        r6 = p.emit("vector", lambda e: e.reciprocal(srow(2), srow(1)), sem=ve, waits=[(se, r5)])
        r8 = p.emit("vector", lambda e: e.scalar_tensor_tensor(
            srow(3), srow(0), -1.0, srow(2), ALU.mult, ALU.mult),
            sem=ve, waits=[(ve, r6)])
        bca = p.emit("tensor", lambda e: e.matmul(pqB0[:, :], sb["ones_row"][:, :], srow[:, 2:3, 0, :]
                                                  if False else srow(2),
                                                  start=True, stop=True), sem=pe,
                     waits=[(ve, r8)] + bank_rd[id(pqB0)])
        bcb = p.emit("tensor", lambda e: e.matmul(pqB1[:, :], sb["ones_row"][:, :], srow(3),
                                                  start=True, stop=True), sem=pe,
                     waits=[(ve, r8)] + bank_rd[id(pqB1)])
        for c in range(8):
            t1 = p.emit("vector", lambda e, c=c: e.tensor_tensor(lnsA[:, :], attn[:, c, :], pqB0[:, :], ALU.mult),
                        sem=ve, waits=([(pe, bca)] if c == 0 else []))
            t2 = p.emit("vector", lambda e, c=c: e.tensor_tensor(lnsB[:, :], lnsA[:, :], pqB1[:, :], ALU.add),
                        sem=ve, waits=([(pe, bcb)] if c == 0 else []) + [(ve, t1)])
            p.emit("vector", lambda e, c=c: e.tensor_scalar(
                xo[:, c, :], lnsB[:, :], sb["lnw"][:, c:c + 1], sb["lnb"][:, c:c + 1],
                ALU.mult, ALU.add), sem=ve, waits=[(ve, t2)])
        xo_done = ve.n

        for ms in range(4):
            slot = ms % 2
            xl = p.dma("sync", xsub[:, slot, :], xin[(qt * 4 + ms) * 128:(qt * 4 + ms + 1) * 128, :], sd,
                       waits=[(ve, ve.n), (se, se.n)])
            for nh in range(2):
                wop = (m1, pqA1)[nh]
                for kk in range(8):
                    w = []
                    if kk == 0:
                        w = [(ve, xo_done)] if ms == 0 and nh == 0 else [(ve, ve.n)]
                        if qt == 0 and ms == 0 and nh == 0:
                            w.append((gd, wo_ld))
                        if nh == 1:
                            w += bank_rd[id(pqA1)]
                            bank_rd[id(pqA1)] = []
                    p.emit("tensor", lambda e, ms=ms, nh=nh, kk=kk, wop=wop: e.matmul(
                        wop[:, :], xo[:, kk, ms * 128:(ms + 1) * 128],
                        wbuf[:, 8 + kk, nh * 512:(nh + 1) * 512],
                        start=(kk == 0), stop=(kk == 7)), sem=pe, waits=w)
                mmo = pe.n
                p.emit("vector", lambda e, slot=slot, nh=nh, wop=wop: e.tensor_tensor(
                    x1t[:, slot, nh * 512:(nh + 1) * 512], xsub[:, slot, nh * 512:(nh + 1) * 512],
                    wop[:, :], ALU.add), sem=ve, waits=[(pe, mmo), (sd, xl), (rd, rd.n)])
            p.dma("sync", resid[(qt * 4 + ms) * 128:(qt * 4 + ms + 1) * 128, :], x1t[:, slot, :], rd,
                  waits=[(ve, ve.n)])





def build_phase2(p, io, E):
    nc = p.nc
    resid = io["resid"]
    sb = io["sb"]
    sd, gd, rd, pe, ve, se, po = p.sd, p.gd, p.rd, p.pe, p.ve, p.se, p.po

    xn_fm = E(nc.sbuf_tensor("p2_xnfm", [128, 8, 1024], BF16))
    hbufA = E(nc.sbuf_tensor("p2_h", [128, 32, 1024], BF16))
    hbufB = E(nc.sbuf_tensor("p2_g1", [128, 32, 1024], BF16))
    w12b = E(nc.sbuf_tensor("p2_w12b", [128, 2, 16, 128], BF16))
    w1b = w12b
    hb = E(nc.sbuf_tensor("p2_hb", [128, 2, 32, 128], BF16))
    xrb = E(nc.sbuf_tensor("p2_xrb", [128, 2, 1024], F32))
    xn_tm = E(nc.sbuf_tensor("p2_xntm", [128, 2, 1024], BF16))
    tsc = E(nc.sbuf_tensor("p2_t", [128, 2, 512], F32))
    x2p = E(nc.sbuf_tensor("p2_x2p", [128, 4, 128], F32))
    s1s = E(nc.sbuf_tensor("p2_s1", [128, 32], F32))
    s2s = E(nc.sbuf_tensor("p2_s2", [128, 32], F32))
    hs1s = E(nc.sbuf_tensor("p2_hs1", [128, 32], F32))
    hs2s = E(nc.sbuf_tensor("p2_hs2", [128, 32], F32))
    s3s = E(nc.sbuf_tensor("p2_s3", [128, 8], F32))
    stsum = E(nc.sbuf_tensor("p2_stsum", [128, 8], F32))
    stsq = E(nc.sbuf_tensor("p2_stsq", [128, 8], F32))
    mu2 = E(nc.sbuf_tensor("p2_mu", [128, 8], F32))
    rstd2 = E(nc.sbuf_tensor("p2_rstd", [128, 8], F32))
    tmp2 = E(nc.sbuf_tensor("p2_tmp", [128, 8], F32))
    sqd = E(nc.sbuf_tensor("p2_sqd", [128, 1024], BF16))

    pa0 = E(nc.psum_tensor("p2_pa0", [128, 512], F32))
    pa1 = E(nc.psum_tensor("p2_pa1", [128, 512], F32))
    pb0 = E(nc.psum_tensor("p2_pb0", [128, 512], F32))
    pb1 = E(nc.psum_tensor("p2_pb1", [128, 512], F32))
    tpA = E(nc.psum_tensor("p2_tpA", [128, 1024], BF16))
    tpB = E(nc.psum_tensor("p2_tpB", [128, 1024], BF16))
    po0 = E(nc.psum_tensor("p2_po0", [128, 512], F32))
    po1 = E(nc.psum_tensor("p2_po1", [128, 512], F32))

    for name, dst in (("s1_d", s1s), ("s2_d", s2s), ("hs1_d", hs1s), ("hs2_d", hs2s), ("s3_d", s3s)):
        p.dma("sync", dst[:, :], io[name], sd)
    sc_ld = sd.n

    tp_last = [(ve, ve.n), (ve, ve.n)]
    xntm_rd = [0, 0]
    bank_guard = {id(pa0): [], id(pa1): [], id(pb0): [], id(pb1): [],
                  id(po0): [], id(po1): []}
    w3mm = {}
    w3tp = [0]
    tsc_rd = [0, 0]
    x2p_rd = [0, 0, 0, 0]
    APOOL = (pa0, pa1, po0)
    BPOOL = (pb0, pb1, po1)
    GPOOL = (pa0, pa1, po0, pb0, pb1, po1)
    po_guard = {id(po0): [], id(po1): []}
    wslot_guard = {id(w1b): [None, None], id(hb): [None, None]}

    def stage_mm(src_fm, wtile, wP, m, n_kk, dst_cb, mplan, tile_first=()):
        slot = m % 2
        g = wslot_guard[id(wtile)][slot]
        wld = p.dma("gpsimd", wtile[:, slot, 0:n_kk, :],
                    wP[m].rearrange("kk q mc -> q kk mc"), gd,
                    waits=([(pe, g)] if g else []))
        for nh in range(2):
            pa = mplan[nh]
            for kk in range(n_kk):
                w = []
                if kk == 0:
                    w = [(gd, wld)] + bank_guard[id(pa)] + list(tile_first)
                p.emit("tensor", lambda e, m=m, nh=nh, kk=kk, pa=pa, slot=slot: e.matmul(
                    pa[:, :], wtile[:, slot, kk, :], src_fm[:, kk, nh * 512:(nh + 1) * 512],
                    start=(kk == 0), stop=(kk == n_kk - 1)), sem=pe, waits=w)
            bank_guard[id(pa)] = dst_cb(m, nh, pa)
        wslot_guard[id(wtile)][slot] = pe.n

    for tt in range(4):
        r0 = tt * 1024
        vst, sst = [], []
        for si in range(8):
            slot = si % 2
            w = [(ve, vst[si - 2]), (se, sst[si - 2])] if si >= 2 else [(ve, ve.n), (se, se.n)]
            ld = p.dma("sync", xrb[:, slot, :], resid[r0 + si * 128:r0 + (si + 1) * 128, :], sd,
                       waits=w + [(gd, gd.n)])
            v = p.emit("vector", lambda e, s=slot, si=si: e.reduce_sum(stsum[:, si:si + 1], xrb[:, s, :], AX.X),
                       sem=ve, waits=[(sd, ld)])
            sv = p.emit("scalar", lambda e, s=slot, si=si: e.activation(sqd[:], xrb[:, s, :], AF.Square,
                                                                        accum_out=stsq[:, si:si + 1]),
                        sem=se, waits=[(sd, ld)])
            vst.append(v)
            sst.append(sv)
        v1 = p.emit("vector", lambda e: e.tensor_scalar_mul(mu2[:], stsum[:], 1.0 / D), sem=ve,
                    waits=[(ve, vst[-1])])
        v2 = p.emit("vector", lambda e: e.tensor_scalar_mul(tmp2[:], stsq[:], 1.0 / D), sem=ve,
                    waits=[(se, sst[-1]), (ve, v1)])
        v3 = p.emit("vector", lambda e: e.tensor_mul(rstd2[:], mu2[:], mu2[:]), sem=ve, waits=[(ve, v2)])
        v4 = p.emit("vector", lambda e: e.tensor_sub(tmp2[:], tmp2[:], rstd2[:]), sem=ve, waits=[(ve, v3)])
        sq1 = p.emit("scalar", lambda e: e.activation(tmp2[:], tmp2[:], AF.Sqrt, bias=sb["fc"][:, 0:1]),
                     sem=se, waits=[(ve, v4)])
        st_done = p.emit("vector", lambda e: e.reciprocal(rstd2[:], tmp2[:]), sem=ve, waits=[(se, sq1)])
        for si in range(8):
            slot = si % 2
            ld = p.dma("sync", xrb[:, slot, :], resid[r0 + si * 128:r0 + (si + 1) * 128, :], sd,
                       waits=[(ve, ve.n), (se, se.n)])
            nv = p.emit("vector", lambda e, s=slot, si=si: e.tensor_scalar(
                xn_tm[:, s, :], xrb[:, s, :], mu2[:, si:si + 1], rstd2[:, si:si + 1],
                ALU.subtract, ALU.mult), sem=ve,
                waits=[(sd, ld), (ve, st_done), (pe, xntm_rd[slot])])
            for c in range(8):
                bk = c % 2
                bank = (tpA, tpB)[bk]
                pv = p.emit("tensor", lambda e, s=slot, c=c, bank=bank: e.transpose(
                    bank[:, 0:128], xn_tm[:, s, c * 128:(c + 1) * 128], sb["id_bf"][:, :]),
                    sem=pe, waits=[tp_last[bk], (ve, nv)])
                lv = p.emit("vector", lambda e, c=c, si=si, bank=bank: e.tensor_copy(
                    xn_fm[:, c, si * 128:(si + 1) * 128], bank[:, 0:128]),
                    sem=ve, waits=[(pe, pv)])
                tp_last[bk] = (ve, lv)
            xntm_rd[slot] = pe.n
        xn_done = ve.n

        def h_epilogue(m, nh, pa):
            return []
        for m in range(32):
            slot = m % 2
            g1g = wslot_guard[id(w1b)][slot]
            wld1 = p.dma("gpsimd", w12b[:, slot, :, :], io["w12P"][m].rearrange("kk q mc -> q kk mc"), gd,
                         waits=([(pe, g1g)] if g1g else []))
            wld2 = wld1
            qa0 = (2 * m) % 3
            qa1 = (2 * m + 1) % 3
            for kk in range(8):
                w = []
                if kk == 0:
                    w = [(gd, wld1)] + bank_guard[id(APOOL[qa0])] + bank_guard[id(APOOL[qa1])]
                    if m == 0:
                        w += [(ve, xn_done), (sd, sc_ld)]
                for nh in range(2):
                    p.emit("tensor", lambda e, m=m, nh=nh, kk=kk, slot=slot,
                           pa=APOOL[(2 * m + nh) % 3]: e.matmul(
                        pa[:, :], w1b[:, slot, kk, :], xn_fm[:, kk, nh * 512:(nh + 1) * 512],
                        start=(kk == 0), stop=(kk == 7)), sem=pe, waits=(w if nh == 0 else []))
            mma = pe.n
            for kk in range(8):
                w = []
                if kk == 0:
                    w = [(gd, wld2)] + bank_guard[id(BPOOL[qa0])] + bank_guard[id(BPOOL[qa1])]
                for nh in range(2):
                    p.emit("tensor", lambda e, m=m, nh=nh, kk=kk, slot=slot,
                           pb=BPOOL[(2 * m + nh) % 3]: e.matmul(
                        pb[:, :], w12b[:, slot, 8 + kk, :], xn_fm[:, kk, nh * 512:(nh + 1) * 512],
                        start=(kk == 0), stop=(kk == 7)), sem=pe, waits=(w if nh == 0 else []))
            mmb = pe.n
            for nh in range(2):
                pa = APOOL[(2 * m + nh) % 3]
                pb = BPOOL[(2 * m + nh) % 3]
                sv = p.emit("scalar", lambda e, m=m, pa=pa, ts=nh: e.activation(
                    tsc[:, ts, :], pa[:, :], AF.Silu, scale=s1s[:, m:m + 1]),
                    sem=se, waits=[(pe, mma), (ve, ve.n)] if nh == 0 else [(pe, mma)])
                vv = p.emit("vector", lambda e, m=m, nh=nh, pb=pb, ts=nh: e.scalar_tensor_tensor(
                    hbufA[:, m, nh * 512:(nh + 1) * 512], pb[:, :], s2s[:, m:m + 1], tsc[:, ts, :],
                    ALU.mult, ALU.mult), sem=ve, waits=[(pe, mmb), (se, sv)])
                bank_guard[id(pa)] = [(se, sv)]
                bank_guard[id(pb)] = [(ve, vv)]
            wslot_guard[id(w1b)][slot] = pe.n
        h_done = ve.n

        def big_stage(srcbuf, dstbuf, wPname, scl, first_ve):
            for m in range(32):
                slot = m % 2
                g = wslot_guard[id(hb)][slot]
                wld = p.dma("gpsimd", hb[:, slot, :, :], io[wPname][m].rearrange("kk q mc -> q kk mc"), gd,
                            waits=([(pe, g)] if g else []))
                for kk in range(32):
                    w = []
                    if kk == 0:
                        w = [(gd, wld)] + bank_guard[id(GPOOL[(2 * m) % 6])] + bank_guard[id(GPOOL[(2 * m + 1) % 6])]
                        if m == 0:
                            w += [(ve, first_ve), (se, se.n), (ve, ve.n)]
                    for nh in range(2):
                        p.emit("tensor", lambda e, m=m, nh=nh, kk=kk, slot=slot, srcbuf=srcbuf,
                               pa=GPOOL[(2 * m + nh) % 6]: e.matmul(
                            pa[:, :], hb[:, slot, kk, :], srcbuf[:, kk, nh * 512:(nh + 1) * 512],
                            start=(kk == 0), stop=(kk == 31)), sem=pe, waits=(w if nh == 0 else []))
                mm = pe.n
                for nh in range(2):
                    pa = GPOOL[(2 * m + nh) % 6]
                    sv = p.emit("scalar", lambda e, m=m, nh=nh, pa=pa, dstbuf=dstbuf, scl=scl: e.activation(
                        dstbuf[:, m, nh * 512:(nh + 1) * 512], pa[:, :], AF.Gelu, scale=scl[:, m:m + 1]),
                        sem=se, waits=[(pe, mm)])
                    bank_guard[id(pa)] = [(se, sv)]
                wslot_guard[id(hb)][slot] = pe.n
            return se.n

        g1_done = big_stage(hbufA, hbufB, "h1P", hs1s, h_done)
        g2_done = big_stage(hbufB, hbufA, "h2P", hs2s, ve.n)

        w3q = []

        def flush_w3(item):
            (m_, pa_, pb_) = item
            for nh in range(2):
                pab = (pa_, pb_)[nh]
                ts = nh
                sv = p.emit("scalar", lambda e, m=m_, pab=pab, ts=ts: e.activation(
                    tsc[:, ts, :], pab[:, :], AF.Copy, scale=s3s[:, m:m + 1]),
                    sem=se, waits=[(pe, w3mm[id(pab)]), (pe, tsc_rd[ts])])
                bank_guard[id(pab)] = [(se, sv)]
                for wq_ in range(4):
                    si = nh * 4 + wq_
                    pob = (po0, po1)[w3tp[0] % 2]
                    w3tp[0] += 1
                    tv2 = p.emit("tensor", lambda e, ts=ts, wq_=wq_, pob=pob: e.transpose(
                        pob[:, 0:128], tsc[:, ts, wq_ * 128:(wq_ + 1) * 128], sb["id_f32"][:, :]),
                        sem=pe, waits=[(se, sv)] + bank_guard[id(pob)])
                    cp = p.emit("vector", lambda e, wq_=wq_, pob=pob, xs=si % 4: e.tensor_copy(
                        x2p[:, xs, :], pob[:, 0:128]), sem=ve,
                        waits=[(pe, tv2), (rd, x2p_rd[si % 4])])
                    bank_guard[id(pob)] = [(ve, cp)]
                    x2p_rd[si % 4] = p.emit(
                        "gpsimd", lambda e, si=si, m=m_, xs=si % 4: e.dma_start(
                            out=resid[r0 + si * 128:r0 + (si + 1) * 128, m * 128:(m + 1) * 128],
                            in_=x2p[:, xs, :], accum_op=ALU.add), sem=rd, inc=16,
                        waits=[(ve, cp), (rd, rd.n)])
                tsc_rd[ts] = pe.n

        for m in range(8):
            slot = m % 2
            g = wslot_guard[id(hb)][slot]
            wld = p.dma("gpsimd", hb[:, slot, :, :], io["w3P"][m].rearrange("kk q mc -> q kk mc"), gd,
                        waits=([(pe, g)] if g else []))
            pa_ = (pa0, pb0)[m % 2]
            pb_ = (pa1, pb1)[m % 2]
            for kk in range(32):
                w = []
                if kk == 0:
                    w = [(gd, wld)] + bank_guard[id(pa_)] + bank_guard[id(pb_)]
                    if m == 0:
                        w += [(se, g2_done), (ve, ve.n)]
                for nh in range(2):
                    p.emit("tensor", lambda e, m=m, nh=nh, kk=kk, slot=slot,
                           pab=(pa_, pb_)[nh]: e.matmul(
                        pab[:, :], hb[:, slot, kk, :], hbufA[:, kk, nh * 512:(nh + 1) * 512],
                        start=(kk == 0), stop=(kk == 31)), sem=pe, waits=(w if nh == 0 else []))
            w3mm[id(pa_)] = pe.n
            w3mm[id(pb_)] = pe.n
            wslot_guard[id(hb)][slot] = pe.n
            w3q.append((m, pa_, pb_))
            if len(w3q) > 1:
                flush_w3(w3q.pop(0))
        while w3q:
            flush_w3(w3q.pop(0))



def build_phase4a(p, io, E):
    nc = p.nc
    l1in = io["l1in"]
    sb = io["sb"]
    sd, gd, rd, pe, ve, se, po = p.sd, p.gd, p.rd, p.pe, p.ve, p.se, p.po

    wbuf = E(nc.sbuf_tensor("p4_wbuf", [128, 3, 8, 1024], BF16))
    xrb = E(nc.sbuf_tensor("p4_xrb", [128, 2, 1024], F32))
    xn_tm = E(nc.sbuf_tensor("p4_xntm", [128, 2, 1024], BF16))
    xn_fm = E(nc.sbuf_tensor("p4_xnfm", [128, 8, 512], BF16))
    sqf = E(nc.sbuf_tensor("p4_sqf", [128, 2, 512], F32))
    sdt = E(nc.sbuf_tensor("p4_sd", [128, 2, 512], F32))
    khc = E(nc.sbuf_tensor("p4_khc", [128, 2, 512], F32))
    vqc = E(nc.sbuf_tensor("p4_vqc", [128, 2, 512], BF16))
    posblk = E(nc.sbuf_tensor("p4_posblk", [128, 128], F32))
    stsum = E(nc.sbuf_tensor("p4_stsum", [128, 4], F32))
    stsq = E(nc.sbuf_tensor("p4_stsq", [128, 4], F32))
    mu4 = E(nc.sbuf_tensor("p4_mu", [128, 4], F32))
    rstd4 = E(nc.sbuf_tensor("p4_rstd", [128, 4], F32))
    tmp4 = E(nc.sbuf_tensor("p4_tmp", [128, 4], F32))
    sqd = E(nc.sbuf_tensor("p4_sqd", [128, 1024], BF16))

    pk0 = E(nc.psum_tensor("p4_pk0", [128, 512], F32))
    pk1 = E(nc.psum_tensor("p4_pk1", [128, 512], F32))
    pn0 = E(nc.psum_tensor("p4_pn0", [128, 512], F32))
    pn1 = E(nc.psum_tensor("p4_pn1", [128, 512], F32))
    tpA = E(nc.psum_tensor("p4_tpA", [128, 1024], BF16))
    tpB = E(nc.psum_tensor("p4_tpB", [128, 1024], BF16))

    p.dma("sync", posblk[:, :], io["posblk_d"], sd)
    wld = p.dma("gpsimd", wbuf[:, 0, :, :], io["wkT"].rearrange("(c q) o -> q c o", q=128), gd)
    wld = p.dma("gpsimd", wbuf[:, 1, :, :], io["wvT"].rearrange("(c q) o -> q c o", q=128), gd)
    wld = p.dma("gpsimd", wbuf[:, 2, :, :], io["wqT"].rearrange("(c q) o -> q c o", q=128), gd)

    tp_last = [(ve, ve.n), (ve, ve.n)]
    pk_g = {id(pk0): [], id(pk1): []}
    pn_g = {id(pn0): [], id(pn1): []}
    sq_g = [[], []]
    sd_g = [[], []]
    kh_g = [[], []]
    vq_g = [[], []]

    for t in range(8):
        r0 = t * 512
        vst, sst = [], []
        for si in range(4):
            slot = si % 2
            w = [(ve, vst[si - 2]), (se, sst[si - 2])] if si >= 2 else [(ve, ve.n), (se, se.n)]
            ld = p.dma("sync", xrb[:, slot, :], l1in[r0 + si * 128:r0 + (si + 1) * 128, :], sd, waits=w)
            v = p.emit("vector", lambda e, s=slot, si=si: e.reduce_sum(stsum[:, si:si + 1], xrb[:, s, :], AX.X),
                       sem=ve, waits=[(sd, ld)])
            sv = p.emit("scalar", lambda e, s=slot, si=si: e.activation(sqd[:], xrb[:, s, :], AF.Square,
                                                                        accum_out=stsq[:, si:si + 1]),
                        sem=se, waits=[(sd, ld)])
            vst.append(v)
            sst.append(sv)
        v1 = p.emit("vector", lambda e: e.tensor_scalar_mul(mu4[:], stsum[:], 1.0 / D), sem=ve,
                    waits=[(ve, vst[-1])])
        v2 = p.emit("vector", lambda e: e.tensor_scalar_mul(tmp4[:], stsq[:], 1.0 / D), sem=ve,
                    waits=[(se, sst[-1]), (ve, v1)])
        v3 = p.emit("vector", lambda e: e.tensor_mul(rstd4[:], mu4[:], mu4[:]), sem=ve, waits=[(ve, v2)])
        v4 = p.emit("vector", lambda e: e.tensor_sub(tmp4[:], tmp4[:], rstd4[:]), sem=ve, waits=[(ve, v3)])
        sq1 = p.emit("scalar", lambda e: e.activation(tmp4[:], tmp4[:], AF.Sqrt, bias=sb["fc"][:, 0:1]),
                     sem=se, waits=[(ve, v4)])
        st_done = p.emit("vector", lambda e: e.reciprocal(rstd4[:], tmp4[:]), sem=ve, waits=[(se, sq1)])
        for si in range(4):
            slot = si % 2
            ld = p.dma("sync", xrb[:, slot, :], l1in[r0 + si * 128:r0 + (si + 1) * 128, :], sd,
                       waits=[(ve, ve.n), (se, se.n)])
            nv = p.emit("vector", lambda e, s=slot, si=si: e.tensor_scalar(
                xn_tm[:, s, :], xrb[:, s, :], mu4[:, si:si + 1], rstd4[:, si:si + 1],
                ALU.subtract, ALU.mult), sem=ve, waits=[(sd, ld), (ve, st_done), (pe, pe.n)])
            for c in range(8):
                bk = c % 2
                bank = (tpA, tpB)[bk]
                pv = p.emit("tensor", lambda e, s=slot, c=c, bank=bank: e.transpose(
                    bank[:, 0:128], xn_tm[:, s, c * 128:(c + 1) * 128], sb["id_bf"][:, :]),
                    sem=pe, waits=[tp_last[bk], (ve, nv)])
                lv = p.emit("vector", lambda e, c=c, si=si, bank=bank: e.tensor_copy(
                    xn_fm[:, c, si * 128:(si + 1) * 128], bank[:, 0:128]),
                    sem=ve, waits=[(pe, pv)])
                tp_last[bk] = (ve, lv)
        xn_done = ve.n
        for wi, spname in ((0, "khsp"), (1, "vsp"), (2, "qsp")):
            for c in range(8):
                pk = (pk0, pk1)[c % 2]
                for kk in range(8):
                    w = []
                    if kk == 0:
                        w = pk_g[id(pk)] + [(ve, xn_done)]
                        if t == 0 and wi == 0 and c == 0:
                            w += [(gd, wld), (sd, sd.n)]
                    p.emit("tensor", lambda e, wi=wi, c=c, kk=kk, pk=pk: e.matmul(
                        pk[:, :], wbuf[:, wi, kk, c * 128:(c + 1) * 128], xn_fm[:, kk, :],
                        start=(kk == 0), stop=(kk == 7)), sem=pe, waits=w)
                mmp = pe.n
                if wi == 0:
                    cs = c % 2
                    pn = (pn0, pn1)[cs]
                    sv = p.emit("scalar", lambda e, cs=cs, pk=pk: e.activation(
                        sqf[:, cs, :], pk[:, :], AF.Square), sem=se,
                        waits=[(pe, mmp)] + sq_g[cs])
                    nm = p.emit("tensor", lambda e, cs=cs, pn=pn: e.matmul(
                        pn[:, :], posblk[:, :], sqf[:, cs, :], start=True, stop=True),
                        sem=pe, waits=[(se, sv)] + pn_g[id(pn)])
                    sq_g[cs] = [(pe, nm)]
                    s2 = p.emit("scalar", lambda e, cs=cs, pn=pn: e.activation(
                        sdt[:, cs, :], pn[:, :], AF.Sqrt), sem=se,
                        waits=[(pe, nm)] + sd_g[cs])
                    a1 = p.emit("vector", lambda e, cs=cs: e.tensor_scalar_add(
                        sdt[:, cs, :], sdt[:, cs, :], 1e-6), sem=ve, waits=[(se, s2)])
                    a2 = p.emit("vector", lambda e, cs=cs: e.reciprocal(sdt[:, cs, :], sdt[:, cs, :]),
                                sem=ve, waits=[(ve, a1)])
                    pn_g[id(pn)] = [(ve, a2)]
                    kv = p.emit("vector", lambda e, cs=cs, pk=pk: e.tensor_tensor(
                        khc[:, cs, :], pk[:, :], sdt[:, cs, :], ALU.mult), sem=ve,
                        waits=[(ve, a2)] + kh_g[cs])
                    sd_g[cs] = [(ve, kv)]
                    pk_g[id(pk)] = [(ve, kv)]
                    dm = p.dma("sync", io["khsp"][t, c, :, :], khc[:, cs, :], rd, waits=[(ve, kv)])
                    kh_g[cs] = [(rd, dm)]
                else:
                    cs = c % 2
                    cv = p.emit("scalar", lambda e, cs=cs, pk=pk: e.copy(vqc[:, cs, :], pk[:, :]),
                                sem=se, waits=[(pe, mmp)] + vq_g[cs])
                    pk_g[id(pk)] = [(se, cv)]
                    dm = p.dma("sync", io[spname][t, c, :, :], vqc[:, cs, :], p.r2, waits=[(se, cv)])
                    vq_g[cs] = [(p.r2, dm)]


def build_phase4b(p, io, E):
    nc = p.nc
    l1in, x3buf = io["l1in"], io["x3buf"]
    sb = io["sb"]
    sd, gd, rd, pe, ve, se, po = p.sd, p.gd, p.rd, p.pe, p.ve, p.se, p.po
    import os as _os
    _ser = _os.environ.get("SER4B", "")
    if _ser:
        _emit0 = p.emit
        def _emit(eng, fn, sem=None, inc=1, waits=()):
            hit = "a" in _ser or eng[0] in _ser
            if "d" in _ser and eng == "gpsimd" and inc == 16:
                hit = True
            if "P" in _ser and eng == "gpsimd" and inc == 1:
                hit = True
            if hit:
                waits = list(waits) + [(pe, pe.n), (ve, ve.n), (se, se.n), (po, po.n),
                                       (rd, rd.n), (gd, gd.n), (sd, sd.n)]
            return _emit0(eng, fn, sem=sem, inc=inc, waits=waits)
        p.emit = _emit

    guard0 = E(nc.sbuf_tensor("p4_guard0", [128, 512], F32))
    S = E(nc.sbuf_tensor("p4_S", [128, 64, 512], F32))
    selv = E(nc.sbuf_tensor("p4_selv", [128, 8, 128], BF16))
    selq = E(nc.sbuf_tensor("p4_selq", [128, 8, 128], BF16))
    negb = E(nc.sbuf_tensor("p4_negb", [128, 128], BF16))
    kh = E(nc.sbuf_tensor("p4b_kh", [128, 3, 512], F32))
    vb = E(nc.sbuf_tensor("p4b_vb", [128, 3, 512], BF16))
    qb = E(nc.sbuf_tensor("p4b_qb", [128, 3, 512], BF16))
    Pb = E(nc.sbuf_tensor("p4b_P", [128, 4, 512], BF16))
    Qb = E(nc.sbuf_tensor("p4b_Q", [128, 4, 512], BF16))
    tf = E(nc.sbuf_tensor("p4b_tf", [128, 2, 512], F32))
    sf = E(nc.sbuf_tensor("p4b_sf", [128, 2, 512], F32))
    of = E(nc.sbuf_tensor("p4b_of", [128, 2, 512], F32))
    of2 = E(nc.sbuf_tensor("p4b_of2", [128, 2, 512], F32))
    xrb = E(nc.sbuf_tensor("p4b_xrb", [128, 4, 1024], F32))
    x3t = E(nc.sbuf_tensor("p4b_x3t", [128, 4, 1024], F32))

    pd0 = E(nc.psum_tensor("p4b_pd0", [128, 512], F32))
    pd1 = E(nc.psum_tensor("p4b_pd1", [128, 512], F32))
    pd2 = E(nc.psum_tensor("p4b_pd2", [128, 512], F32))
    pd3 = E(nc.psum_tensor("p4b_pd3", [128, 512], F32))
    pq0 = E(nc.psum_tensor("p4b_pq0", [128, 512], F32))
    pq1 = E(nc.psum_tensor("p4b_pq1", [128, 512], F32))
    po0 = E(nc.psum_tensor("p4b_po0", [128, 512], F32))
    po1 = E(nc.psum_tensor("p4b_po1", [128, 512], F32))

    p.dma("sync", selv[:, :, :], io["selv_d"].rearrange("i q m -> q i m"), sd)
    p.dma("sync", selq[:, :, :], io["selq_d"].rearrange("i q m -> q i m"), sd)
    sel_ld = p.dma("sync", negb[:, :], io["negblk_d"], sd)

    mz = 0
    for z in range(16):
        mz = p.emit("vector", lambda e, z=z: e.memset(S[:, z * 4:(z + 1) * 4, :], 0.0), sem=ve)

    tanh_at = [0] * 64
    Sread_po = [(po, 0)] * 64
    Sread_ve = [(ve, mz)] * 64
    Pb_rd = [0] * 4
    Qb_rd = [0] * 4
    tf_rd = [0] * 2
    sf_rd = [0] * 2
    pd_rd = [0, 0, 0, 0]
    pq_rd = [[], []]
    po_g = [[], []]
    x3_dma = [0, 0, 0, 0]
    of_rd = [0, 0]
    of2_rd = [0, 0]
    kld_prev = [0, 0]
    kh_rd = [[], [], []]
    vb_rd = [[], [], []]
    qb_rd = [[], [], []]

    for t in range(8):
        r0 = t * 512
        xld = []
        for ls in range(4):
            xld.append(p.dma("sync", xrb[:, ls, :], l1in[r0 + ls * 128:r0 + (ls + 1) * 128, :], sd,
                             waits=po_g[0] + po_g[1]))
        kld = [None, None, None]
        vld = [None, None, None]
        qld = [None, None, None]
        out_q = []

        def flush_out(item):
            (cc_, ovv_) = item
            last_tp = 0
            for ls in range(4):
                pob = (po0, po1)[ls % 2]
                tv2 = p.emit("tensor", lambda e, ls=ls, pob=pob, oc=cc_ % 2: e.transpose(
                    pob[:, 0:128], of2[:, oc, ls * 128:(ls + 1) * 128], sb["id_f32"][:, :]),
                    sem=pe, waits=[(ve, ovv_)] + po_g[ls % 2])
                last_tp = tv2
                av = p.emit("vector", lambda e, ls=ls, c=cc_, pob=pob: e.tensor_tensor(
                    x3t[:, ls, c * 128:(c + 1) * 128], xrb[:, ls, c * 128:(c + 1) * 128],
                    pob[:, 0:128], ALU.add), sem=ve,
                    waits=[(pe, tv2), (sd, xld[ls]), (rd, x3_dma[ls])])
                po_g[ls % 2] = [(ve, av)]
            of2_rd[cc_ % 2] = last_tp

        for c in range(8):
            cs = (t * 8 + c) % 3
            kld[cs] = p.dma("sync", kh[:, cs, :], io["khsp"][t, c, :, :], p.r2,
                            waits=kh_rd[cs])
            vld[cs] = p.dma("gpsimd", vb[:, cs, :], io["vsp"][t, c, :, :], gd,
                            waits=vb_rd[cs])
            qld[cs] = p.dma("gpsimd", qb[:, cs, :], io["qsp"][t, c, :, :], p.g2,
                            waits=qb_rd[cs])
            pqb = (pq0, pq1)[c % 2]
            qv_q = []
            mq_q = []
            for i in range(10):
                if i < 8:
                    ci = c * 8 + i
                    ps = i % 4
                    pdb = (pd0, pd1)[i % 2]
                    pv = p.emit("gpsimd", lambda e, ci=ci, cs=cs, ps=ps: e.tensor_tensor(
                        Pb[:, ps, :], S[:, ci, :], kh[:, cs, :], ALU.mult), sem=po,
                        waits=[(p.r2, kld[cs]), (se, tanh_at[ci]), (pe, Pb_rd[ps])]
                        + ([(ve, mz)] if t == 0 else []))
                    Sread_po[ci] = (po, pv)
                    m1v = p.emit("tensor", lambda e, i=i, cs=cs, pdb=pdb: e.matmul(
                        pdb[:, :], selv[:, i, :], vb[:, cs, :], start=True, stop=False),
                        sem=pe, waits=[(gd, vld[cs]), (po, pv), (sd, sel_ld), (ve, pd_rd[i % 2])])
                    m2v = p.emit("tensor", lambda e, i=i, ps=ps, pdb=pdb: e.matmul(
                        pdb[:, :], negb[:, :], Pb[:, ps, :], start=False, stop=True), sem=pe)
                    Pb_rd[ps] = m2v
                    tv = p.emit("vector", lambda e, cs=cs, pdb=pdb, ts=i % 2: e.tensor_tensor(
                        tf[:, ts, :], pdb[:, :], kh[:, cs, :], ALU.mult), sem=ve,
                        waits=[(pe, m2v), (ve, tf_rd[i % 2])])
                    pd_rd[i % 2] = tv
                    sv = p.emit("vector", lambda e, ci=ci, ts=i % 2: e.tensor_add(
                        sf[:, ts, :], S[:, ci, :], tf[:, ts, :]), sem=ve,
                        waits=[(ve, tv), (se, tanh_at[ci]), (se, sf_rd[i % 2])])
                    tf_rd[i % 2] = sv
                    th = p.emit("scalar", lambda e, ci=ci, ts=i % 2: e.activation(
                        S[:, ci, :], sf[:, ts, :], AF.Tanh), sem=se,
                        waits=[(ve, sv)] + [Sread_po[ci]])
                    sf_rd[i % 2] = th
                    tanh_at[ci] = th
                    qv_q.append((i, ci, th))
                if len(qv_q) > (1 if i < 8 else 0):
                    (iq, ciq, thq) = qv_q.pop(0)
                    psq = iq % 4
                    qv = p.emit("vector", lambda e, ci=ciq, cs=cs, ps=psq: e.tensor_tensor(
                        Qb[:, ps, :], S[:, ci, :], qb[:, cs, :], ALU.mult), sem=ve,
                        waits=[(se, thq), (p.g2, qld[cs]), (pe, Qb_rd[psq])])
                    mq_q.append((iq, psq, qv))
                if len(mq_q) > (1 if i < 9 else 0):
                    (im, psm, qvm) = mq_q.pop(0)
                    mq = p.emit("tensor", lambda e, i=im, ps=psm, pqb=pqb: e.matmul(
                        pqb[:, :], selq[:, i, :], Qb[:, ps, :], start=(i == 0), stop=(i == 7)),
                        sem=pe, waits=[(ve, qvm)] + (pq_rd[c % 2] if im == 0 else []))
                    Qb_rd[psm] = mq
            mmq = pe.n
            ssl = p.emit("scalar", lambda e, pqb=pqb, oc=c % 2: e.activation(
                of[:, oc, :], pqb[:, :], AF.Silu),
                sem=se, waits=[(pe, mmq), (ve, of_rd[c % 2])])
            ovv = p.emit("vector", lambda e, pqb=pqb, oc=c % 2: e.tensor_tensor(
                of2[:, oc, :], of[:, oc, :], pqb[:, :], ALU.mult), sem=ve,
                waits=[(se, ssl), (pe, of2_rd[c % 2])])
            of_rd[c % 2] = ovv
            pq_rd[c % 2] = [(ve, ovv), (se, ssl)]
            kh_rd[cs] = [(ve, ve.n), (po, po.n)]
            vb_rd[cs] = [(pe, pe.n)]
            qb_rd[cs] = [(ve, ve.n)]
            out_q.append((c, ovv))
            if len(out_q) > (1 if c < 7 else 0):
                flush_out(out_q.pop(0))
        while out_q:
            flush_out(out_q.pop(0))
        for ls in range(4):
            x3_dma[ls] = p.dma("sync", x3buf[r0 + ls * 128:r0 + (ls + 1) * 128, :], x3t[:, ls, :], rd,
                               waits=[(ve, ve.n)])
    if io.get("dump_S"):
        for ci in range(64):
            p.dma("sync", io["sdump"][ci, :, :], S[:, ci, :], rd,
                  waits=[(se, se.n), (ve, ve.n)])
    if _ser:
        p.emit = _emit0



def build_phase5(p, io, E, fused):
    nc = p.nc
    x3buf, yout = io["x3buf"], io["yout"]
    sb = io["sb"]
    sd, gd, rd, pe, ve, se, po, cc = p.sd, p.gd, p.rd, p.pe, p.ve, p.se, p.po, p.cc

    wf = E(nc.sbuf_tensor("p5_wf", [128, 8, 1024], F32))
    wq8 = E(nc.sbuf_tensor("p5_wq8", [128, 8, 1024], FP8))
    xq8 = E(nc.sbuf_tensor("p5_xq8", [128, 8, 512], FP8))
    xrb = E(nc.sbuf_tensor("p5_xrb", [128, 4, 1024], F32))
    xn5 = E(nc.sbuf_tensor("p5_xn5", [128, 2, 1024], F32))
    tsc = E(nc.sbuf_tensor("p5_t", [128, 2, 512], F32))
    ytile = E(nc.sbuf_tensor("p5_y", [128, 4, 1024], F32))
    bias_sb = E(nc.sbuf_tensor("p5_bias", [128, 8], F32))
    stsum = E(nc.sbuf_tensor("p5_stsum", [128, 32], F32))
    stsq = E(nc.sbuf_tensor("p5_stsq", [128, 32], F32))
    mu5 = E(nc.sbuf_tensor("p5_mu", [128, 32], F32))
    rstd5 = E(nc.sbuf_tensor("p5_rstd", [128, 32], F32))
    tmp5 = E(nc.sbuf_tensor("p5_tmp", [128, 32], F32))
    am5 = E(nc.sbuf_tensor("p5_am", [128, 34], F32))
    wam = E(nc.sbuf_tensor("p5_wam", [128, 9], F32))
    sqd = E(nc.sbuf_tensor("p5_sqd", [128, 1024], BF16))
    scal = E(nc.sbuf_tensor("p5_scal", [1, 8], F32))
    scb = E(nc.sbuf_tensor("p5_scb", [128, 2], F32))

    pa0 = E(nc.psum_tensor("p5_pa0", [128, 512], F32))
    pa1 = E(nc.psum_tensor("p5_pa1", [128, 512], F32))
    po0 = E(nc.psum_tensor("p5_po0", [128, 512], F32))
    po1 = E(nc.psum_tensor("p5_po1", [128, 512], F32))
    pt = E(nc.psum_tensor("p5_pt", [1, 128], F32))

    p.dma("sync", bias_sb[:, :], io["bias_d"], sd)
    wld = p.dma("sync", wf[:, :, :], io["wfT"].rearrange("(c q) o -> q c o", q=128), sd)

    wv_ = []
    for c in range(8):
        wv_.append(p.emit("vector", lambda e, c=c: e.reduce_max(
            wam[:, c:c + 1], wf[:, c, :], AX.X, apply_absolute_value=True), sem=ve,
            waits=[(sd, wld)] if c == 0 else []))
    wm1 = p.emit("vector", lambda e: e.reduce_max(wam[:, 8:9], wam[:, 0:8], AX.X), sem=ve,
                 waits=[(ve, wv_[-1])])
    wtp = p.emit("tensor", lambda e: e.transpose(pt[:, :], wam[:, 8:9], sb["id_f32"][:, :]),
                 sem=pe, waits=[(ve, wm1)])
    wm2 = p.emit("vector", lambda e: e.reduce_max(scal[:, 1:2], pt[0:1, :], AX.X), sem=ve,
                 waits=[(pe, wtp)])

    vst, sst = [], []
    for si in range(32):
        slot = si % 2
        w = [(ve, vst[si - 2]), (se, sst[si - 2])] if si >= 2 else [(ve, ve.n), (se, se.n)]
        ld = p.dma("sync", xrb[:, slot, :], x3buf[si * 128:(si + 1) * 128, :], sd, waits=w)
        v = p.emit("vector", lambda e, s=slot, si=si: e.reduce_sum(stsum[:, si:si + 1], xrb[:, s, :], AX.X),
                   sem=ve, waits=[(sd, ld)])
        sv = p.emit("scalar", lambda e, s=slot, si=si: e.activation(sqd[:], xrb[:, s, :], AF.Square,
                                                                    accum_out=stsq[:, si:si + 1]),
                    sem=se, waits=[(sd, ld)])
        vst.append(v)
        sst.append(sv)
    v1 = p.emit("vector", lambda e: e.tensor_scalar_mul(mu5[:], stsum[:], 1.0 / D), sem=ve,
                waits=[(ve, vst[-1])])
    v2 = p.emit("vector", lambda e: e.tensor_scalar_mul(tmp5[:], stsq[:], 1.0 / D), sem=ve,
                waits=[(se, sst[-1]), (ve, v1)])
    v3 = p.emit("vector", lambda e: e.tensor_mul(rstd5[:], mu5[:], mu5[:]), sem=ve, waits=[(ve, v2)])
    v4 = p.emit("vector", lambda e: e.tensor_sub(tmp5[:], tmp5[:], rstd5[:]), sem=ve, waits=[(ve, v3)])
    sq1 = p.emit("scalar", lambda e: e.activation(tmp5[:], tmp5[:], AF.Sqrt, bias=sb["fc"][:, 0:1]),
                 sem=se, waits=[(ve, v4)])
    st_done = p.emit("vector", lambda e: e.reciprocal(rstd5[:], tmp5[:]), sem=ve, waits=[(se, sq1)])

    amv = []
    for si in range(32):
        slot = si % 2
        ld = p.dma("sync", xrb[:, slot, :], x3buf[si * 128:(si + 1) * 128, :], sd,
                   waits=[(ve, ve.n), (se, se.n)])
        nv = p.emit("vector", lambda e, s=slot, si=si: e.tensor_scalar(
            xn5[:, s, :], xrb[:, s, :], mu5[:, si:si + 1], rstd5[:, si:si + 1],
            ALU.subtract, ALU.mult), sem=ve, waits=[(sd, ld), (ve, st_done)])
        av = p.emit("vector", lambda e, s=slot, si=si: e.reduce_max(
            am5[:, si:si + 1], xn5[:, s, :], AX.X, apply_absolute_value=True), sem=ve,
            waits=[(ve, nv)])
        amv.append(av)
    am1 = p.emit("vector", lambda e: e.reduce_max(am5[:, 32:33], am5[:, 0:32], AX.X), sem=ve,
                 waits=[(ve, amv[-1])])
    atp = p.emit("tensor", lambda e: e.transpose(pt[:, :], am5[:, 32:33], sb["id_f32"][:, :]),
                 sem=pe, waits=[(ve, am1), (ve, wm2)])
    am2 = p.emit("vector", lambda e: e.reduce_max(scal[:, 0:1], pt[0:1, :], AX.X), sem=ve,
                 waits=[(pe, atp)])

    if fused:
        dm = p.dma("gpsimd", io["amx_in"], scal[:, 0:1], gd, waits=[(ve, am2)])
        ccv = p.emit("gpsimd", lambda e: e.collective_compute(
            "AllReduce", ALU.max, replica_groups=[list(range(NCORES))],
            ins=[io["amx_in_t"].ap().opt()], outs=[io["amx_out_t"].ap().opt()]),
            sem=cc, waits=[(gd, dm)])
        am3 = p.dma("sync", scal[:, 0:1], io["amx_out"], sd, waits=[(cc, ccv)])
    else:
        am3 = sd.n

    c1 = p.emit("vector", lambda e: e.tensor_scalar_max(scal[:, 0:1], scal[:, 0:1], 1e-12), sem=ve,
                waits=[(sd, am3), (ve, am2)])
    c2 = p.emit("vector", lambda e: e.reciprocal(scal[:, 2:3], scal[:, 0:1]), sem=ve, waits=[(ve, c1)])
    c3 = p.emit("vector", lambda e: e.tensor_scalar_mul(scal[:, 2:3], scal[:, 2:3], 224.0), sem=ve,
                waits=[(ve, c2)])
    c4 = p.emit("vector", lambda e: e.reciprocal(scal[:, 3:4], scal[:, 1:2]), sem=ve, waits=[(ve, c3)])
    c5 = p.emit("vector", lambda e: e.tensor_scalar_mul(scal[:, 3:4], scal[:, 3:4], 224.0), sem=ve,
                waits=[(ve, c4)])
    c6 = p.emit("vector", lambda e: e.tensor_tensor(scal[:, 4:5], scal[:, 2:3], scal[:, 3:4], ALU.mult),
                sem=ve, waits=[(ve, c5)])
    c7 = p.emit("vector", lambda e: e.reciprocal(scal[:, 4:5], scal[:, 4:5]), sem=ve, waits=[(ve, c6)])
    b1 = p.emit("tensor", lambda e: e.matmul(po0[:, 0:2], sb["ones_row"][:, :],
                                             scal[:, 2:5:2], start=True, stop=True),
                sem=pe, waits=[(ve, c7)])
    b2 = p.emit("vector", lambda e: e.tensor_copy(scb[:, :], po0[:, 0:2]), sem=ve, waits=[(pe, b1)])

    b3 = p.emit("tensor", lambda e: e.matmul(po1[:, 0:1], sb["ones_row"][:, :], scal[:, 3:4],
                                             start=True, stop=True), sem=pe, waits=[(ve, c7)])
    b4 = p.emit("vector", lambda e: e.tensor_copy(wam[:, 0:1], po1[:, 0:1]), sem=ve, waits=[(pe, b3)])
    wq_done = []
    for c in range(8):
        wq_done.append(p.emit("vector", lambda e, c=c: e.tensor_scalar_mul(
            wq8[:, c, :], wf[:, c, :], wam[:, 0:1]), sem=ve,
            waits=[(ve, b4)] if c == 0 else []))

    po_guard = {id(po0): [(ve, b2)], id(po1): [(ve, b4)]}
    pa_guard = {id(pa0): [], id(pa1): []}
    y_dma = [0, 0, 0, 0]
    for rg in range(8):
        r0 = rg * 512
        xld = []
        for ls in range(4):
            ld = p.dma("sync", xrb[:, ls, :], x3buf[r0 + ls * 128:r0 + (ls + 1) * 128, :], sd,
                       waits=[(ve, ve.n)])
            xld.append(ld)
            nv = p.emit("vector", lambda e, ls=ls, si=rg * 4 + ls: e.tensor_scalar(
                xn5[:, ls % 2, :], xrb[:, ls, :], mu5[:, si:si + 1], rstd5[:, si:si + 1],
                ALU.subtract, ALU.mult), sem=ve, waits=[(sd, ld), (ve, st_done), (pe, pe.n)])
            for c in range(8):
                pob = (po0, po1)[c % 2]
                tv = p.emit("tensor", lambda e, ls=ls, c=c, pob=pob: e.transpose(
                    pob[:, 0:128], xn5[:, ls % 2, c * 128:(c + 1) * 128], sb["id_f32"][:, :]),
                    sem=pe, waits=[(ve, nv)] + po_guard[id(pob)])
                qv = p.emit("vector", lambda e, ls=ls, c=c, pob=pob: e.tensor_scalar_mul(
                    xq8[:, c, ls * 128:(ls + 1) * 128], pob[:, 0:128], scb[:, 0:1]),
                    sem=ve, waits=[(pe, tv), (ve, b2)])
                po_guard[id(pob)] = [(ve, qv)]
        q_done = ve.n
        for m in range(8):
            pa = (pa0, pa1)[m % 2]
            for kk in range(8):
                w = []
                if kk == 0:
                    w = pa_guard[id(pa)] + [(ve, q_done)]
                    if rg == 0 and m == 0:
                        w.append((ve, wq_done[-1]))
                p.emit("tensor", lambda e, m=m, kk=kk, pa=pa: e.matmul(
                    pa[:, :], wq8[:, kk, m * 128:(m + 1) * 128], xq8[:, kk, :],
                    start=(kk == 0), stop=(kk == 7)), sem=pe, waits=w)
            mm = pe.n
            ts = m % 2
            sv = p.emit("vector", lambda e, m=m, pa=pa, ts=ts: e.tensor_scalar(
                tsc[:, ts, :], pa[:, :], scb[:, 1:2], bias_sb[:, m:m + 1], ALU.mult, ALU.add),
                sem=ve, waits=[(pe, mm), (pe, pe.n), (ve, ve.n)])
            pa_guard[id(pa)] = [(ve, sv)]
            for wq_ in range(4):
                ls = wq_
                pob = (po0, po1)[wq_ % 2]
                tv2 = p.emit("tensor", lambda e, ts=ts, wq_=wq_, pob=pob: e.transpose(
                    pob[:, 0:128], tsc[:, ts, wq_ * 128:(wq_ + 1) * 128], sb["id_f32"][:, :]),
                    sem=pe, waits=[(ve, sv)] + po_guard[id(pob)])
                av = p.emit("vector", lambda e, ls=ls, m=m, pob=pob: e.tensor_tensor(
                    ytile[:, ls, m * 128:(m + 1) * 128], xrb[:, ls, m * 128:(m + 1) * 128],
                    pob[:, 0:128], ALU.add), sem=ve,
                    waits=[(pe, tv2), (p.od, y_dma[ls])])
                po_guard[id(pob)] = [(ve, av)]
        for ls in range(4):
            y_dma[ls] = p.dma("sync", yout[r0 + ls * 128:r0 + (ls + 1) * 128, :], ytile[:, ls, :],
                              p.od, waits=[(ve, ve.n)])



def build_nc(phases=("p1",), dev_out=(), l1_external=False, fused_amax=True, dump_S=False):
    nc = bass.Bass("TRN2", target_bir_lowering=False, debug=False)
    ctx = contextlib.ExitStack()
    io = {}

    def dram(name, shape, dtype, kind="Internal"):
        t = nc.dram_tensor(name, shape, dtype, kind=kind)
        io[name] = t.ap()
        io[name + "_t"] = t

    has_l0 = "p1" in phases or "p2" in phases
    has_l1 = "p4a" in phases or "p4b" in phases or "p5" in phases

    for n, sh, dt in (("pshift_d", [128, 128], BF16), ("tri_d", [128, 128], F32),
                      ("id_bf_d", [128, 128], BF16), ("id_f32_d", [128, 128], F32),
                      ("ones_row_d", [1, 128], F32), ("ones_col_bf_d", [128, 1], BF16),
                      ("lnw_d", [128, 8], F32), ("lnb_d", [128, 8], F32),
                      ("fc_d", [128, 4], F32), ("sel4_d", [4, 4, 128], F32)):
        dram(n, sh, dt, kind="ExternalInput")

    if has_l0:
        dram("xin", [SEQ, D], F32, kind="ExternalInput")
        dram("xinT", [D, SEQ], F32, kind="ExternalInput")
        for n, sh in (("Wq", [D, D]), ("Wk", [D, D]), ("Wv", [D, D]), ("Wo", [D, D]),
                      ("Wg1", [D, 64]), ("Wg2", [64, D])):
            dram(n, sh, F32, kind="ExternalInput")
        dram("cosd", [128, SEQ], BF16, kind="ExternalInput")
        dram("sind", [128, SEQ], BF16, kind="ExternalInput")
        dram("resid", [SEQ, D], F32,
             kind="ExternalOutput" if "resid" in dev_out else "Internal")
        dram("kspill", [8, 128, SEQ], BF16)
        dram("vspill", [8, 128, 32, 128], BF16)
        dram("w12P", [32, 16, 128, 128], F32, kind="ExternalInput")
        dram("h1P", [32, 32, 128, 128], F32, kind="ExternalInput")
        dram("h2P", [32, 32, 128, 128], F32, kind="ExternalInput")
        dram("w3P", [8, 32, 128, 128], F32, kind="ExternalInput")
        for n in ("s1_d", "s2_d", "hs1_d", "hs2_d"):
            dram(n, [128, 32], F32, kind="ExternalInput")
        dram("s3_d", [128, 8], F32, kind="ExternalInput")

    if has_l1:
        for n in ("wkT", "wvT", "wqT", "wfT"):
            dram(n, [D, D], F32, kind="ExternalInput")
        dram("bias_d", [128, 8], F32, kind="ExternalInput")
        dram("posblk_d", [128, 128], F32, kind="ExternalInput")
        dram("selv_d", [8, 128, 128], BF16, kind="ExternalInput")
        dram("selq_d", [8, 128, 128], BF16, kind="ExternalInput")
        dram("negblk_d", [128, 128], BF16, kind="ExternalInput")
        dram("khsp", [8, 8, 128, 512], F32,
             kind="ExternalOutput" if "khsp" in dev_out else "Internal")
        dram("vsp", [8, 8, 128, 512], BF16,
             kind="ExternalOutput" if "vsp" in dev_out else "Internal")
        dram("qsp", [8, 8, 128, 512], BF16,
             kind="ExternalOutput" if "qsp" in dev_out else "Internal")
        dram("x3buf", [SEQ, D], F32,
             kind="ExternalOutput" if "x3buf" in dev_out else "Internal")
        dram("sdump", [64, 128, 512], F32,
             kind="ExternalOutput" if "sdump" in dev_out else "Internal")
        dram("amx_in", [1, 1], F32)
        dram("amx_out", [1, 1], F32)
        dram("yout", [SEQ, D], F32,
             kind="ExternalOutput" if "yout" in dev_out else "Internal")
        if l1_external:
            dram("xl1", [SEQ, D], F32, kind="ExternalInput")
            io["l1in"] = io["xl1"]
        else:
            dram("a2aout", [SEQ, D], F32)
            io["l1in"] = io["a2aout"]

    with ctx:
        E = ctx.enter_context
        p = Prog(nc, ctx)
        sb = {}
        sb["id_bf"] = E(nc.sbuf_tensor("c_idbf", [128, 128], BF16))
        sb["pshift"] = E(nc.sbuf_tensor("c_pshift", [128, 128], BF16))
        sb["tri"] = E(nc.sbuf_tensor("c_tri", [128, 128], F32))
        sb["ones_row"] = E(nc.sbuf_tensor("c_onesrow", [1, 128], F32))
        sb["ones_col_bf"] = E(nc.sbuf_tensor("c_onescol", [128, 1], BF16))
        sb["lnw"] = E(nc.sbuf_tensor("c_lnw", [128, 8], F32))
        sb["lnb"] = E(nc.sbuf_tensor("c_lnb", [128, 8], F32))
        sb["fc"] = E(nc.sbuf_tensor("c_fc", [128, 4], F32))
        sb["id_f32"] = E(nc.sbuf_tensor("c_idf32", [128, 128], F32))
        sb["sel4"] = E(nc.sbuf_tensor("c_sel4", [4, 4, 128], F32))
        io["sb"] = sb
        for name, d in (("id_bf", "id_bf_d"), ("pshift", "pshift_d"), ("tri", "tri_d"),
                        ("ones_row", "ones_row_d"), ("ones_col_bf", "ones_col_bf_d"),
                        ("lnw", "lnw_d"), ("lnb", "lnb_d"), ("fc", "fc_d"),
                        ("id_f32", "id_f32_d"), ("sel4", "sel4_d")):
            p.dma("sync", sb[name][tuple([slice(None)] * 2)], io[d], p.sd)

        if "p1" in phases:
            with contextlib.ExitStack() as phase_ctx:
                build_phase1(p, io, phase_ctx.enter_context)
            p.barrier()

        if "p2" in phases:
            with contextlib.ExitStack() as phase_ctx:
                build_phase2(p, io, phase_ctx.enter_context)
            p.barrier()

        if "p3" in phases:
            ccv = p.emit("gpsimd", lambda e: e.collective_compute(
                "AllToAll", ALU.bypass, replica_groups=[list(range(NCORES))],
                ins=[io["resid_t"].ap().opt()], outs=[io["a2aout_t"].ap().opt()]),
                sem=p.cc, waits=p.wait_all())
            p.barrier()

        if "p4a" in phases:
            with contextlib.ExitStack() as phase_ctx:
                build_phase4a(p, io, phase_ctx.enter_context)
            p.barrier()

        io["dump_S"] = dump_S
        if "p4b" in phases:
            with contextlib.ExitStack() as phase_ctx:
                build_phase4b(p, io, phase_ctx.enter_context)
            p.barrier()

        if "p5" in phases:
            with contextlib.ExitStack() as phase_ctx:
                build_phase5(p, io, phase_ctx.enter_context, fused_amax)
            p.barrier()

        vals = tuple(p.wait_all())
        with nc.Block() as block:
            p.realize(block)

            @block.sync
            def _(e):
                for sm, v in vals:
                    e.wait_ge(sm.h, v)
    return nc


def make_inmaps(inputs, consts=None, weights=None, l0=True, l1=True, xl1=None):
    consts = consts or host_consts()
    weights = weights or host_weights(inputs)
    base = {
        "pshift_d": consts["pshift"], "tri_d": consts["tri"], "id_bf_d": consts["id_bf"],
        "id_f32_d": consts["id_f32"], "ones_row_d": consts["ones_row"],
        "ones_col_bf_d": consts["ones_col_bf"], "lnw_d": weights["lnw"],
        "lnb_d": weights["lnb"], "fc_d": consts["fconsts"], "sel4_d": consts["sel4"],
    }
    if l0:
        base.update({
            "Wq": weights["L0_attn_Wq"], "Wk": weights["L0_attn_Wk"], "Wv": weights["L0_attn_Wv"],
            "Wo": weights["L0_attn_Wo"], "Wg1": weights["L0_attn_Wg1"], "Wg2": weights["L0_attn_Wg2"],
            "cosd": consts["cosd"], "sind": consts["sind"],
            "w12P": weights["w12P"], "h1P": weights["h1P"],
            "h2P": weights["h2P"], "w3P": weights["w3P"],
            "s1_d": weights["s1"], "s2_d": weights["s2"], "hs1_d": weights["hs1"],
            "hs2_d": weights["hs2"], "s3_d": weights["s3"],
        })
    if l1:
        base.update({
            "wkT": weights["wkT"], "wvT": weights["wvT"], "wqT": weights["wqT"],
            "wfT": weights["wfT"], "bias_d": weights["bias"],
            "posblk_d": consts["posblk"], "selv_d": consts["selv"], "selq_d": consts["selq"],
            "negblk_d": consts["negblk"],
        })
    in_maps = []
    x = np.asarray(inputs["L0_attn_x"], dtype=np.float32) if l0 else None
    for c in range(NCORES):
        m = dict(base)
        if l0:
            m["xin"] = np.ascontiguousarray(x[c])
            m["xinT"] = np.ascontiguousarray(x[c].T)
        if xl1 is not None:
            m["xl1"] = np.ascontiguousarray(xl1[c])
        in_maps.append(m)
    return in_maps


def kernel(**inputs):
    consts = host_consts()
    weights = host_weights(inputs)
    with allow_ici_all_to_all():
        nc = build_nc(phases=("p1", "p2", "p3", "p4a", "p4b", "p5"), dev_out=("yout",),
                      l1_external=False)
        in_maps = make_inmaps(inputs, consts, weights, l0=True, l1=True)
        res = run_bass_kernel_spmd(nc, in_maps, core_ids=list(range(NCORES)))
    out = np.empty((NCORES, SEQ, D), dtype=np.float32)
    for c in range(NCORES):
        out[:, c * LANES:(c + 1) * LANES, :] = res.results[c]["yout"].reshape(NCORES, LANES, D)
    return out


# version 26
# speedup vs baseline: 1.0364x; 1.0364x over previous
import contextlib
import numpy as np
import ml_dtypes
import concourse.bass as bass
import concourse.mybir as mybir
from concourse.bass_utils import run_bass_kernel_spmd
from concourse.aot_env import allow_ici_all_to_all

F32 = mybir.dt.float32
BF16 = mybir.dt.bfloat16
FP8 = mybir.dt.float8e4
AF = mybir.ActivationFunctionType
ALU = mybir.AluOpType
AX = mybir.AxisListType

NCORES = 8
D = 1024
SEQ = 4096
HEADS = 16
HD = 64
DHID = 4096
LANES = SEQ // NCORES
EPS = 1e-5

ENGINES = ("sync", "gpsimd", "tensor", "vector", "scalar")



def host_consts():
    c = {}
    inv = (1.0 / (10000.0 ** (np.arange(0, HD, 2, dtype=np.float32) / np.float32(HD)))).astype(np.float32)
    pos = np.arange(SEQ, dtype=np.float32)
    ang = pos[:, None] * inv[None, :]
    cos = np.cos(ang).astype(np.float32)
    sin = np.sin(ang).astype(np.float32)
    cos64 = np.concatenate([cos, cos], axis=1).T
    sin64 = np.concatenate([sin, sin], axis=1).T
    c["cosd"] = np.ascontiguousarray(np.vstack([cos64, cos64])).astype(ml_dtypes.bfloat16)
    c["sind"] = np.ascontiguousarray(np.vstack([sin64, sin64])).astype(ml_dtypes.bfloat16)
    P = np.zeros((128, 128), dtype=np.float32)
    for b in (0, 64):
        for m in range(32):
            P[b + m + 32, b + m] = -1.0
            P[b + m, b + m + 32] = 1.0
    c["pshift"] = P.astype(ml_dtypes.bfloat16)
    tri = (np.arange(128)[None, :] >= np.arange(128)[:, None]).astype(np.float32)
    c["tri"] = tri
    c["id_bf"] = np.eye(128, dtype=np.float32).astype(ml_dtypes.bfloat16)
    c["id_f32"] = np.eye(128, dtype=np.float32)
    c["ones_row"] = np.ones((1, 128), dtype=np.float32)
    c["ones_col_bf"] = np.ones((128, 1), dtype=np.float32).astype(ml_dtypes.bfloat16)
    fc = np.zeros((128, 4), dtype=np.float32)
    fc[:, 0] = 1e-5
    fc[:, 1] = 1e-6
    fc[:, 2] = 0.5
    fc[:, 3] = -0.5
    c["fconsts"] = fc
    selv = np.zeros((8, 128, 128), dtype=np.float32)
    selq = np.zeros((8, 128, 128), dtype=np.float32)
    negblk = np.zeros((128, 128), dtype=np.float32)
    for pp in range(128):
        for m in range(128):
            if pp // 8 == m // 8:
                negblk[pp, m] = -1.0
                selq[m % 8, pp, m] = 1.0
    for i in range(8):
        for m in range(128):
            selv[i, (m // 8) * 8 + i, m] = 1.0
    c["selv"] = selv.astype(ml_dtypes.bfloat16)
    c["selq"] = selq.astype(ml_dtypes.bfloat16)
    c["negblk"] = negblk.astype(ml_dtypes.bfloat16)
    c["posblk"] = (-negblk).astype(np.float32)
    sel4 = np.zeros((4, 4, 128), dtype=np.float32)
    for i in range(4):
        sel4[i, i, :] = 1.0
    c["sel4"] = sel4
    return c


def host_weights(inputs):
    w = {}
    t = lambda a: np.ascontiguousarray(np.asarray(a, dtype=np.float32).T)
    for n in ("L0_attn_Wq", "L0_attn_Wk", "L0_attn_Wv", "L0_attn_Wo", "L0_attn_Wg1", "L0_attn_Wg2"):
        w[n] = np.ascontiguousarray(np.asarray(inputs[n], dtype=np.float32))
    def pack(a):
        K, M = a.shape
        return np.ascontiguousarray(a.reshape(K // 128, 128, M // 128, 128).transpose(2, 0, 1, 3))
    w["w1P"] = pack(t(inputs["L0_ffn_w1_w"]))
    w["w2P"] = pack(t(inputs["L0_ffn_w2_w"]))
    w["w12P"] = np.ascontiguousarray(np.concatenate(
        [w["w1P"], w["w2P"]], axis=1))
    w["h1P"] = pack(t(inputs["L0_ffn_h1_w"]))
    w["h2P"] = pack(t(inputs["L0_ffn_h2_w"]))
    w["w3P"] = pack(t(inputs["L0_ffn_w3_w"]))
    pc = lambda a: np.ascontiguousarray(np.asarray(a, dtype=np.float32).reshape(-1, 128).T)
    w["s1"] = pc(inputs["L0_ffn_w1_s"])
    w["s2"] = pc(inputs["L0_ffn_w2_s"])
    w["hs1"] = pc(inputs["L0_ffn_h1_s"])
    w["hs2"] = pc(inputs["L0_ffn_h2_s"])
    w["s3"] = pc(inputs["L0_ffn_w3_s"])
    w["lnw"] = pc(inputs["L0_attn_ln_w"])
    w["lnb"] = pc(inputs["L0_attn_ln_b"])
    w["wkT"] = t(inputs["L1_attn_W_k"])
    w["wvT"] = t(inputs["L1_attn_W_v"])
    w["wqT"] = t(inputs["L1_attn_W_q"])
    w["wfT"] = t(inputs["L1_ffn_weight"])
    w["bias"] = pc(inputs["L1_ffn_bias"])
    return w



class Sem:
    __slots__ = ("h", "n", "name")

    def __init__(self, h, name):
        self.h = h
        self.n = 0
        self.name = name


class Prog:
    SEMS = ("sd", "gd", "rd", "pe", "ve", "se", "po", "od", "cc", "s2", "g2", "r2", "g3")

    def __init__(self, nc, ctx):
        self.nc = nc
        self.ctx = ctx
        self.st = {e: [] for e in ENGINES}
        for name in self.SEMS:
            setattr(self, name, Sem(ctx.enter_context(nc.semaphore(name)), name))

    def emit(self, eng, fn, sem=None, inc=1, waits=()):
        waits = tuple(waits)
        if sem is not None:
            sem.n += inc
        semh = sem.h if sem is not None else None

        def run(e, fn=fn, waits=waits, semh=semh, inc=inc):
            for s, v in waits:
                e.wait_ge(s.h, v)
            ins = fn(e)
            if semh is not None:
                ins.then_inc(semh, inc)
        self.st[eng].append(run)
        return sem.n if sem is not None else None

    def dma(self, q, out, in_, sem, waits=()):
        gate = (sem, sem.n)
        return self.emit(q, lambda e: e.dma_start(out=out, in_=in_), sem=sem, inc=16,
                         waits=tuple(waits) + (gate,))

    def wait_all(self):
        return [(getattr(self, n), getattr(self, n).n) for n in self.SEMS if getattr(self, n).n > 0]

    def barrier(self):
        vals = tuple(self.wait_all())
        for eng in ENGINES:
            def run(e, vals=vals):
                for s, v in vals:
                    e.wait_ge(s.h, v)
            self.st[eng].append(run)

    def realize(self, block):
        for eng in ENGINES:
            ops = self.st[eng]
            if ops:
                getattr(block, eng)(lambda e, ops=ops: [f(e) for f in ops])



def build_phase1(p, io, E):
    nc = p.nc
    xin, resid, kspill = io["xin"], io["resid"], io["kspill"]
    sd, gd, rd, pe, ve, se, po = p.sd, p.gd, p.rd, p.pe, p.ve, p.se, p.po

    wbuf = E(nc.sbuf_tensor("p1_wbuf", [128, 16, 1024], BF16))
    wg1 = E(nc.sbuf_tensor("p1_wg1", [128, 8, 64], BF16))
    wg2 = E(nc.sbuf_tensor("p1_wg2", [64, 1024], BF16))
    vbuf = E(nc.sbuf_tensor("p1_vbuf", [128, 3, 32, 128], BF16))
    vstage = E(nc.sbuf_tensor("p1_vstage", [128, 4, 512], BF16))
    kbuf = E(nc.sbuf_tensor("p1_kbuf", [128, 3, 4096], BF16))
    cosg = E(nc.sbuf_tensor("p1_cosg", [128, 2, 2, 512], BF16))
    xsub = E(nc.sbuf_tensor("p1_xsub", [128, 2, 1024], F32))
    xtf = E(nc.sbuf_tensor("p1_xtf", [128, 2, 8, 512], BF16))
    xn_fm = E(nc.sbuf_tensor("p1_xnfm", [128, 2, 8, 512], BF16))
    murow = E(nc.sbuf_tensor("p1_murow", [4, 128], F32))
    rsrow = E(nc.sbuf_tensor("p1_rsrow", [4, 128], F32))
    q_fm = E(nc.sbuf_tensor("p1_qfm", [128, 8, 512], BF16))
    g1sb = E(nc.sbuf_tensor("p1_g1", [64, 512], BF16))
    gate = E(nc.sbuf_tensor("p1_gate", [128, 8, 512], BF16))
    attn = E(nc.sbuf_tensor("p1_attn", [128, 8, 512], BF16))
    xo = E(nc.sbuf_tensor("p1_xo", [128, 8, 512], BF16))
    esb = E(nc.sbuf_tensor("p1_esb", [128, 6, 512], BF16))
    ksil = E(nc.sbuf_tensor("p1_ksil", [128, 3, 512], BF16))
    ropeA = E(nc.sbuf_tensor("p1_ropeA", [128, 2, 512], F32))
    ropeB = E(nc.sbuf_tensor("p1_ropeB", [128, 512], F32))
    lnsA = E(nc.sbuf_tensor("p1_lnsA", [128, 512], F32))
    lnsB = E(nc.sbuf_tensor("p1_lnsB", [128, 512], F32))
    sq = E(nc.sbuf_tensor("p1_sq", [128, 512], BF16))
    x1t = E(nc.sbuf_tensor("p1_x1t", [128, 2, 1024], F32))
    ssum = E(nc.sbuf_tensor("p1_ssum", [128, 32], F32))
    ssq = E(nc.sbuf_tensor("p1_ssq", [128, 32], F32))
    sqdump = xtf
    mu = E(nc.sbuf_tensor("p1_mu", [128, 32], F32))
    rstd = E(nc.sbuf_tensor("p1_rstd", [128, 32], F32))
    tmp32 = E(nc.sbuf_tensor("p1_tmp32", [128, 32], F32))

    srow = lambda i: x1t[0:1, i // 2, (i % 2) * 512:(i % 2) * 512 + 512]

    pqA0 = E(nc.psum_tensor("p1_pqA0", [128, 512], F32))
    pqA1 = E(nc.psum_tensor("p1_pqA1", [128, 512], F32))
    pqA2 = E(nc.psum_tensor("p1_pqA2", [128, 512], F32))
    pqB0 = E(nc.psum_tensor("p1_pqB0", [128, 512], F32))
    pqB1 = E(nc.psum_tensor("p1_pqB1", [128, 512], F32))
    pqB2 = E(nc.psum_tensor("p1_pqB2", [128, 512], F32))
    ov = E(nc.psum_tensor("p1_ov", [128, 512], F32))
    m1 = E(nc.psum_tensor("p1_m1", [128, 512], F32))

    sb = io["sb"]

    wk_ld = p.dma("gpsimd", wbuf[:, 0:8, :], io["Wk"].rearrange("(c p) o -> p c o", p=128), gd)
    wv_ld = p.dma("gpsimd", wbuf[:, 8:16, :], io["Wv"].rearrange("(c p) o -> p c o", p=128), gd)

    ve_tile, se_tile = [], []
    for i in range(32):
        slot = i % 2
        w = [(ve, ve_tile[i - 2]), (se, se_tile[i - 2])] if i >= 2 else []
        ld = p.dma("sync", xsub[:, slot, :], xin[i * 128:(i + 1) * 128, :], sd, waits=w)
        v = p.emit("vector", lambda e, s=slot, i=i: e.reduce_sum(ssum[:, i:i + 1], xsub[:, s, :], AX.X),
                   sem=ve, waits=[(sd, ld)])
        s = p.emit("scalar", lambda e, s=slot, i=i: e.activation(sqdump[:, 0, 0:2, :], xsub[:, s, :],
                                                                 AF.Square, accum_out=ssq[:, i:i + 1]),
                   sem=se, waits=[(sd, ld)])
        ve_tile.append(v)
        se_tile.append(s)
    v1 = p.emit("vector", lambda e: e.tensor_scalar_mul(mu[:], ssum[:], 1.0 / D), sem=ve,
                waits=[(ve, ve_tile[-1])])
    v2 = p.emit("vector", lambda e: e.tensor_scalar_mul(tmp32[:], ssq[:], 1.0 / D), sem=ve,
                waits=[(se, se_tile[-1]), (ve, v1)])
    v3 = p.emit("vector", lambda e: e.tensor_mul(rstd[:], mu[:], mu[:]), sem=ve, waits=[(ve, v2)])
    v4 = p.emit("vector", lambda e: e.tensor_sub(tmp32[:], tmp32[:], rstd[:]), sem=ve, waits=[(ve, v3)])
    s1 = p.emit("scalar", lambda e: e.activation(tmp32[:], tmp32[:], AF.Sqrt, bias=sb["fc"][:, 0:1]), sem=se,
                waits=[(ve, v4)])
    stats_done = p.emit("vector", lambda e: e.reciprocal(rstd[:], tmp32[:]), sem=ve, waits=[(se, s1)])

    xtf_rd = [0, 0]
    sqdump_se = [se.n]
    bc_rd = [[], []]
    murow_rd = [0]

    def xtf_load(g):
        xs = g % 2
        return p.dma("gpsimd", xtf[:, xs, :, :],
                     io["xinT"][:, g * 512:(g + 1) * 512].rearrange("(c q) n -> q c n", q=128),
                     gd, waits=[(ve, xtf_rd[xs]), (se, sqdump_se[0])])

    def build_xn(g, xld2):
        xs = g % 2
        tmu = p.emit("tensor", lambda e, g=g: e.transpose(
            m1[0:4, 0:128], mu[:, g * 4:(g + 1) * 4], sb["id_f32"][:, :]),
            sem=pe, waits=[(ve, stats_done), (se, proj_se_g[0]), (se, proj_se_g[1]),
                           (ve, ve.n)])
        cm = p.emit("vector", lambda e: e.tensor_copy(murow[:, :], m1[0:4, 0:128]),
                    sem=ve, waits=[(pe, tmu)])
        tru = p.emit("tensor", lambda e, g=g: e.transpose(
            m1[0:4, 128:256], rstd[:, g * 4:(g + 1) * 4], sb["id_f32"][:, :]),
            sem=pe, waits=[(ve, cm)])
        cr = p.emit("vector", lambda e: e.tensor_copy(rsrow[:, :], m1[0:4, 128:256]),
                    sem=ve, waits=[(pe, tru)])
        for si in range(4):
            p.emit("tensor", lambda e, si=si: e.matmul(
                pqA2[:, si * 128:(si + 1) * 128], sb["sel4"][:, si, :], murow[:, :],
                start=True, stop=True),
                sem=pe, waits=([(ve, cm), (ve, ve.n), (se, se.n)] + bc_rd[0]) if si == 0 else [])
            p.emit("tensor", lambda e, si=si: e.matmul(
                pqB2[:, si * 128:(si + 1) * 128], sb["sel4"][:, si, :], rsrow[:, :],
                start=True, stop=True),
                sem=pe, waits=([(ve, cr)] + bc_rd[1]) if si == 0 else [])
        bdone = pe.n
        lastv = None
        for c in range(8):
            t1 = p.emit("vector", lambda e, c=c, xs=xs: e.tensor_tensor(
                lnsA[:, :], xtf[:, xs, c, :], pqA2[:, :], ALU.subtract), sem=ve,
                waits=[(pe, bdone), (gd, xld2)] + ([(pe, pe.n)] if c == 0 else []))
            lastv = p.emit("vector", lambda e, c=c, xs=xs: e.tensor_tensor(
                xn_fm[:, xs, c, :], lnsA[:, :], pqB2[:, :], ALU.mult), sem=ve,
                waits=[(ve, t1)])
        xtf_rd[xs] = lastv
        bc_rd[0] = [(ve, lastv)]
        bc_rd[1] = [(ve, lastv)]
        murow_rd[0] = lastv
        return lastv

    def proj_rope_mm(c, cslot, xs, dst_ap, first_wait=()):
        kb = c % 2
        ks = c % 3
        mm_bank = (m1, pqB0)[kb]
        for kk in range(8):
            w = list(first_wait) if kk == 0 else []
            if kk == 0:
                w.append((se, proj_se_g[kb]))
            p.emit("tensor", lambda e, c=c, kk=kk, mm_bank=mm_bank, xs=xs: e.matmul(
                mm_bank[:, :], wbuf[:, kk, c * 128:(c + 1) * 128], xn_fm[:, xs, kk, :],
                start=(kk == 0), stop=(kk == 7)), sem=pe, waits=w)
        mmk = pe.n
        kw = kst_dma[ks]
        sv = p.emit("scalar", lambda e, ks=ks, mm_bank=mm_bank: e.activation(
            ksil[:, ks, :], mm_bank[:, :], AF.Silu),
            sem=se, waits=[(pe, mmk), (ve, ksil_ve_g[ks])] + ([kw] if kw else []))
        proj_se_g[kb] = sv
        return (c, cslot, dst_ap, kb, ks, sv)

    def proj_rope_tail(tok, cldw):
        (c, cslot, dst_ap, kb, ks, sv) = tok
        rp_bank = (pqA0, pqA1)[kb]
        rv = p.emit("tensor", lambda e, ks=ks, rp_bank=rp_bank: e.matmul(
            rp_bank[:, :], sb["pshift"][:, :], ksil[:, ks, :], start=True, stop=True),
            sem=pe, waits=[(se, sv), (ve, rope_ve_g[kb]), (se, rope_se_g[kb])])
        av = p.emit("vector", lambda e, ks=ks, kb=kb, cs=cslot: e.tensor_tensor(
            ropeA[:, kb, :], ksil[:, ks, :], cosg[:, cs, 0, :], ALU.mult),
            sem=ve, waits=[(se, sv)])
        tv = p.emit("vector", lambda e, cs=cslot, kb=kb, rp_bank=rp_bank: e.tensor_tensor(
            ropeB[:, :], rp_bank[:, :], cosg[:, cs, 1, :], ALU.mult),
            sem=ve, waits=[(pe, rv), (p.s2, cldw)])
        rope_ve_g[kb] = tv
        kv2 = p.emit("vector", lambda e, dst=dst_ap, kb=kb: e.tensor_add(
            dst, ropeA[:, kb, :], ropeB[:, :]), sem=ve, waits=[(ve, tv)])
        ksil_ve_g[ks] = kv2
        return ks, kv2

    proj_se_g = [0, 0]
    rope_se_g = [0, 0]
    ksil_ve_g = [0, 0, 0]
    rope_ve_g = [0, 0]
    kst_dma = [None, None, None]
    vst_dma = [0, 0]

    bank_vrd = {id(pqB1): [], id(ov): []}
    xld = [None, None]
    xld[0] = xtf_load(0)
    xn_ve_next = None
    for g in range(8):
        cslot = g % 2
        p.dma("sync", cosg[:, cslot, 0, :], io["cosd"][:, g * 512:(g + 1) * 512], p.s2,
              waits=[(ve, ve.n)])
        cld = p.dma("sync", cosg[:, cslot, 1, :], io["sind"][:, g * 512:(g + 1) * 512], p.s2)
        if g == 0:
            xn_ve = build_xn(0, xld[0])
        else:
            xn_ve = xn_ve_next
        if g < 7:
            xld[(g + 1) % 2] = xtf_load(g + 1)

        def flush_k(tok, g=g, cld=cld):
            ks, kv2 = proj_rope_tail(tok, cld)
            qsem = (rd, sd)[tok[0] % 2]
            kst_dma[ks] = (qsem, p.dma("sync", kspill[tok[0], :, g * 512:(g + 1) * 512],
                                       ksil[:, ks, :], qsem, waits=[(ve, kv2)]))

        pend = None
        for c in range(8):
            fw = [(ve, xn_ve), (p.s2, cld)]
            if g == 0 and c == 0:
                fw.append((gd, wk_ld))
            tok = proj_rope_mm(c, cslot, g % 2, ksil[:, c % 3, :], first_wait=fw)
            if pend is not None:
                flush_k(pend)
            pend = tok
        flush_k(pend)
        if g < 7:
            xn_ve_next = build_xn(g + 1, xld[(g + 1) % 2])
        cvs = []
        for nh in range(2):
            for m in range(4):
                vb = (pqB1, ov)[m % 2]
                for kk in range(8):
                    w = []
                    if kk == 0:
                        w = list(bank_vrd[id(vb)])
                        if g == 0 and m == 0 and nh == 0:
                            w.append((gd, wv_ld))
                    p.emit("tensor", lambda e, m=m, nh=nh, kk=kk, vb=vb, xs=g % 2: e.matmul(
                        vb[:, :], xn_fm[:, xs, kk, m * 128:(m + 1) * 128],
                        wbuf[:, 8 + kk, nh * 512:(nh + 1) * 512],
                        start=(kk == 0), stop=(kk == 7)), sem=pe, waits=w)
                mmv = pe.n
                cv = p.emit("scalar", lambda e, m=m, vb=vb: e.copy(
                    vstage[:, m, :], vb[:, :]), sem=se,
                    waits=[(pe, mmv)] + ([(p.g2, vst_dma[0]), (p.g3, vst_dma[1])] if m == 0 else []))
                bank_vrd[id(vb)] = [(se, cv)]
                cvs.append(cv)
            for pq4 in range(4):
                pr = nh * 4 + pq4
                vst_dma[pq4 % 2] = p.dma(
                    "gpsimd",
                    io["vspill"][pr, :, g * 4:(g + 1) * 4, :],
                    vstage[:, :, pq4 * 128:(pq4 + 1) * 128].rearrange("q m d -> q m d"),
                    (p.g2, p.g3)[pq4 % 2], waits=[(se, cvs[-1])])
            cvs = []

    wq_ld = p.dma("gpsimd", wbuf[:, 0:8, :], io["Wq"].rearrange("(c p) o -> p c o", p=128), gd,
                  waits=[(pe, pe.n)])
    wo_ld = p.dma("gpsimd", wbuf[:, 8:16, :], io["Wo"].rearrange("(c p) o -> p c o", p=128), gd)
    wg1_ld = p.dma("gpsimd", wg1[:, :, :], io["Wg1"].rearrange("(c p) o -> p c o", p=128), gd)
    wg2_ld = p.dma("gpsimd", wg2[:, :], io["Wg2"][:, :], gd)
    rd_spill_done = rd.n

    kbuf_dma = [None, None, None]
    vbuf_dma = [0, 0, 0]
    kbuf_rd = [0, 0, 0]
    vbuf_rd = [0, 0, 0]
    ov_guard = [[(se, se.n)]]
    vspill_done = p.g2.n

    xldq = [None, None]
    xldq[0] = xtf_load(0)
    xn_ve_next = build_xn(0, xldq[0])
    for qt in range(8):
        span = (qt + 1) * 512
        cslot = qt % 2
        p.dma("sync", cosg[:, cslot, 0, :], io["cosd"][:, qt * 512:(qt + 1) * 512], p.s2,
              waits=[(ve, ve.n)])
        cld = p.dma("sync", cosg[:, cslot, 1, :], io["sind"][:, qt * 512:(qt + 1) * 512], p.s2)
        xn_ve = xn_ve_next
        if qt < 7:
            xldq[(qt + 1) % 2] = xtf_load(qt + 1)
        rope_ve_g[0] = rope_ve_g[1] = ve.n
        rope_se_g[0] = rope_se_g[1] = se.n
        pend = None
        for c in range(8):
            fw = [(ve, xn_ve), (p.s2, cld)]
            if qt == 0 and c == 0:
                fw.append((gd, wq_ld))
            tok = proj_rope_mm(c, cslot, qt % 2, q_fm[:, c, :], first_wait=fw)
            if pend is not None:
                proj_rope_tail(pend, cld)
            pend = tok
        proj_rope_tail(pend, cld)
        q_done_ve = ve.n
        if qt < 7:
            xn_ve_next = build_xn(qt + 1, xldq[(qt + 1) % 2])
        for kk in range(8):
            w = [(ve, q_done_ve), (se, se.n)] if kk == 0 else []
            if kk == 0 and qt == 0:
                w.append((gd, wg1_ld))
            p.emit("tensor", lambda e, kk=kk, xs=qt % 2: e.matmul(
                m1[0:64, :], wg1[:, kk, :], xn_fm[:, xs, kk, :], start=(kk == 0), stop=(kk == 7)),
                sem=pe, waits=w)
        g1mm = pe.n
        g1v = p.emit("vector", lambda e: e.tensor_copy(g1sb[:, :], m1[0:64, :]), sem=ve, waits=[(pe, g1mm)])
        gate_se = [0] * 8
        m1_guard = [[(ve, g1v)]]

        def emit_gate(c, qt=qt, g1v=g1v):
            w = [(ve, g1v)] + m1_guard[0]
            if qt == 0 and c == 0:
                w.append((gd, wg2_ld))
            gm = p.emit("tensor", lambda e, c=c: e.matmul(
                m1[:, :], wg2[:, c * 128:(c + 1) * 128], g1sb[:, :], start=True, stop=True),
                sem=pe, waits=w)
            sg = p.emit("scalar", lambda e, c=c: e.activation(gate[:, c, :], m1[:, :], AF.Sigmoid),
                        sem=se, waits=[(pe, gm)])
            gate_se[c] = sg
            m1_guard[0] = [(se, sg)]

        for sl in range(2):
            kbuf_dma[sl] = p.dma("sync", kbuf[:, sl, 0:span], kspill[sl, :, 0:span], p.r2,
                                 waits=[(rd, rd_spill_done), (pe, kbuf_rd[sl])])
            vbuf_dma[sl] = p.dma("sync", vbuf[:, sl, 0:span // 128, :],
                                 io["vspill"][sl, :, 0:span // 128, :],
                                 (p.g2, p.g3)[sl % 2], waits=[(p.g2, vspill_done), (pe, vbuf_rd[sl])])

        esb_last = [[(pe, pe.n)]] * 6
        bank_rd = {}
        for b in (pqA0, pqA1, pqA2, pqB0, pqB1, pqB2):
            bank_rd[id(b)] = [(ve, ve.n), (se, se.n)]
        njt = (qt + 1) * 4
        gvs = []
        for pr in range(8):
            ksl = pr % 3
            emit_gate(pr)
            nxt = pr + 2
            if nxt < 8:
                dst = nxt % 3
                kbuf_dma[dst] = p.dma("sync", kbuf[:, dst, 0:span], kspill[nxt, :, 0:span],
                                      p.r2, waits=[(rd, rd_spill_done), (pe, kbuf_rd[dst])])
                vbuf_dma[dst] = p.dma("sync", vbuf[:, dst, 0:span // 128, :],
                                      io["vspill"][nxt, :, 0:span // 128, :],
                                      (p.g2, p.g3)[dst % 2],
                                      waits=[(p.g2, vspill_done), (pe, vbuf_rd[dst])])
            ev_q = []
            for j in range(njt + 1):
                if j < njt:
                    kj0 = j * 128
                    off = max(0, kj0 - qt * 512)
                    diag = kj0 >= qt * 512
                    r3 = j % 3
                    pa = (pqA0, pqA1, pqA2)[r3]
                    pb = (pqB0, pqB1, pqB2)[r3]
                    ea = 2 * r3
                    eb = ea + 1
                    wA = ([(p.r2, kbuf_dma[ksl])] if j == 0 else []) + bank_rd[id(pa)]
                    if j == 0:
                        wA.append((ve, q_done_ve))
                    mA = p.emit("tensor", lambda e, pa=pa, off=off, j=j, ks=ksl, pr=pr: e.matmul(
                        pa[:, off:512], kbuf[0:64, ks, j * 128:(j + 1) * 128], q_fm[0:64, pr, off:512],
                        start=True, stop=True, tile_position=(0, 0)), sem=pe, waits=wA)
                    mB = p.emit("tensor", lambda e, pb=pb, off=off, j=j, ks=ksl, pr=pr: e.matmul(
                        pb[:, off:512], kbuf[64:128, ks, j * 128:(j + 1) * 128], q_fm[64:128, pr, off:512],
                        start=True, stop=True, tile_position=(64, 0)), sem=pe, waits=bank_rd[id(pb)])
                if ev_q:
                    (jd, offd, ead, ebd, evw) = ev_q.pop(0)
                    evA = p.emit("tensor", lambda e, j=jd, off=offd, ea=ead, ksl=ksl, ovb=ov: e.matmul(
                        ovb[0:64, off:512], vbuf[:, ksl, j, 0:64], esb[:, ea, off:512],
                        start=(j == 0), stop=False, tile_position=(0, 0)), sem=pe, waits=evw)
                    evB = p.emit("tensor", lambda e, j=jd, off=offd, eb=ebd, ksl=ksl, ovb=ov, njt=njt: e.matmul(
                        ovb[64:128, off:512], vbuf[:, ksl, j, 64:128], esb[:, eb, off:512],
                        start=(j == 0), stop=(j == njt - 1), tile_position=(0, 64)), sem=pe, waits=evw)
                    esb_last[ead] = [(pe, evA)]
                    esb_last[ebd] = [(pe, evB)]
                if j < njt:
                    if diag:
                        va = p.emit("vector", lambda e, pa=pa, off=off, ea=ea: e.tensor_tensor(
                            esb[:, ea, off:off + 128], pa[:, off:off + 128], sb["tri"][:, :], ALU.mult),
                            sem=ve, waits=[(pe, mA)] + esb_last[ea])
                        if off + 128 < 512:
                            va = p.emit("vector", lambda e, pa=pa, off=off, ea=ea: e.tensor_copy(
                                esb[:, ea, off + 128:512], pa[:, off + 128:512]), sem=ve)
                        vb2 = p.emit("vector", lambda e, pb=pb, off=off, eb=eb: e.tensor_tensor(
                            esb[:, eb, off:off + 128], pb[:, off:off + 128], sb["tri"][:, :], ALU.mult),
                            sem=ve, waits=[(pe, mB)] + esb_last[eb])
                        evw = [(ve, va), (ve, vb2)]
                        if off + 128 < 512:
                            sc2 = p.emit("scalar", lambda e, pb=pb, off=off, eb=eb: e.copy(
                                esb[:, eb, off + 128:512], pb[:, off + 128:512]), sem=se, waits=[(pe, mB)])
                            evw.append((se, sc2))
                        bank_rd[id(pa)] = [(ve, va)]
                        bank_rd[id(pb)] = [(ve, vb2)] + ([(se, sc2)] if off + 128 < 512 else [])
                    else:
                        va = p.emit("vector", lambda e, pa=pa, ea=ea: e.tensor_copy(
                            esb[:, ea, :], pa[:, :]), sem=ve, waits=[(pe, mA)] + esb_last[ea])
                        sc2 = p.emit("scalar", lambda e, pb=pb, eb=eb: e.copy(
                            esb[:, eb, :], pb[:, :]), sem=se, waits=[(pe, mB)] + esb_last[eb])
                        evw = [(ve, va), (se, sc2)]
                        bank_rd[id(pa)] = [(ve, va)]
                        bank_rd[id(pb)] = [(se, sc2)]
                    if j == 0:
                        evw = evw + ov_guard[0] + [((p.g2, p.g3)[ksl % 2], vbuf_dma[ksl])]
                    ev_q.append((j, off, ea, eb, evw))
            gv = p.emit("vector", lambda e, pr=pr: e.tensor_tensor(
                attn[:, pr, :], ov[:, :], gate[:, pr, :], ALU.mult), sem=ve,
                waits=[(pe, pe.n), (se, gate_se[pr])])
            gvs.append(gv)
            ov_guard[0] = [(ve, gv)]
            kbuf_rd[ksl] = pe.n
            vbuf_rd[ksl] = pe.n
        attn_done_ve = ve.n

        for c in range(8):
            sqv = p.emit("scalar", lambda e, c=c: e.activation(sq[:, :], attn[:, c, :], AF.Square),
                         sem=se, waits=[(ve, gvs[c])] + ([] if c == 0 else [(pe, pe.n)]))
            p.emit("tensor", lambda e, c=c: e.matmul(
                m1[0:1, :], sb["ones_col_bf"][:, :], attn[:, c, :], start=(c == 0), stop=(c == 7)),
                sem=pe, waits=([(ve, gvs[0]), (se, se.n)] if c == 0 else [(ve, gvs[c])]))
            p.emit("tensor", lambda e, c=c: e.matmul(
                pqA0[0:1, :], sb["ones_col_bf"][:, :], sq[:, :], start=(c == 0), stop=(c == 7)),
                sem=pe, waits=[(se, sqv)] + (bank_rd[id(pqA0)] if c == 0 else []))
        sums_pe = pe.n
        r1 = p.emit("vector", lambda e: e.tensor_scalar_mul(srow(0), m1[0:1, :], 1.0 / D),
                    sem=ve, waits=[(pe, sums_pe), (rd, rd.n)])
        r2 = p.emit("vector", lambda e: e.tensor_scalar_mul(srow(1), pqA0[0:1, :], 1.0 / D),
                    sem=ve, waits=[(ve, r1)])
        r3 = p.emit("vector", lambda e: e.tensor_tensor(srow(2), srow(0), srow(0), ALU.mult),
                    sem=ve, waits=[(ve, r2)])
        r4 = p.emit("vector", lambda e: e.tensor_sub(srow(1), srow(1), srow(2)),
                    sem=ve, waits=[(ve, r3)])
        r5 = p.emit("scalar", lambda e: e.activation(srow(1), srow(1), AF.Sqrt, bias=sb["fc"][0:1, 0:1]),
                    sem=se, waits=[(ve, r4)])
        r6 = p.emit("vector", lambda e: e.reciprocal(srow(2), srow(1)), sem=ve, waits=[(se, r5)])
        r8 = p.emit("vector", lambda e: e.scalar_tensor_tensor(
            srow(3), srow(0), -1.0, srow(2), ALU.mult, ALU.mult),
            sem=ve, waits=[(ve, r6)])
        bca = p.emit("tensor", lambda e: e.matmul(pqB0[:, :], sb["ones_row"][:, :], srow[:, 2:3, 0, :]
                                                  if False else srow(2),
                                                  start=True, stop=True), sem=pe,
                     waits=[(ve, r8)] + bank_rd[id(pqB0)])
        bcb = p.emit("tensor", lambda e: e.matmul(pqB1[:, :], sb["ones_row"][:, :], srow(3),
                                                  start=True, stop=True), sem=pe,
                     waits=[(ve, r8)] + bank_rd[id(pqB1)])
        for c in range(8):
            t1 = p.emit("vector", lambda e, c=c: e.tensor_tensor(lnsA[:, :], attn[:, c, :], pqB0[:, :], ALU.mult),
                        sem=ve, waits=([(pe, bca)] if c == 0 else []))
            t2 = p.emit("vector", lambda e, c=c: e.tensor_tensor(lnsB[:, :], lnsA[:, :], pqB1[:, :], ALU.add),
                        sem=ve, waits=([(pe, bcb)] if c == 0 else []) + [(ve, t1)])
            p.emit("vector", lambda e, c=c: e.tensor_scalar(
                xo[:, c, :], lnsB[:, :], sb["lnw"][:, c:c + 1], sb["lnb"][:, c:c + 1],
                ALU.mult, ALU.add), sem=ve, waits=[(ve, t2)])
        xo_done = ve.n

        for ms in range(4):
            slot = ms % 2
            xl = p.dma("sync", xsub[:, slot, :], xin[(qt * 4 + ms) * 128:(qt * 4 + ms + 1) * 128, :], sd,
                       waits=[(ve, ve.n), (se, se.n)])
            for nh in range(2):
                wop = (m1, pqA1)[nh]
                for kk in range(8):
                    w = []
                    if kk == 0:
                        w = [(ve, xo_done)] if ms == 0 and nh == 0 else [(ve, ve.n)]
                        if qt == 0 and ms == 0 and nh == 0:
                            w.append((gd, wo_ld))
                        if nh == 1:
                            w += bank_rd[id(pqA1)]
                            bank_rd[id(pqA1)] = []
                    p.emit("tensor", lambda e, ms=ms, nh=nh, kk=kk, wop=wop: e.matmul(
                        wop[:, :], xo[:, kk, ms * 128:(ms + 1) * 128],
                        wbuf[:, 8 + kk, nh * 512:(nh + 1) * 512],
                        start=(kk == 0), stop=(kk == 7)), sem=pe, waits=w)
                mmo = pe.n
                p.emit("vector", lambda e, slot=slot, nh=nh, wop=wop: e.tensor_tensor(
                    x1t[:, slot, nh * 512:(nh + 1) * 512], xsub[:, slot, nh * 512:(nh + 1) * 512],
                    wop[:, :], ALU.add), sem=ve, waits=[(pe, mmo), (sd, xl), (rd, rd.n)])
            p.dma("sync", resid[(qt * 4 + ms) * 128:(qt * 4 + ms + 1) * 128, :], x1t[:, slot, :], rd,
                  waits=[(ve, ve.n)])





def build_phase2(p, io, E):
    nc = p.nc
    resid = io["resid"]
    sb = io["sb"]
    sd, gd, rd, pe, ve, se, po = p.sd, p.gd, p.rd, p.pe, p.ve, p.se, p.po

    xn_fm = E(nc.sbuf_tensor("p2_xnfm", [128, 8, 1024], BF16))
    hbufA = E(nc.sbuf_tensor("p2_h", [128, 32, 1024], BF16))
    hbufB = E(nc.sbuf_tensor("p2_g1", [128, 32, 1024], BF16))
    w12b = E(nc.sbuf_tensor("p2_w12b", [128, 2, 16, 128], BF16))
    w1b = w12b
    hb = E(nc.sbuf_tensor("p2_hb", [128, 2, 32, 128], BF16))
    xrb = E(nc.sbuf_tensor("p2_xrb", [128, 2, 1024], F32))
    xn_tm = E(nc.sbuf_tensor("p2_xntm", [128, 2, 1024], BF16))
    tsc = E(nc.sbuf_tensor("p2_t", [128, 2, 512], F32))
    x2p = E(nc.sbuf_tensor("p2_x2p", [128, 8, 128], F32))
    s1s = E(nc.sbuf_tensor("p2_s1", [128, 32], F32))
    s2s = E(nc.sbuf_tensor("p2_s2", [128, 32], F32))
    hs1s = E(nc.sbuf_tensor("p2_hs1", [128, 32], F32))
    hs2s = E(nc.sbuf_tensor("p2_hs2", [128, 32], F32))
    s3s = E(nc.sbuf_tensor("p2_s3", [128, 8], F32))
    stsum = E(nc.sbuf_tensor("p2_stsum", [128, 8], F32))
    stsq = E(nc.sbuf_tensor("p2_stsq", [128, 8], F32))
    mu2 = E(nc.sbuf_tensor("p2_mu", [128, 8], F32))
    rstd2 = E(nc.sbuf_tensor("p2_rstd", [128, 8], F32))
    tmp2 = E(nc.sbuf_tensor("p2_tmp", [128, 8], F32))
    sqd = E(nc.sbuf_tensor("p2_sqd", [128, 1024], BF16))

    pa0 = E(nc.psum_tensor("p2_pa0", [128, 512], F32))
    pa1 = E(nc.psum_tensor("p2_pa1", [128, 512], F32))
    pb0 = E(nc.psum_tensor("p2_pb0", [128, 512], F32))
    pb1 = E(nc.psum_tensor("p2_pb1", [128, 512], F32))
    tpA = E(nc.psum_tensor("p2_tpA", [128, 1024], BF16))
    tpB = E(nc.psum_tensor("p2_tpB", [128, 1024], BF16))
    po0 = E(nc.psum_tensor("p2_po0", [128, 512], F32))
    po1 = E(nc.psum_tensor("p2_po1", [128, 512], F32))

    for name, dst in (("s1_d", s1s), ("s2_d", s2s), ("hs1_d", hs1s), ("hs2_d", hs2s), ("s3_d", s3s)):
        p.dma("sync", dst[:, :], io[name], sd)
    sc_ld = sd.n

    tp_last = [(ve, ve.n), (ve, ve.n)]
    xntm_rd = [0, 0]
    bank_guard = {id(pa0): [], id(pa1): [], id(pb0): [], id(pb1): [],
                  id(po0): [], id(po1): []}
    w3mm = {}
    w3tp = [0]
    tsc_rd = [0, 0]
    x2p_rd = [None] * 8
    xrb_rd = [[], []]
    st_done_box = [0]
    xn_done_box = [0]
    APOOL = (pa0, pa1, po0)
    BPOOL = (pb0, pb1, po1)
    GPOOL = (pa0, pa1, po0, pb0, pb1, po1)
    po_guard = {id(po0): [], id(po1): []}
    wslot_guard = {id(w1b): [None, None], id(hb): [None, None]}

    def stage_mm(src_fm, wtile, wP, m, n_kk, dst_cb, mplan, tile_first=()):
        slot = m % 2
        g = wslot_guard[id(wtile)][slot]
        wld = p.dma("gpsimd", wtile[:, slot, 0:n_kk, :],
                    wP[m].rearrange("kk q mc -> q kk mc"), gd,
                    waits=([(pe, g)] if g else []))
        for nh in range(2):
            pa = mplan[nh]
            for kk in range(n_kk):
                w = []
                if kk == 0:
                    w = [(gd, wld)] + bank_guard[id(pa)] + list(tile_first)
                p.emit("tensor", lambda e, m=m, nh=nh, kk=kk, pa=pa, slot=slot: e.matmul(
                    pa[:, :], wtile[:, slot, kk, :], src_fm[:, kk, nh * 512:(nh + 1) * 512],
                    start=(kk == 0), stop=(kk == n_kk - 1)), sem=pe, waits=w)
            bank_guard[id(pa)] = dst_cb(m, nh, pa)
        wslot_guard[id(wtile)][slot] = pe.n

    def stats_emit(tt):
        r0 = tt * 1024
        vst, sst = [], []
        for si in range(8):
            slot = si % 2
            ld = p.dma("sync", xrb[:, slot, :], resid[r0 + si * 128:r0 + (si + 1) * 128, :], sd,
                       waits=list(xrb_rd[slot]))
            v = p.emit("vector", lambda e, s=slot, si=si: e.reduce_sum(stsum[:, si:si + 1], xrb[:, s, :], AX.X),
                       sem=ve, waits=[(sd, ld)])
            sv = p.emit("scalar", lambda e, s=slot, si=si: e.activation(sqd[:], xrb[:, s, :], AF.Square,
                                                                        accum_out=stsq[:, si:si + 1]),
                        sem=se, waits=[(sd, ld)])
            xrb_rd[slot] = [(ve, v), (se, sv)]
            vst.append(v)
            sst.append(sv)
        v1 = p.emit("vector", lambda e: e.tensor_scalar_mul(mu2[:], stsum[:], 1.0 / D), sem=ve,
                    waits=[(ve, vst[-1])])
        v2 = p.emit("vector", lambda e: e.tensor_scalar_mul(tmp2[:], stsq[:], 1.0 / D), sem=ve,
                    waits=[(se, sst[-1]), (ve, v1)])
        v3 = p.emit("vector", lambda e: e.tensor_mul(rstd2[:], mu2[:], mu2[:]), sem=ve, waits=[(ve, v2)])
        v4 = p.emit("vector", lambda e: e.tensor_sub(tmp2[:], tmp2[:], rstd2[:]), sem=ve, waits=[(ve, v3)])
        sq1 = p.emit("scalar", lambda e: e.activation(tmp2[:], tmp2[:], AF.Sqrt, bias=sb["fc"][:, 0:1]),
                     sem=se, waits=[(ve, v4)])
        st_done_box[0] = p.emit("vector", lambda e: e.reciprocal(rstd2[:], tmp2[:]), sem=ve, waits=[(se, sq1)])

    def norm_gen(tt):
        r0 = tt * 1024
        st_done = st_done_box[0]
        for si in range(8):
            slot = si % 2
            ld = p.dma("sync", xrb[:, slot, :], resid[r0 + si * 128:r0 + (si + 1) * 128, :], sd,
                       waits=list(xrb_rd[slot]))
            nv = p.emit("vector", lambda e, s=slot, si=si: e.tensor_scalar(
                xn_tm[:, s, :], xrb[:, s, :], mu2[:, si:si + 1], rstd2[:, si:si + 1],
                ALU.subtract, ALU.mult), sem=ve,
                waits=[(sd, ld), (ve, st_done), (pe, xntm_rd[slot])])
            xrb_rd[slot] = [(ve, nv)]
            for c in range(8):
                bk = c % 2
                bank = (tpA, tpB)[bk]
                pv = p.emit("tensor", lambda e, s=slot, c=c, bank=bank: e.transpose(
                    bank[:, 0:128], xn_tm[:, s, c * 128:(c + 1) * 128], sb["id_bf"][:, :]),
                    sem=pe, waits=[tp_last[bk], (ve, nv)])
                lv = p.emit("vector", lambda e, c=c, si=si, bank=bank: e.tensor_copy(
                    xn_fm[:, c, si * 128:(si + 1) * 128], bank[:, 0:128]),
                    sem=ve, waits=[(pe, pv)])
                tp_last[bk] = (ve, lv)
            xntm_rd[slot] = pe.n
            yield
        xn_done_box[0] = ve.n

    stats_emit(0)
    for _ in norm_gen(0):
        pass

    for tt in range(4):
        r0 = tt * 1024
        xn_done = xn_done_box[0]
        def h_epilogue(m, nh, pa):
            return []
        for m in range(32):
            slot = m % 2
            g1g = wslot_guard[id(w1b)][slot]
            wld1 = p.dma("gpsimd", w12b[:, slot, :, :], io["w12P"][m].rearrange("kk q mc -> q kk mc"), gd,
                         waits=([(pe, g1g)] if g1g else []))
            wld2 = wld1
            qa0 = (2 * m) % 3
            qa1 = (2 * m + 1) % 3
            for kk in range(8):
                w = []
                if kk == 0:
                    w = [(gd, wld1)] + bank_guard[id(APOOL[qa0])] + bank_guard[id(APOOL[qa1])]
                    if m == 0:
                        w += [(ve, xn_done), (sd, sc_ld)]
                for nh in range(2):
                    p.emit("tensor", lambda e, m=m, nh=nh, kk=kk, slot=slot,
                           pa=APOOL[(2 * m + nh) % 3]: e.matmul(
                        pa[:, :], w1b[:, slot, kk, :], xn_fm[:, kk, nh * 512:(nh + 1) * 512],
                        start=(kk == 0), stop=(kk == 7)), sem=pe, waits=(w if nh == 0 else []))
            mma = pe.n
            for kk in range(8):
                w = []
                if kk == 0:
                    w = [(gd, wld2)] + bank_guard[id(BPOOL[qa0])] + bank_guard[id(BPOOL[qa1])]
                for nh in range(2):
                    p.emit("tensor", lambda e, m=m, nh=nh, kk=kk, slot=slot,
                           pb=BPOOL[(2 * m + nh) % 3]: e.matmul(
                        pb[:, :], w12b[:, slot, 8 + kk, :], xn_fm[:, kk, nh * 512:(nh + 1) * 512],
                        start=(kk == 0), stop=(kk == 7)), sem=pe, waits=(w if nh == 0 else []))
            mmb = pe.n
            for nh in range(2):
                pa = APOOL[(2 * m + nh) % 3]
                pb = BPOOL[(2 * m + nh) % 3]
                sv = p.emit("scalar", lambda e, m=m, pa=pa, ts=nh: e.activation(
                    tsc[:, ts, :], pa[:, :], AF.Silu, scale=s1s[:, m:m + 1]),
                    sem=se, waits=[(pe, mma), (ve, ve.n)] if nh == 0 else [(pe, mma)])
                vv = p.emit("vector", lambda e, m=m, nh=nh, pb=pb, ts=nh: e.scalar_tensor_tensor(
                    hbufA[:, m, nh * 512:(nh + 1) * 512], pb[:, :], s2s[:, m:m + 1], tsc[:, ts, :],
                    ALU.mult, ALU.mult), sem=ve, waits=[(pe, mmb), (se, sv)])
                bank_guard[id(pa)] = [(se, sv)]
                bank_guard[id(pb)] = [(ve, vv)]
            wslot_guard[id(w1b)][slot] = pe.n
        h_done = ve.n
        if tt < 3:
            stats_emit(tt + 1)

        def big_stage(srcbuf, dstbuf, wPname, scl, first_ve, gen=None):
            for m in range(32):
                if gen is not None:
                    next(gen, None)
                slot = m % 2
                g = wslot_guard[id(hb)][slot]
                wld = p.dma("gpsimd", hb[:, slot, :, :], io[wPname][m].rearrange("kk q mc -> q kk mc"), gd,
                            waits=([(pe, g)] if g else []))
                for kk in range(32):
                    w = []
                    if kk == 0:
                        w = [(gd, wld)] + bank_guard[id(GPOOL[(2 * m) % 6])] + bank_guard[id(GPOOL[(2 * m + 1) % 6])]
                        if m == 0:
                            w += [(ve, first_ve), (se, se.n), (ve, ve.n)]
                    for nh in range(2):
                        p.emit("tensor", lambda e, m=m, nh=nh, kk=kk, slot=slot, srcbuf=srcbuf,
                               pa=GPOOL[(2 * m + nh) % 6]: e.matmul(
                            pa[:, :], hb[:, slot, kk, :], srcbuf[:, kk, nh * 512:(nh + 1) * 512],
                            start=(kk == 0), stop=(kk == 31)), sem=pe, waits=(w if nh == 0 else []))
                mm = pe.n
                for nh in range(2):
                    pa = GPOOL[(2 * m + nh) % 6]
                    sv = p.emit("scalar", lambda e, m=m, nh=nh, pa=pa, dstbuf=dstbuf, scl=scl: e.activation(
                        dstbuf[:, m, nh * 512:(nh + 1) * 512], pa[:, :], AF.Gelu, scale=scl[:, m:m + 1]),
                        sem=se, waits=[(pe, mm)])
                    bank_guard[id(pa)] = [(se, sv)]
                wslot_guard[id(hb)][slot] = pe.n
            return se.n

        g1_done = big_stage(hbufA, hbufB, "h1P", hs1s, h_done)
        gen = norm_gen(tt + 1) if tt < 3 else None
        g2_done = big_stage(hbufB, hbufA, "h2P", hs2s, ve.n, gen=gen)
        if gen is not None:
            for _ in gen:
                pass

        w3q = []

        def flush_w3(item):
            (m_, pa_, pb_) = item
            for nh in range(2):
                pab = (pa_, pb_)[nh]
                ts = nh
                sv = p.emit("scalar", lambda e, m=m_, pab=pab, ts=ts: e.activation(
                    tsc[:, ts, :], pab[:, :], AF.Copy, scale=s3s[:, m:m + 1]),
                    sem=se, waits=[(pe, w3mm[id(pab)]), (pe, tsc_rd[ts])])
                bank_guard[id(pab)] = [(se, sv)]
                for wq_ in range(4):
                    si = nh * 4 + wq_
                    pob = (po0, po1)[w3tp[0] % 2]
                    w3tp[0] += 1
                    tv2 = p.emit("tensor", lambda e, ts=ts, wq_=wq_, pob=pob: e.transpose(
                        pob[:, 0:128], tsc[:, ts, wq_ * 128:(wq_ + 1) * 128], sb["id_f32"][:, :]),
                        sem=pe, waits=[(se, sv)] + bank_guard[id(pob)])
                    prev = x2p_rd[si]
                    cp = p.emit("vector", lambda e, wq_=wq_, pob=pob, xs=si: e.tensor_copy(
                        x2p[:, xs, :], pob[:, 0:128]), sem=ve,
                        waits=[(pe, tv2)] + ([prev] if prev else []))
                    bank_guard[id(pob)] = [(ve, cp)]
                    qsem = (rd, p.od)[si % 2]
                    x2p_rd[si] = (qsem, p.emit(
                        "gpsimd", lambda e, si=si, m=m_, xs=si: e.dma_start(
                            out=resid[r0 + si * 128:r0 + (si + 1) * 128, m * 128:(m + 1) * 128],
                            in_=x2p[:, xs, :], accum_op=ALU.add), sem=qsem, inc=16,
                        waits=[(ve, cp), (qsem, qsem.n)]))
                tsc_rd[ts] = pe.n

        for m in range(8):
            slot = m % 2
            g = wslot_guard[id(hb)][slot]
            wld = p.dma("gpsimd", hb[:, slot, :, :], io["w3P"][m].rearrange("kk q mc -> q kk mc"), gd,
                        waits=([(pe, g)] if g else []))
            pa_ = (pa0, pb0)[m % 2]
            pb_ = (pa1, pb1)[m % 2]
            for kk in range(32):
                w = []
                if kk == 0:
                    w = [(gd, wld)] + bank_guard[id(pa_)] + bank_guard[id(pb_)]
                    if m == 0:
                        w += [(se, g2_done), (ve, ve.n)]
                for nh in range(2):
                    p.emit("tensor", lambda e, m=m, nh=nh, kk=kk, slot=slot,
                           pab=(pa_, pb_)[nh]: e.matmul(
                        pab[:, :], hb[:, slot, kk, :], hbufA[:, kk, nh * 512:(nh + 1) * 512],
                        start=(kk == 0), stop=(kk == 31)), sem=pe, waits=(w if nh == 0 else []))
            w3mm[id(pa_)] = pe.n
            w3mm[id(pb_)] = pe.n
            wslot_guard[id(hb)][slot] = pe.n
            w3q.append((m, pa_, pb_))
            if len(w3q) > 1:
                flush_w3(w3q.pop(0))
        while w3q:
            flush_w3(w3q.pop(0))



def build_phase4a(p, io, E):
    nc = p.nc
    l1in = io["l1in"]
    sb = io["sb"]
    sd, gd, rd, pe, ve, se, po = p.sd, p.gd, p.rd, p.pe, p.ve, p.se, p.po

    wbuf = E(nc.sbuf_tensor("p4_wbuf", [128, 3, 8, 1024], BF16))
    xrb = E(nc.sbuf_tensor("p4_xrb", [128, 2, 1024], F32))
    xn_tm = E(nc.sbuf_tensor("p4_xntm", [128, 2, 1024], BF16))
    xn_fm = E(nc.sbuf_tensor("p4_xnfm", [128, 8, 512], BF16))
    sqf = E(nc.sbuf_tensor("p4_sqf", [128, 2, 512], F32))
    sdt = E(nc.sbuf_tensor("p4_sd", [128, 2, 512], F32))
    khc = E(nc.sbuf_tensor("p4_khc", [128, 2, 512], F32))
    vqc = E(nc.sbuf_tensor("p4_vqc", [128, 2, 512], BF16))
    posblk = E(nc.sbuf_tensor("p4_posblk", [128, 128], F32))
    stsum = E(nc.sbuf_tensor("p4_stsum", [128, 4], F32))
    stsq = E(nc.sbuf_tensor("p4_stsq", [128, 4], F32))
    mu4 = E(nc.sbuf_tensor("p4_mu", [128, 4], F32))
    rstd4 = E(nc.sbuf_tensor("p4_rstd", [128, 4], F32))
    tmp4 = E(nc.sbuf_tensor("p4_tmp", [128, 4], F32))
    sqd = E(nc.sbuf_tensor("p4_sqd", [128, 1024], BF16))

    pk0 = E(nc.psum_tensor("p4_pk0", [128, 512], F32))
    pk1 = E(nc.psum_tensor("p4_pk1", [128, 512], F32))
    pn0 = E(nc.psum_tensor("p4_pn0", [128, 512], F32))
    pn1 = E(nc.psum_tensor("p4_pn1", [128, 512], F32))
    tpA = E(nc.psum_tensor("p4_tpA", [128, 1024], BF16))
    tpB = E(nc.psum_tensor("p4_tpB", [128, 1024], BF16))

    p.dma("sync", posblk[:, :], io["posblk_d"], sd)
    wld = p.dma("gpsimd", wbuf[:, 0, :, :], io["wkT"].rearrange("(c q) o -> q c o", q=128), gd)
    wld = p.dma("gpsimd", wbuf[:, 1, :, :], io["wvT"].rearrange("(c q) o -> q c o", q=128), gd)
    wld = p.dma("gpsimd", wbuf[:, 2, :, :], io["wqT"].rearrange("(c q) o -> q c o", q=128), gd)

    tp_last = [(ve, ve.n), (ve, ve.n)]
    pk_g = {id(pk0): [], id(pk1): []}
    pn_g = {id(pn0): [], id(pn1): []}
    sq_g = [[], []]
    sd_g = [[], []]
    kh_g = [[], []]
    vq_g = [[], []]

    for t in range(8):
        r0 = t * 512
        vst, sst = [], []
        for si in range(4):
            slot = si % 2
            w = [(ve, vst[si - 2]), (se, sst[si - 2])] if si >= 2 else [(ve, ve.n), (se, se.n)]
            ld = p.dma("sync", xrb[:, slot, :], l1in[r0 + si * 128:r0 + (si + 1) * 128, :], sd, waits=w)
            v = p.emit("vector", lambda e, s=slot, si=si: e.reduce_sum(stsum[:, si:si + 1], xrb[:, s, :], AX.X),
                       sem=ve, waits=[(sd, ld)])
            sv = p.emit("scalar", lambda e, s=slot, si=si: e.activation(sqd[:], xrb[:, s, :], AF.Square,
                                                                        accum_out=stsq[:, si:si + 1]),
                        sem=se, waits=[(sd, ld)])
            vst.append(v)
            sst.append(sv)
        v1 = p.emit("vector", lambda e: e.tensor_scalar_mul(mu4[:], stsum[:], 1.0 / D), sem=ve,
                    waits=[(ve, vst[-1])])
        v2 = p.emit("vector", lambda e: e.tensor_scalar_mul(tmp4[:], stsq[:], 1.0 / D), sem=ve,
                    waits=[(se, sst[-1]), (ve, v1)])
        v3 = p.emit("vector", lambda e: e.tensor_mul(rstd4[:], mu4[:], mu4[:]), sem=ve, waits=[(ve, v2)])
        v4 = p.emit("vector", lambda e: e.tensor_sub(tmp4[:], tmp4[:], rstd4[:]), sem=ve, waits=[(ve, v3)])
        sq1 = p.emit("scalar", lambda e: e.activation(tmp4[:], tmp4[:], AF.Sqrt, bias=sb["fc"][:, 0:1]),
                     sem=se, waits=[(ve, v4)])
        st_done = p.emit("vector", lambda e: e.reciprocal(rstd4[:], tmp4[:]), sem=ve, waits=[(se, sq1)])
        for si in range(4):
            slot = si % 2
            ld = p.dma("sync", xrb[:, slot, :], l1in[r0 + si * 128:r0 + (si + 1) * 128, :], sd,
                       waits=[(ve, ve.n), (se, se.n)])
            nv = p.emit("vector", lambda e, s=slot, si=si: e.tensor_scalar(
                xn_tm[:, s, :], xrb[:, s, :], mu4[:, si:si + 1], rstd4[:, si:si + 1],
                ALU.subtract, ALU.mult), sem=ve, waits=[(sd, ld), (ve, st_done), (pe, pe.n)])
            for c in range(8):
                bk = c % 2
                bank = (tpA, tpB)[bk]
                pv = p.emit("tensor", lambda e, s=slot, c=c, bank=bank: e.transpose(
                    bank[:, 0:128], xn_tm[:, s, c * 128:(c + 1) * 128], sb["id_bf"][:, :]),
                    sem=pe, waits=[tp_last[bk], (ve, nv)])
                lv = p.emit("vector", lambda e, c=c, si=si, bank=bank: e.tensor_copy(
                    xn_fm[:, c, si * 128:(si + 1) * 128], bank[:, 0:128]),
                    sem=ve, waits=[(pe, pv)])
                tp_last[bk] = (ve, lv)
        xn_done = ve.n
        for wi, spname in ((0, "khsp"), (1, "vsp"), (2, "qsp")):
            for c in range(8):
                pk = (pk0, pk1)[c % 2]
                for kk in range(8):
                    w = []
                    if kk == 0:
                        w = pk_g[id(pk)] + [(ve, xn_done)]
                        if t == 0 and wi == 0 and c == 0:
                            w += [(gd, wld), (sd, sd.n)]
                    p.emit("tensor", lambda e, wi=wi, c=c, kk=kk, pk=pk: e.matmul(
                        pk[:, :], wbuf[:, wi, kk, c * 128:(c + 1) * 128], xn_fm[:, kk, :],
                        start=(kk == 0), stop=(kk == 7)), sem=pe, waits=w)
                mmp = pe.n
                if wi == 0:
                    cs = c % 2
                    pn = (pn0, pn1)[cs]
                    sv = p.emit("scalar", lambda e, cs=cs, pk=pk: e.activation(
                        sqf[:, cs, :], pk[:, :], AF.Square), sem=se,
                        waits=[(pe, mmp)] + sq_g[cs])
                    nm = p.emit("tensor", lambda e, cs=cs, pn=pn: e.matmul(
                        pn[:, :], posblk[:, :], sqf[:, cs, :], start=True, stop=True),
                        sem=pe, waits=[(se, sv)] + pn_g[id(pn)])
                    sq_g[cs] = [(pe, nm)]
                    s2 = p.emit("scalar", lambda e, cs=cs, pn=pn: e.activation(
                        sdt[:, cs, :], pn[:, :], AF.Sqrt), sem=se,
                        waits=[(pe, nm)] + sd_g[cs])
                    a1 = p.emit("vector", lambda e, cs=cs: e.tensor_scalar_add(
                        sdt[:, cs, :], sdt[:, cs, :], 1e-6), sem=ve, waits=[(se, s2)])
                    a2 = p.emit("vector", lambda e, cs=cs: e.reciprocal(sdt[:, cs, :], sdt[:, cs, :]),
                                sem=ve, waits=[(ve, a1)])
                    pn_g[id(pn)] = [(ve, a2)]
                    kv = p.emit("vector", lambda e, cs=cs, pk=pk: e.tensor_tensor(
                        khc[:, cs, :], pk[:, :], sdt[:, cs, :], ALU.mult), sem=ve,
                        waits=[(ve, a2)] + kh_g[cs])
                    sd_g[cs] = [(ve, kv)]
                    pk_g[id(pk)] = [(ve, kv)]
                    dm = p.dma("sync", io["khsp"][t, c, :, :], khc[:, cs, :], rd, waits=[(ve, kv)])
                    kh_g[cs] = [(rd, dm)]
                else:
                    cs = c % 2
                    cv = p.emit("scalar", lambda e, cs=cs, pk=pk: e.copy(vqc[:, cs, :], pk[:, :]),
                                sem=se, waits=[(pe, mmp)] + vq_g[cs])
                    pk_g[id(pk)] = [(se, cv)]
                    dm = p.dma("sync", io[spname][t, c, :, :], vqc[:, cs, :], p.r2, waits=[(se, cv)])
                    vq_g[cs] = [(p.r2, dm)]


def build_phase4b(p, io, E):
    nc = p.nc
    l1in, x3buf = io["l1in"], io["x3buf"]
    sb = io["sb"]
    sd, gd, rd, pe, ve, se, po = p.sd, p.gd, p.rd, p.pe, p.ve, p.se, p.po
    import os as _os
    _ser = _os.environ.get("SER4B", "")
    if _ser:
        _emit0 = p.emit
        def _emit(eng, fn, sem=None, inc=1, waits=()):
            hit = "a" in _ser or eng[0] in _ser
            if "d" in _ser and eng == "gpsimd" and inc == 16:
                hit = True
            if "P" in _ser and eng == "gpsimd" and inc == 1:
                hit = True
            if hit:
                waits = list(waits) + [(pe, pe.n), (ve, ve.n), (se, se.n), (po, po.n),
                                       (rd, rd.n), (gd, gd.n), (sd, sd.n)]
            return _emit0(eng, fn, sem=sem, inc=inc, waits=waits)
        p.emit = _emit

    guard0 = E(nc.sbuf_tensor("p4_guard0", [128, 512], F32))
    S = E(nc.sbuf_tensor("p4_S", [128, 64, 512], F32))
    selv = E(nc.sbuf_tensor("p4_selv", [128, 8, 128], BF16))
    selq = E(nc.sbuf_tensor("p4_selq", [128, 8, 128], BF16))
    negb = E(nc.sbuf_tensor("p4_negb", [128, 128], BF16))
    kh = E(nc.sbuf_tensor("p4b_kh", [128, 3, 512], F32))
    vb = E(nc.sbuf_tensor("p4b_vb", [128, 3, 512], BF16))
    qb = E(nc.sbuf_tensor("p4b_qb", [128, 3, 512], BF16))
    Pb = E(nc.sbuf_tensor("p4b_P", [128, 4, 512], BF16))
    Qb = E(nc.sbuf_tensor("p4b_Q", [128, 4, 512], BF16))
    tf = E(nc.sbuf_tensor("p4b_tf", [128, 2, 512], F32))
    sf = E(nc.sbuf_tensor("p4b_sf", [128, 2, 512], F32))
    of = E(nc.sbuf_tensor("p4b_of", [128, 2, 512], F32))
    of2 = E(nc.sbuf_tensor("p4b_of2", [128, 2, 512], F32))
    xrb = E(nc.sbuf_tensor("p4b_xrb", [128, 4, 1024], F32))
    x3t = E(nc.sbuf_tensor("p4b_x3t", [128, 4, 1024], F32))

    pd0 = E(nc.psum_tensor("p4b_pd0", [128, 512], F32))
    pd1 = E(nc.psum_tensor("p4b_pd1", [128, 512], F32))
    pd2 = E(nc.psum_tensor("p4b_pd2", [128, 512], F32))
    pd3 = E(nc.psum_tensor("p4b_pd3", [128, 512], F32))
    pq0 = E(nc.psum_tensor("p4b_pq0", [128, 512], F32))
    pq1 = E(nc.psum_tensor("p4b_pq1", [128, 512], F32))
    po0 = E(nc.psum_tensor("p4b_po0", [128, 512], F32))
    po1 = E(nc.psum_tensor("p4b_po1", [128, 512], F32))

    p.dma("sync", selv[:, :, :], io["selv_d"].rearrange("i q m -> q i m"), sd)
    p.dma("sync", selq[:, :, :], io["selq_d"].rearrange("i q m -> q i m"), sd)
    sel_ld = p.dma("sync", negb[:, :], io["negblk_d"], sd)

    mz = 0
    for z in range(16):
        mz = p.emit("vector", lambda e, z=z: e.memset(S[:, z * 4:(z + 1) * 4, :], 0.0), sem=ve)

    tanh_at = [0] * 64
    Sread_po = [(po, 0)] * 64
    Sread_ve = [(ve, mz)] * 64
    Pb_rd = [0] * 4
    Qb_rd = [0] * 4
    tf_rd = [0] * 2
    sf_rd = [0] * 2
    pd_rd = [0, 0, 0, 0]
    pq_rd = [[], []]
    po_g = [[], []]
    x3_dma = [0, 0, 0, 0]
    of_rd = [0, 0]
    of2_rd = [0, 0]
    kld_prev = [0, 0]
    kh_rd = [[], [], []]
    vb_rd = [[], [], []]
    qb_rd = [[], [], []]

    for t in range(8):
        r0 = t * 512
        xld = []
        for ls in range(4):
            xld.append(p.dma("sync", xrb[:, ls, :], l1in[r0 + ls * 128:r0 + (ls + 1) * 128, :], sd,
                             waits=po_g[0] + po_g[1]))
        kld = [None, None, None]
        vld = [None, None, None]
        qld = [None, None, None]
        out_q = []

        def flush_out(item):
            (cc_, ovv_) = item
            last_tp = 0
            for ls in range(4):
                pob = (po0, po1)[ls % 2]
                tv2 = p.emit("tensor", lambda e, ls=ls, pob=pob, oc=cc_ % 2: e.transpose(
                    pob[:, 0:128], of2[:, oc, ls * 128:(ls + 1) * 128], sb["id_f32"][:, :]),
                    sem=pe, waits=[(ve, ovv_)] + po_g[ls % 2])
                last_tp = tv2
                av = p.emit("vector", lambda e, ls=ls, c=cc_, pob=pob: e.tensor_tensor(
                    x3t[:, ls, c * 128:(c + 1) * 128], xrb[:, ls, c * 128:(c + 1) * 128],
                    pob[:, 0:128], ALU.add), sem=ve,
                    waits=[(pe, tv2), (sd, xld[ls]), (rd, x3_dma[ls])])
                po_g[ls % 2] = [(ve, av)]
            of2_rd[cc_ % 2] = last_tp

        for c in range(8):
            cs = (t * 8 + c) % 3
            kld[cs] = p.dma("sync", kh[:, cs, :], io["khsp"][t, c, :, :], p.r2,
                            waits=kh_rd[cs])
            vld[cs] = p.dma("gpsimd", vb[:, cs, :], io["vsp"][t, c, :, :], gd,
                            waits=vb_rd[cs])
            qld[cs] = p.dma("gpsimd", qb[:, cs, :], io["qsp"][t, c, :, :], p.g2,
                            waits=qb_rd[cs])
            pqb = (pq0, pq1)[c % 2]
            qv_q = []
            mq_q = []
            for i in range(10):
                if i < 8:
                    ci = c * 8 + i
                    ps = i % 4
                    pdb = (pd0, pd1)[i % 2]
                    pv = p.emit("gpsimd", lambda e, ci=ci, cs=cs, ps=ps: e.tensor_tensor(
                        Pb[:, ps, :], S[:, ci, :], kh[:, cs, :], ALU.mult), sem=po,
                        waits=[(p.r2, kld[cs]), (se, tanh_at[ci]), (pe, Pb_rd[ps])]
                        + ([(ve, mz)] if t == 0 else []))
                    Sread_po[ci] = (po, pv)
                    m1v = p.emit("tensor", lambda e, i=i, cs=cs, pdb=pdb: e.matmul(
                        pdb[:, :], selv[:, i, :], vb[:, cs, :], start=True, stop=False),
                        sem=pe, waits=[(gd, vld[cs]), (po, pv), (sd, sel_ld), (ve, pd_rd[i % 2])])
                    m2v = p.emit("tensor", lambda e, i=i, ps=ps, pdb=pdb: e.matmul(
                        pdb[:, :], negb[:, :], Pb[:, ps, :], start=False, stop=True), sem=pe)
                    Pb_rd[ps] = m2v
                    tv = p.emit("vector", lambda e, cs=cs, pdb=pdb, ts=i % 2: e.tensor_tensor(
                        tf[:, ts, :], pdb[:, :], kh[:, cs, :], ALU.mult), sem=ve,
                        waits=[(pe, m2v), (ve, tf_rd[i % 2])])
                    pd_rd[i % 2] = tv
                    sv = p.emit("vector", lambda e, ci=ci, ts=i % 2: e.tensor_add(
                        sf[:, ts, :], S[:, ci, :], tf[:, ts, :]), sem=ve,
                        waits=[(ve, tv), (se, tanh_at[ci]), (se, sf_rd[i % 2])])
                    tf_rd[i % 2] = sv
                    th = p.emit("scalar", lambda e, ci=ci, ts=i % 2: e.activation(
                        S[:, ci, :], sf[:, ts, :], AF.Tanh), sem=se,
                        waits=[(ve, sv)] + [Sread_po[ci]])
                    sf_rd[i % 2] = th
                    tanh_at[ci] = th
                    qv_q.append((i, ci, th))
                if len(qv_q) > (1 if i < 8 else 0):
                    (iq, ciq, thq) = qv_q.pop(0)
                    psq = iq % 4
                    qv = p.emit("vector", lambda e, ci=ciq, cs=cs, ps=psq: e.tensor_tensor(
                        Qb[:, ps, :], S[:, ci, :], qb[:, cs, :], ALU.mult), sem=ve,
                        waits=[(se, thq), (p.g2, qld[cs]), (pe, Qb_rd[psq])])
                    mq_q.append((iq, psq, qv))
                if len(mq_q) > (1 if i < 9 else 0):
                    (im, psm, qvm) = mq_q.pop(0)
                    mq = p.emit("tensor", lambda e, i=im, ps=psm, pqb=pqb: e.matmul(
                        pqb[:, :], selq[:, i, :], Qb[:, ps, :], start=(i == 0), stop=(i == 7)),
                        sem=pe, waits=[(ve, qvm)] + (pq_rd[c % 2] if im == 0 else []))
                    Qb_rd[psm] = mq
            mmq = pe.n
            ssl = p.emit("scalar", lambda e, pqb=pqb, oc=c % 2: e.activation(
                of[:, oc, :], pqb[:, :], AF.Silu),
                sem=se, waits=[(pe, mmq), (ve, of_rd[c % 2])])
            ovv = p.emit("vector", lambda e, pqb=pqb, oc=c % 2: e.tensor_tensor(
                of2[:, oc, :], of[:, oc, :], pqb[:, :], ALU.mult), sem=ve,
                waits=[(se, ssl), (pe, of2_rd[c % 2])])
            of_rd[c % 2] = ovv
            pq_rd[c % 2] = [(ve, ovv), (se, ssl)]
            kh_rd[cs] = [(ve, ve.n), (po, po.n)]
            vb_rd[cs] = [(pe, pe.n)]
            qb_rd[cs] = [(ve, ve.n)]
            out_q.append((c, ovv))
            if len(out_q) > (1 if c < 7 else 0):
                flush_out(out_q.pop(0))
        while out_q:
            flush_out(out_q.pop(0))
        for ls in range(4):
            x3_dma[ls] = p.dma("sync", x3buf[r0 + ls * 128:r0 + (ls + 1) * 128, :], x3t[:, ls, :], rd,
                               waits=[(ve, ve.n)])
    if io.get("dump_S"):
        for ci in range(64):
            p.dma("sync", io["sdump"][ci, :, :], S[:, ci, :], rd,
                  waits=[(se, se.n), (ve, ve.n)])
    if _ser:
        p.emit = _emit0



def build_phase5(p, io, E, fused):
    nc = p.nc
    x3buf, yout = io["x3buf"], io["yout"]
    sb = io["sb"]
    sd, gd, rd, pe, ve, se, po, cc = p.sd, p.gd, p.rd, p.pe, p.ve, p.se, p.po, p.cc

    wf = E(nc.sbuf_tensor("p5_wf", [128, 8, 1024], F32))
    wq8 = E(nc.sbuf_tensor("p5_wq8", [128, 8, 1024], FP8))
    xq8 = E(nc.sbuf_tensor("p5_xq8", [128, 8, 512], FP8))
    xrb = E(nc.sbuf_tensor("p5_xrb", [128, 4, 1024], F32))
    xn5 = E(nc.sbuf_tensor("p5_xn5", [128, 2, 1024], F32))
    tsc = E(nc.sbuf_tensor("p5_t", [128, 2, 512], F32))
    ytile = E(nc.sbuf_tensor("p5_y", [128, 4, 1024], F32))
    bias_sb = E(nc.sbuf_tensor("p5_bias", [128, 8], F32))
    stsum = E(nc.sbuf_tensor("p5_stsum", [128, 32], F32))
    stsq = E(nc.sbuf_tensor("p5_stsq", [128, 32], F32))
    mu5 = E(nc.sbuf_tensor("p5_mu", [128, 32], F32))
    rstd5 = E(nc.sbuf_tensor("p5_rstd", [128, 32], F32))
    tmp5 = E(nc.sbuf_tensor("p5_tmp", [128, 32], F32))
    am5 = E(nc.sbuf_tensor("p5_am", [128, 34], F32))
    wam = E(nc.sbuf_tensor("p5_wam", [128, 9], F32))
    sqd = E(nc.sbuf_tensor("p5_sqd", [128, 1024], BF16))
    scal = E(nc.sbuf_tensor("p5_scal", [1, 8], F32))
    scb = E(nc.sbuf_tensor("p5_scb", [128, 2], F32))

    pa0 = E(nc.psum_tensor("p5_pa0", [128, 512], F32))
    pa1 = E(nc.psum_tensor("p5_pa1", [128, 512], F32))
    po0 = E(nc.psum_tensor("p5_po0", [128, 512], F32))
    po1 = E(nc.psum_tensor("p5_po1", [128, 512], F32))
    pt = E(nc.psum_tensor("p5_pt", [1, 128], F32))

    p.dma("sync", bias_sb[:, :], io["bias_d"], sd)
    wld = p.dma("sync", wf[:, :, :], io["wfT"].rearrange("(c q) o -> q c o", q=128), sd)

    wv_ = []
    for c in range(8):
        wv_.append(p.emit("vector", lambda e, c=c: e.reduce_max(
            wam[:, c:c + 1], wf[:, c, :], AX.X, apply_absolute_value=True), sem=ve,
            waits=[(sd, wld)] if c == 0 else []))
    wm1 = p.emit("vector", lambda e: e.reduce_max(wam[:, 8:9], wam[:, 0:8], AX.X), sem=ve,
                 waits=[(ve, wv_[-1])])
    wtp = p.emit("tensor", lambda e: e.transpose(pt[:, :], wam[:, 8:9], sb["id_f32"][:, :]),
                 sem=pe, waits=[(ve, wm1)])
    wm2 = p.emit("vector", lambda e: e.reduce_max(scal[:, 1:2], pt[0:1, :], AX.X), sem=ve,
                 waits=[(pe, wtp)])

    vst, sst = [], []
    for si in range(32):
        slot = si % 2
        w = [(ve, vst[si - 2]), (se, sst[si - 2])] if si >= 2 else [(ve, ve.n), (se, se.n)]
        ld = p.dma("sync", xrb[:, slot, :], x3buf[si * 128:(si + 1) * 128, :], sd, waits=w)
        v = p.emit("vector", lambda e, s=slot, si=si: e.reduce_sum(stsum[:, si:si + 1], xrb[:, s, :], AX.X),
                   sem=ve, waits=[(sd, ld)])
        sv = p.emit("scalar", lambda e, s=slot, si=si: e.activation(sqd[:], xrb[:, s, :], AF.Square,
                                                                    accum_out=stsq[:, si:si + 1]),
                    sem=se, waits=[(sd, ld)])
        vst.append(v)
        sst.append(sv)
    v1 = p.emit("vector", lambda e: e.tensor_scalar_mul(mu5[:], stsum[:], 1.0 / D), sem=ve,
                waits=[(ve, vst[-1])])
    v2 = p.emit("vector", lambda e: e.tensor_scalar_mul(tmp5[:], stsq[:], 1.0 / D), sem=ve,
                waits=[(se, sst[-1]), (ve, v1)])
    v3 = p.emit("vector", lambda e: e.tensor_mul(rstd5[:], mu5[:], mu5[:]), sem=ve, waits=[(ve, v2)])
    v4 = p.emit("vector", lambda e: e.tensor_sub(tmp5[:], tmp5[:], rstd5[:]), sem=ve, waits=[(ve, v3)])
    sq1 = p.emit("scalar", lambda e: e.activation(tmp5[:], tmp5[:], AF.Sqrt, bias=sb["fc"][:, 0:1]),
                 sem=se, waits=[(ve, v4)])
    st_done = p.emit("vector", lambda e: e.reciprocal(rstd5[:], tmp5[:]), sem=ve, waits=[(se, sq1)])

    amv = []
    for si in range(32):
        slot = si % 2
        ld = p.dma("sync", xrb[:, slot, :], x3buf[si * 128:(si + 1) * 128, :], sd,
                   waits=[(ve, ve.n), (se, se.n)])
        nv = p.emit("vector", lambda e, s=slot, si=si: e.tensor_scalar(
            xn5[:, s, :], xrb[:, s, :], mu5[:, si:si + 1], rstd5[:, si:si + 1],
            ALU.subtract, ALU.mult), sem=ve, waits=[(sd, ld), (ve, st_done)])
        av = p.emit("vector", lambda e, s=slot, si=si: e.reduce_max(
            am5[:, si:si + 1], xn5[:, s, :], AX.X, apply_absolute_value=True), sem=ve,
            waits=[(ve, nv)])
        amv.append(av)
    am1 = p.emit("vector", lambda e: e.reduce_max(am5[:, 32:33], am5[:, 0:32], AX.X), sem=ve,
                 waits=[(ve, amv[-1])])
    atp = p.emit("tensor", lambda e: e.transpose(pt[:, :], am5[:, 32:33], sb["id_f32"][:, :]),
                 sem=pe, waits=[(ve, am1), (ve, wm2)])
    am2 = p.emit("vector", lambda e: e.reduce_max(scal[:, 0:1], pt[0:1, :], AX.X), sem=ve,
                 waits=[(pe, atp)])

    if fused:
        dm = p.dma("gpsimd", io["amx_in"], scal[:, 0:1], gd, waits=[(ve, am2)])
        ccv = p.emit("gpsimd", lambda e: e.collective_compute(
            "AllReduce", ALU.max, replica_groups=[list(range(NCORES))],
            ins=[io["amx_in_t"].ap().opt()], outs=[io["amx_out_t"].ap().opt()]),
            sem=cc, waits=[(gd, dm)])
        am3 = p.dma("sync", scal[:, 0:1], io["amx_out"], sd, waits=[(cc, ccv)])
    else:
        am3 = sd.n

    c1 = p.emit("vector", lambda e: e.tensor_scalar_max(scal[:, 0:1], scal[:, 0:1], 1e-12), sem=ve,
                waits=[(sd, am3), (ve, am2)])
    c2 = p.emit("vector", lambda e: e.reciprocal(scal[:, 2:3], scal[:, 0:1]), sem=ve, waits=[(ve, c1)])
    c3 = p.emit("vector", lambda e: e.tensor_scalar_mul(scal[:, 2:3], scal[:, 2:3], 224.0), sem=ve,
                waits=[(ve, c2)])
    c4 = p.emit("vector", lambda e: e.reciprocal(scal[:, 3:4], scal[:, 1:2]), sem=ve, waits=[(ve, c3)])
    c5 = p.emit("vector", lambda e: e.tensor_scalar_mul(scal[:, 3:4], scal[:, 3:4], 224.0), sem=ve,
                waits=[(ve, c4)])
    c6 = p.emit("vector", lambda e: e.tensor_tensor(scal[:, 4:5], scal[:, 2:3], scal[:, 3:4], ALU.mult),
                sem=ve, waits=[(ve, c5)])
    c7 = p.emit("vector", lambda e: e.reciprocal(scal[:, 4:5], scal[:, 4:5]), sem=ve, waits=[(ve, c6)])
    b1 = p.emit("tensor", lambda e: e.matmul(po0[:, 0:2], sb["ones_row"][:, :],
                                             scal[:, 2:5:2], start=True, stop=True),
                sem=pe, waits=[(ve, c7)])
    b2 = p.emit("vector", lambda e: e.tensor_copy(scb[:, :], po0[:, 0:2]), sem=ve, waits=[(pe, b1)])

    b3 = p.emit("tensor", lambda e: e.matmul(po1[:, 0:1], sb["ones_row"][:, :], scal[:, 3:4],
                                             start=True, stop=True), sem=pe, waits=[(ve, c7)])
    b4 = p.emit("vector", lambda e: e.tensor_copy(wam[:, 0:1], po1[:, 0:1]), sem=ve, waits=[(pe, b3)])
    wq_done = []
    for c in range(8):
        wq_done.append(p.emit("vector", lambda e, c=c: e.tensor_scalar_mul(
            wq8[:, c, :], wf[:, c, :], wam[:, 0:1]), sem=ve,
            waits=[(ve, b4)] if c == 0 else []))

    po_guard = {id(po0): [(ve, b2)], id(po1): [(ve, b4)]}
    pa_guard = {id(pa0): [], id(pa1): []}
    y_dma = [0, 0, 0, 0]
    for rg in range(8):
        r0 = rg * 512
        xld = []
        for ls in range(4):
            ld = p.dma("sync", xrb[:, ls, :], x3buf[r0 + ls * 128:r0 + (ls + 1) * 128, :], sd,
                       waits=[(ve, ve.n)])
            xld.append(ld)
            nv = p.emit("vector", lambda e, ls=ls, si=rg * 4 + ls: e.tensor_scalar(
                xn5[:, ls % 2, :], xrb[:, ls, :], mu5[:, si:si + 1], rstd5[:, si:si + 1],
                ALU.subtract, ALU.mult), sem=ve, waits=[(sd, ld), (ve, st_done), (pe, pe.n)])
            for c in range(8):
                pob = (po0, po1)[c % 2]
                tv = p.emit("tensor", lambda e, ls=ls, c=c, pob=pob: e.transpose(
                    pob[:, 0:128], xn5[:, ls % 2, c * 128:(c + 1) * 128], sb["id_f32"][:, :]),
                    sem=pe, waits=[(ve, nv)] + po_guard[id(pob)])
                qv = p.emit("vector", lambda e, ls=ls, c=c, pob=pob: e.tensor_scalar_mul(
                    xq8[:, c, ls * 128:(ls + 1) * 128], pob[:, 0:128], scb[:, 0:1]),
                    sem=ve, waits=[(pe, tv), (ve, b2)])
                po_guard[id(pob)] = [(ve, qv)]
        q_done = ve.n
        for m in range(8):
            pa = (pa0, pa1)[m % 2]
            for kk in range(8):
                w = []
                if kk == 0:
                    w = pa_guard[id(pa)] + [(ve, q_done)]
                    if rg == 0 and m == 0:
                        w.append((ve, wq_done[-1]))
                p.emit("tensor", lambda e, m=m, kk=kk, pa=pa: e.matmul(
                    pa[:, :], wq8[:, kk, m * 128:(m + 1) * 128], xq8[:, kk, :],
                    start=(kk == 0), stop=(kk == 7)), sem=pe, waits=w)
            mm = pe.n
            ts = m % 2
            sv = p.emit("vector", lambda e, m=m, pa=pa, ts=ts: e.tensor_scalar(
                tsc[:, ts, :], pa[:, :], scb[:, 1:2], bias_sb[:, m:m + 1], ALU.mult, ALU.add),
                sem=ve, waits=[(pe, mm), (pe, pe.n), (ve, ve.n)])
            pa_guard[id(pa)] = [(ve, sv)]
            for wq_ in range(4):
                ls = wq_
                pob = (po0, po1)[wq_ % 2]
                tv2 = p.emit("tensor", lambda e, ts=ts, wq_=wq_, pob=pob: e.transpose(
                    pob[:, 0:128], tsc[:, ts, wq_ * 128:(wq_ + 1) * 128], sb["id_f32"][:, :]),
                    sem=pe, waits=[(ve, sv)] + po_guard[id(pob)])
                av = p.emit("vector", lambda e, ls=ls, m=m, pob=pob: e.tensor_tensor(
                    ytile[:, ls, m * 128:(m + 1) * 128], xrb[:, ls, m * 128:(m + 1) * 128],
                    pob[:, 0:128], ALU.add), sem=ve,
                    waits=[(pe, tv2), (p.od, y_dma[ls])])
                po_guard[id(pob)] = [(ve, av)]
        for ls in range(4):
            y_dma[ls] = p.dma("sync", yout[r0 + ls * 128:r0 + (ls + 1) * 128, :], ytile[:, ls, :],
                              p.od, waits=[(ve, ve.n)])



def build_nc(phases=("p1",), dev_out=(), l1_external=False, fused_amax=True, dump_S=False):
    nc = bass.Bass("TRN2", target_bir_lowering=False, debug=False)
    ctx = contextlib.ExitStack()
    io = {}

    def dram(name, shape, dtype, kind="Internal"):
        t = nc.dram_tensor(name, shape, dtype, kind=kind)
        io[name] = t.ap()
        io[name + "_t"] = t

    has_l0 = "p1" in phases or "p2" in phases
    has_l1 = "p4a" in phases or "p4b" in phases or "p5" in phases

    for n, sh, dt in (("pshift_d", [128, 128], BF16), ("tri_d", [128, 128], F32),
                      ("id_bf_d", [128, 128], BF16), ("id_f32_d", [128, 128], F32),
                      ("ones_row_d", [1, 128], F32), ("ones_col_bf_d", [128, 1], BF16),
                      ("lnw_d", [128, 8], F32), ("lnb_d", [128, 8], F32),
                      ("fc_d", [128, 4], F32), ("sel4_d", [4, 4, 128], F32)):
        dram(n, sh, dt, kind="ExternalInput")

    if has_l0:
        dram("xin", [SEQ, D], F32, kind="ExternalInput")
        dram("xinT", [D, SEQ], F32, kind="ExternalInput")
        for n, sh in (("Wq", [D, D]), ("Wk", [D, D]), ("Wv", [D, D]), ("Wo", [D, D]),
                      ("Wg1", [D, 64]), ("Wg2", [64, D])):
            dram(n, sh, F32, kind="ExternalInput")
        dram("cosd", [128, SEQ], BF16, kind="ExternalInput")
        dram("sind", [128, SEQ], BF16, kind="ExternalInput")
        dram("resid", [SEQ, D], F32,
             kind="ExternalOutput" if "resid" in dev_out else "Internal")
        dram("kspill", [8, 128, SEQ], BF16)
        dram("vspill", [8, 128, 32, 128], BF16)
        dram("w12P", [32, 16, 128, 128], F32, kind="ExternalInput")
        dram("h1P", [32, 32, 128, 128], F32, kind="ExternalInput")
        dram("h2P", [32, 32, 128, 128], F32, kind="ExternalInput")
        dram("w3P", [8, 32, 128, 128], F32, kind="ExternalInput")
        for n in ("s1_d", "s2_d", "hs1_d", "hs2_d"):
            dram(n, [128, 32], F32, kind="ExternalInput")
        dram("s3_d", [128, 8], F32, kind="ExternalInput")

    if has_l1:
        for n in ("wkT", "wvT", "wqT", "wfT"):
            dram(n, [D, D], F32, kind="ExternalInput")
        dram("bias_d", [128, 8], F32, kind="ExternalInput")
        dram("posblk_d", [128, 128], F32, kind="ExternalInput")
        dram("selv_d", [8, 128, 128], BF16, kind="ExternalInput")
        dram("selq_d", [8, 128, 128], BF16, kind="ExternalInput")
        dram("negblk_d", [128, 128], BF16, kind="ExternalInput")
        dram("khsp", [8, 8, 128, 512], F32,
             kind="ExternalOutput" if "khsp" in dev_out else "Internal")
        dram("vsp", [8, 8, 128, 512], BF16,
             kind="ExternalOutput" if "vsp" in dev_out else "Internal")
        dram("qsp", [8, 8, 128, 512], BF16,
             kind="ExternalOutput" if "qsp" in dev_out else "Internal")
        dram("x3buf", [SEQ, D], F32,
             kind="ExternalOutput" if "x3buf" in dev_out else "Internal")
        dram("sdump", [64, 128, 512], F32,
             kind="ExternalOutput" if "sdump" in dev_out else "Internal")
        dram("amx_in", [1, 1], F32)
        dram("amx_out", [1, 1], F32)
        dram("yout", [SEQ, D], F32,
             kind="ExternalOutput" if "yout" in dev_out else "Internal")
        if l1_external:
            dram("xl1", [SEQ, D], F32, kind="ExternalInput")
            io["l1in"] = io["xl1"]
        else:
            dram("a2aout", [SEQ, D], F32)
            io["l1in"] = io["a2aout"]

    with ctx:
        E = ctx.enter_context
        p = Prog(nc, ctx)
        sb = {}
        sb["id_bf"] = E(nc.sbuf_tensor("c_idbf", [128, 128], BF16))
        sb["pshift"] = E(nc.sbuf_tensor("c_pshift", [128, 128], BF16))
        sb["tri"] = E(nc.sbuf_tensor("c_tri", [128, 128], F32))
        sb["ones_row"] = E(nc.sbuf_tensor("c_onesrow", [1, 128], F32))
        sb["ones_col_bf"] = E(nc.sbuf_tensor("c_onescol", [128, 1], BF16))
        sb["lnw"] = E(nc.sbuf_tensor("c_lnw", [128, 8], F32))
        sb["lnb"] = E(nc.sbuf_tensor("c_lnb", [128, 8], F32))
        sb["fc"] = E(nc.sbuf_tensor("c_fc", [128, 4], F32))
        sb["id_f32"] = E(nc.sbuf_tensor("c_idf32", [128, 128], F32))
        sb["sel4"] = E(nc.sbuf_tensor("c_sel4", [4, 4, 128], F32))
        io["sb"] = sb
        for name, d in (("id_bf", "id_bf_d"), ("pshift", "pshift_d"), ("tri", "tri_d"),
                        ("ones_row", "ones_row_d"), ("ones_col_bf", "ones_col_bf_d"),
                        ("lnw", "lnw_d"), ("lnb", "lnb_d"), ("fc", "fc_d"),
                        ("id_f32", "id_f32_d"), ("sel4", "sel4_d")):
            p.dma("sync", sb[name][tuple([slice(None)] * 2)], io[d], p.sd)

        if "p1" in phases:
            with contextlib.ExitStack() as phase_ctx:
                build_phase1(p, io, phase_ctx.enter_context)
            p.barrier()

        if "p2" in phases:
            with contextlib.ExitStack() as phase_ctx:
                build_phase2(p, io, phase_ctx.enter_context)
            p.barrier()

        if "p3" in phases:
            ccv = p.emit("gpsimd", lambda e: e.collective_compute(
                "AllToAll", ALU.bypass, replica_groups=[list(range(NCORES))],
                ins=[io["resid_t"].ap().opt()], outs=[io["a2aout_t"].ap().opt()]),
                sem=p.cc, waits=p.wait_all())
            p.barrier()

        if "p4a" in phases:
            with contextlib.ExitStack() as phase_ctx:
                build_phase4a(p, io, phase_ctx.enter_context)
            p.barrier()

        io["dump_S"] = dump_S
        if "p4b" in phases:
            with contextlib.ExitStack() as phase_ctx:
                build_phase4b(p, io, phase_ctx.enter_context)
            p.barrier()

        if "p5" in phases:
            with contextlib.ExitStack() as phase_ctx:
                build_phase5(p, io, phase_ctx.enter_context, fused_amax)
            p.barrier()

        vals = tuple(p.wait_all())
        with nc.Block() as block:
            p.realize(block)

            @block.sync
            def _(e):
                for sm, v in vals:
                    e.wait_ge(sm.h, v)
    return nc


def make_inmaps(inputs, consts=None, weights=None, l0=True, l1=True, xl1=None):
    consts = consts or host_consts()
    weights = weights or host_weights(inputs)
    base = {
        "pshift_d": consts["pshift"], "tri_d": consts["tri"], "id_bf_d": consts["id_bf"],
        "id_f32_d": consts["id_f32"], "ones_row_d": consts["ones_row"],
        "ones_col_bf_d": consts["ones_col_bf"], "lnw_d": weights["lnw"],
        "lnb_d": weights["lnb"], "fc_d": consts["fconsts"], "sel4_d": consts["sel4"],
    }
    if l0:
        base.update({
            "Wq": weights["L0_attn_Wq"], "Wk": weights["L0_attn_Wk"], "Wv": weights["L0_attn_Wv"],
            "Wo": weights["L0_attn_Wo"], "Wg1": weights["L0_attn_Wg1"], "Wg2": weights["L0_attn_Wg2"],
            "cosd": consts["cosd"], "sind": consts["sind"],
            "w12P": weights["w12P"], "h1P": weights["h1P"],
            "h2P": weights["h2P"], "w3P": weights["w3P"],
            "s1_d": weights["s1"], "s2_d": weights["s2"], "hs1_d": weights["hs1"],
            "hs2_d": weights["hs2"], "s3_d": weights["s3"],
        })
    if l1:
        base.update({
            "wkT": weights["wkT"], "wvT": weights["wvT"], "wqT": weights["wqT"],
            "wfT": weights["wfT"], "bias_d": weights["bias"],
            "posblk_d": consts["posblk"], "selv_d": consts["selv"], "selq_d": consts["selq"],
            "negblk_d": consts["negblk"],
        })
    in_maps = []
    x = np.asarray(inputs["L0_attn_x"], dtype=np.float32) if l0 else None
    for c in range(NCORES):
        m = dict(base)
        if l0:
            m["xin"] = np.ascontiguousarray(x[c])
            m["xinT"] = np.ascontiguousarray(x[c].T)
        if xl1 is not None:
            m["xl1"] = np.ascontiguousarray(xl1[c])
        in_maps.append(m)
    return in_maps


def kernel(**inputs):
    consts = host_consts()
    weights = host_weights(inputs)
    with allow_ici_all_to_all():
        nc = build_nc(phases=("p1", "p2", "p3", "p4a", "p4b", "p5"), dev_out=("yout",),
                      l1_external=False)
        in_maps = make_inmaps(inputs, consts, weights, l0=True, l1=True)
        res = run_bass_kernel_spmd(nc, in_maps, core_ids=list(range(NCORES)))
    out = np.empty((NCORES, SEQ, D), dtype=np.float32)
    for c in range(NCORES):
        out[:, c * LANES:(c + 1) * LANES, :] = res.results[c]["yout"].reshape(NCORES, LANES, D)
    return out


# version 27
# speedup vs baseline: 1.0685x; 1.0310x over previous
import contextlib
import numpy as np
import ml_dtypes
import concourse.bass as bass
import concourse.mybir as mybir
from concourse.bass_utils import run_bass_kernel_spmd
from concourse.aot_env import allow_ici_all_to_all

F32 = mybir.dt.float32
BF16 = mybir.dt.bfloat16
FP8 = mybir.dt.float8e4
AF = mybir.ActivationFunctionType
ALU = mybir.AluOpType
AX = mybir.AxisListType

NCORES = 8
D = 1024
SEQ = 4096
HEADS = 16
HD = 64
DHID = 4096
LANES = SEQ // NCORES
EPS = 1e-5

ENGINES = ("sync", "gpsimd", "tensor", "vector", "scalar")



def host_consts():
    c = {}
    inv = (1.0 / (10000.0 ** (np.arange(0, HD, 2, dtype=np.float32) / np.float32(HD)))).astype(np.float32)
    pos = np.arange(SEQ, dtype=np.float32)
    ang = pos[:, None] * inv[None, :]
    cos = np.cos(ang).astype(np.float32)
    sin = np.sin(ang).astype(np.float32)
    cos64 = np.concatenate([cos, cos], axis=1).T
    sin64 = np.concatenate([sin, sin], axis=1).T
    c["cosd"] = np.ascontiguousarray(np.vstack([cos64, cos64])).astype(ml_dtypes.bfloat16)
    c["sind"] = np.ascontiguousarray(np.vstack([sin64, sin64])).astype(ml_dtypes.bfloat16)
    P = np.zeros((128, 128), dtype=np.float32)
    for b in (0, 64):
        for m in range(32):
            P[b + m + 32, b + m] = -1.0
            P[b + m, b + m + 32] = 1.0
    c["pshift"] = P.astype(ml_dtypes.bfloat16)
    tri = (np.arange(128)[None, :] >= np.arange(128)[:, None]).astype(np.float32)
    c["tri"] = tri
    c["id_bf"] = np.eye(128, dtype=np.float32).astype(ml_dtypes.bfloat16)
    c["id_f32"] = np.eye(128, dtype=np.float32)
    c["ones_row"] = np.ones((1, 128), dtype=np.float32)
    c["ones_col_bf"] = np.ones((128, 1), dtype=np.float32).astype(ml_dtypes.bfloat16)
    fc = np.zeros((128, 4), dtype=np.float32)
    fc[:, 0] = 1e-5
    fc[:, 1] = 1e-6
    fc[:, 2] = 0.5
    fc[:, 3] = -0.5
    c["fconsts"] = fc
    selv = np.zeros((8, 128, 128), dtype=np.float32)
    selq = np.zeros((8, 128, 128), dtype=np.float32)
    negblk = np.zeros((128, 128), dtype=np.float32)
    for pp in range(128):
        for m in range(128):
            if pp // 8 == m // 8:
                negblk[pp, m] = -1.0
                selq[m % 8, pp, m] = 1.0
    for i in range(8):
        for m in range(128):
            selv[i, (m // 8) * 8 + i, m] = 1.0
    c["selv"] = selv.astype(ml_dtypes.bfloat16)
    c["selq"] = selq.astype(ml_dtypes.bfloat16)
    c["negblk"] = negblk.astype(ml_dtypes.bfloat16)
    c["posblk"] = (-negblk).astype(np.float32)
    sel4 = np.zeros((4, 4, 128), dtype=np.float32)
    for i in range(4):
        sel4[i, i, :] = 1.0
    c["sel4"] = sel4
    return c


def host_weights(inputs):
    w = {}
    t = lambda a: np.ascontiguousarray(np.asarray(a, dtype=np.float32).T)
    for n in ("L0_attn_Wq", "L0_attn_Wk", "L0_attn_Wv", "L0_attn_Wo", "L0_attn_Wg1", "L0_attn_Wg2"):
        w[n] = np.ascontiguousarray(np.asarray(inputs[n], dtype=np.float32))
    def pack(a):
        K, M = a.shape
        return np.ascontiguousarray(a.reshape(K // 128, 128, M // 128, 128).transpose(2, 0, 1, 3))
    w["w1P"] = pack(t(inputs["L0_ffn_w1_w"]))
    w["w2P"] = pack(t(inputs["L0_ffn_w2_w"]))
    w["w12P"] = np.ascontiguousarray(np.concatenate(
        [w["w1P"], w["w2P"]], axis=1))
    w["h1P"] = pack(t(inputs["L0_ffn_h1_w"]))
    w["h2P"] = pack(t(inputs["L0_ffn_h2_w"]))
    w["w3P"] = pack(t(inputs["L0_ffn_w3_w"]))
    pc = lambda a: np.ascontiguousarray(np.asarray(a, dtype=np.float32).reshape(-1, 128).T)
    w["s1"] = pc(inputs["L0_ffn_w1_s"])
    w["s2"] = pc(inputs["L0_ffn_w2_s"])
    w["hs1"] = pc(inputs["L0_ffn_h1_s"])
    w["hs2"] = pc(inputs["L0_ffn_h2_s"])
    w["s3"] = pc(inputs["L0_ffn_w3_s"])
    w["lnw"] = pc(inputs["L0_attn_ln_w"])
    w["lnb"] = pc(inputs["L0_attn_ln_b"])
    w["wkT"] = t(inputs["L1_attn_W_k"])
    w["wvT"] = t(inputs["L1_attn_W_v"])
    w["wqT"] = t(inputs["L1_attn_W_q"])
    w["wfT"] = t(inputs["L1_ffn_weight"])
    w["bias"] = pc(inputs["L1_ffn_bias"])
    return w



class Sem:
    __slots__ = ("h", "n", "name")

    def __init__(self, h, name):
        self.h = h
        self.n = 0
        self.name = name


class Prog:
    SEMS = ("sd", "gd", "rd", "pe", "ve", "se", "po", "od", "cc", "s2", "g2", "r2", "g3")

    def __init__(self, nc, ctx):
        self.nc = nc
        self.ctx = ctx
        self.st = {e: [] for e in ENGINES}
        for name in self.SEMS:
            setattr(self, name, Sem(ctx.enter_context(nc.semaphore(name)), name))

    def emit(self, eng, fn, sem=None, inc=1, waits=()):
        waits = tuple(waits)
        if sem is not None:
            sem.n += inc
        semh = sem.h if sem is not None else None

        def run(e, fn=fn, waits=waits, semh=semh, inc=inc):
            for s, v in waits:
                e.wait_ge(s.h, v)
            ins = fn(e)
            if semh is not None:
                ins.then_inc(semh, inc)
        self.st[eng].append(run)
        return sem.n if sem is not None else None

    def dma(self, q, out, in_, sem, waits=()):
        gate = (sem, sem.n)
        return self.emit(q, lambda e: e.dma_start(out=out, in_=in_), sem=sem, inc=16,
                         waits=tuple(waits) + (gate,))

    def wait_all(self):
        return [(getattr(self, n), getattr(self, n).n) for n in self.SEMS if getattr(self, n).n > 0]

    def barrier(self):
        vals = tuple(self.wait_all())
        for eng in ENGINES:
            def run(e, vals=vals):
                for s, v in vals:
                    e.wait_ge(s.h, v)
            self.st[eng].append(run)

    def realize(self, block):
        for eng in ENGINES:
            ops = self.st[eng]
            if ops:
                getattr(block, eng)(lambda e, ops=ops: [f(e) for f in ops])



def build_phase1(p, io, E):
    nc = p.nc
    xin, resid, kspill = io["xin"], io["resid"], io["kspill"]
    sd, gd, rd, pe, ve, se, po = p.sd, p.gd, p.rd, p.pe, p.ve, p.se, p.po

    wbuf = E(nc.sbuf_tensor("p1_wbuf", [128, 16, 1024], BF16))
    wg1 = E(nc.sbuf_tensor("p1_wg1", [128, 8, 64], BF16))
    wg2 = E(nc.sbuf_tensor("p1_wg2", [64, 1024], BF16))
    vbuf = E(nc.sbuf_tensor("p1_vbuf", [128, 3, 32, 128], BF16))
    vstage = E(nc.sbuf_tensor("p1_vstage", [128, 4, 512], BF16))
    kbuf = E(nc.sbuf_tensor("p1_kbuf", [128, 3, 4096], BF16))
    cosg = E(nc.sbuf_tensor("p1_cosg", [128, 2, 2, 512], BF16))
    xsub = E(nc.sbuf_tensor("p1_xsub", [128, 2, 1024], F32))
    xtf = E(nc.sbuf_tensor("p1_xtf", [128, 2, 8, 512], BF16))
    xn_fm = E(nc.sbuf_tensor("p1_xnfm", [128, 2, 8, 512], BF16))
    murow = E(nc.sbuf_tensor("p1_murow", [4, 128], F32))
    rsrow = E(nc.sbuf_tensor("p1_rsrow", [4, 128], F32))
    q_fm = E(nc.sbuf_tensor("p1_qfm", [128, 8, 512], BF16))
    g1sb = E(nc.sbuf_tensor("p1_g1", [64, 512], BF16))
    gate = E(nc.sbuf_tensor("p1_gate", [128, 8, 512], BF16))
    attn = E(nc.sbuf_tensor("p1_attn", [128, 8, 512], BF16))
    xo = E(nc.sbuf_tensor("p1_xo", [128, 8, 512], BF16))
    esb = E(nc.sbuf_tensor("p1_esb", [128, 6, 512], BF16))
    ksil = E(nc.sbuf_tensor("p1_ksil", [128, 3, 512], BF16))
    ropeA = E(nc.sbuf_tensor("p1_ropeA", [128, 2, 512], F32))
    ropeB = E(nc.sbuf_tensor("p1_ropeB", [128, 512], F32))
    lnsA = E(nc.sbuf_tensor("p1_lnsA", [128, 512], F32))
    lnsB = E(nc.sbuf_tensor("p1_lnsB", [128, 512], F32))
    sq = E(nc.sbuf_tensor("p1_sq", [128, 512], BF16))
    x1t = E(nc.sbuf_tensor("p1_x1t", [128, 2, 1024], F32))
    ssum = E(nc.sbuf_tensor("p1_ssum", [128, 32], F32))
    ssq = E(nc.sbuf_tensor("p1_ssq", [128, 32], F32))
    sqdump = xtf
    mu = E(nc.sbuf_tensor("p1_mu", [128, 32], F32))
    rstd = E(nc.sbuf_tensor("p1_rstd", [128, 32], F32))
    tmp32 = E(nc.sbuf_tensor("p1_tmp32", [128, 32], F32))

    srow = lambda i: x1t[0:1, i // 2, (i % 2) * 512:(i % 2) * 512 + 512]

    pqA0 = E(nc.psum_tensor("p1_pqA0", [128, 512], F32))
    pqA1 = E(nc.psum_tensor("p1_pqA1", [128, 512], F32))
    pqA2 = E(nc.psum_tensor("p1_pqA2", [128, 512], F32))
    pqB0 = E(nc.psum_tensor("p1_pqB0", [128, 512], F32))
    pqB1 = E(nc.psum_tensor("p1_pqB1", [128, 512], F32))
    pqB2 = E(nc.psum_tensor("p1_pqB2", [128, 512], F32))
    ov = E(nc.psum_tensor("p1_ov", [128, 512], F32))
    m1 = E(nc.psum_tensor("p1_m1", [128, 512], F32))

    sb = io["sb"]

    wk_ld = p.dma("gpsimd", wbuf[:, 0:8, :], io["Wk"].rearrange("(c p) o -> p c o", p=128), gd)
    wv_ld = p.dma("gpsimd", wbuf[:, 8:16, :], io["Wv"].rearrange("(c p) o -> p c o", p=128), gd)

    ve_tile, se_tile = [], []
    for i in range(32):
        slot = i % 2
        w = [(ve, ve_tile[i - 2]), (se, se_tile[i - 2])] if i >= 2 else []
        ld = p.dma("sync", xsub[:, slot, :], xin[i * 128:(i + 1) * 128, :], sd, waits=w)
        v = p.emit("vector", lambda e, s=slot, i=i: e.reduce_sum(ssum[:, i:i + 1], xsub[:, s, :], AX.X),
                   sem=ve, waits=[(sd, ld)])
        s = p.emit("scalar", lambda e, s=slot, i=i: e.activation(sqdump[:, 0, 0:2, :], xsub[:, s, :],
                                                                 AF.Square, accum_out=ssq[:, i:i + 1]),
                   sem=se, waits=[(sd, ld)])
        ve_tile.append(v)
        se_tile.append(s)
    v1 = p.emit("vector", lambda e: e.tensor_scalar_mul(mu[:], ssum[:], 1.0 / D), sem=ve,
                waits=[(ve, ve_tile[-1])])
    v2 = p.emit("vector", lambda e: e.tensor_scalar_mul(tmp32[:], ssq[:], 1.0 / D), sem=ve,
                waits=[(se, se_tile[-1]), (ve, v1)])
    v3 = p.emit("vector", lambda e: e.tensor_mul(rstd[:], mu[:], mu[:]), sem=ve, waits=[(ve, v2)])
    v4 = p.emit("vector", lambda e: e.tensor_sub(tmp32[:], tmp32[:], rstd[:]), sem=ve, waits=[(ve, v3)])
    s1 = p.emit("scalar", lambda e: e.activation(tmp32[:], tmp32[:], AF.Sqrt, bias=sb["fc"][:, 0:1]), sem=se,
                waits=[(ve, v4)])
    stats_done = p.emit("vector", lambda e: e.reciprocal(rstd[:], tmp32[:]), sem=ve, waits=[(se, s1)])

    xtf_rd = [0, 0]
    sqdump_se = [se.n]
    bc_rd = [[], []]
    murow_rd = [0]

    def xtf_load(g):
        xs = g % 2
        return p.dma("gpsimd", xtf[:, xs, :, :],
                     io["xinT"][:, g * 512:(g + 1) * 512].rearrange("(c q) n -> q c n", q=128),
                     gd, waits=[(ve, xtf_rd[xs]), (se, sqdump_se[0])])

    def build_xn(g, xld2):
        xs = g % 2
        tmu = p.emit("tensor", lambda e, g=g: e.transpose(
            m1[0:4, 0:128], mu[:, g * 4:(g + 1) * 4], sb["id_f32"][:, :]),
            sem=pe, waits=[(ve, stats_done), (se, proj_se_g[0]), (se, proj_se_g[1]),
                           (ve, ve.n)])
        cm = p.emit("vector", lambda e: e.tensor_copy(murow[:, :], m1[0:4, 0:128]),
                    sem=ve, waits=[(pe, tmu)])
        tru = p.emit("tensor", lambda e, g=g: e.transpose(
            m1[0:4, 128:256], rstd[:, g * 4:(g + 1) * 4], sb["id_f32"][:, :]),
            sem=pe, waits=[(ve, cm)])
        cr = p.emit("vector", lambda e: e.tensor_copy(rsrow[:, :], m1[0:4, 128:256]),
                    sem=ve, waits=[(pe, tru)])
        for si in range(4):
            p.emit("tensor", lambda e, si=si: e.matmul(
                pqA2[:, si * 128:(si + 1) * 128], sb["sel4"][:, si, :], murow[:, :],
                start=True, stop=True),
                sem=pe, waits=([(ve, cm), (ve, ve.n), (se, se.n)] + bc_rd[0]) if si == 0 else [])
            p.emit("tensor", lambda e, si=si: e.matmul(
                pqB2[:, si * 128:(si + 1) * 128], sb["sel4"][:, si, :], rsrow[:, :],
                start=True, stop=True),
                sem=pe, waits=([(ve, cr)] + bc_rd[1]) if si == 0 else [])
        bdone = pe.n
        lastv = None
        for c in range(8):
            t1 = p.emit("vector", lambda e, c=c, xs=xs: e.tensor_tensor(
                lnsA[:, :], xtf[:, xs, c, :], pqA2[:, :], ALU.subtract), sem=ve,
                waits=[(pe, bdone), (gd, xld2)] + ([(pe, pe.n)] if c == 0 else []))
            lastv = p.emit("vector", lambda e, c=c, xs=xs: e.tensor_tensor(
                xn_fm[:, xs, c, :], lnsA[:, :], pqB2[:, :], ALU.mult), sem=ve,
                waits=[(ve, t1)])
        xtf_rd[xs] = lastv
        bc_rd[0] = [(ve, lastv)]
        bc_rd[1] = [(ve, lastv)]
        murow_rd[0] = lastv
        return lastv

    def proj_rope_mm(c, cslot, xs, dst_ap, first_wait=()):
        kb = c % 2
        ks = c % 3
        mm_bank = (m1, pqB0)[kb]
        for kk in range(8):
            w = list(first_wait) if kk == 0 else []
            if kk == 0:
                w.append((se, proj_se_g[kb]))
            p.emit("tensor", lambda e, c=c, kk=kk, mm_bank=mm_bank, xs=xs: e.matmul(
                mm_bank[:, :], wbuf[:, kk, c * 128:(c + 1) * 128], xn_fm[:, xs, kk, :],
                start=(kk == 0), stop=(kk == 7)), sem=pe, waits=w)
        mmk = pe.n
        kw = kst_dma[ks]
        sv = p.emit("scalar", lambda e, ks=ks, mm_bank=mm_bank: e.activation(
            ksil[:, ks, :], mm_bank[:, :], AF.Silu),
            sem=se, waits=[(pe, mmk), (ve, ksil_ve_g[ks])] + ([kw] if kw else []))
        proj_se_g[kb] = sv
        return (c, cslot, dst_ap, kb, ks, sv)

    def proj_rope_tail(tok, cldw):
        (c, cslot, dst_ap, kb, ks, sv) = tok
        rp_bank = (pqA0, pqA1)[kb]
        rv = p.emit("tensor", lambda e, ks=ks, rp_bank=rp_bank: e.matmul(
            rp_bank[:, :], sb["pshift"][:, :], ksil[:, ks, :], start=True, stop=True),
            sem=pe, waits=[(se, sv), (ve, rope_ve_g[kb]), (se, rope_se_g[kb])])
        av = p.emit("vector", lambda e, ks=ks, kb=kb, cs=cslot: e.tensor_tensor(
            ropeA[:, kb, :], ksil[:, ks, :], cosg[:, cs, 0, :], ALU.mult),
            sem=ve, waits=[(se, sv)])
        tv = p.emit("vector", lambda e, cs=cslot, kb=kb, rp_bank=rp_bank: e.tensor_tensor(
            ropeB[:, :], rp_bank[:, :], cosg[:, cs, 1, :], ALU.mult),
            sem=ve, waits=[(pe, rv), (p.s2, cldw)])
        rope_ve_g[kb] = tv
        kv2 = p.emit("vector", lambda e, dst=dst_ap, kb=kb: e.tensor_add(
            dst, ropeA[:, kb, :], ropeB[:, :]), sem=ve, waits=[(ve, tv)])
        ksil_ve_g[ks] = kv2
        return ks, kv2

    proj_se_g = [0, 0]
    rope_se_g = [0, 0]
    ksil_ve_g = [0, 0, 0]
    rope_ve_g = [0, 0]
    kst_dma = [None, None, None]
    vst_dma = [0, 0]

    bank_vrd = {id(pqB1): [], id(ov): []}
    xld = [None, None]
    xld[0] = xtf_load(0)
    xn_ve_next = None
    for g in range(8):
        cslot = g % 2
        p.dma("sync", cosg[:, cslot, 0, :], io["cosd"][:, g * 512:(g + 1) * 512], p.s2,
              waits=[(ve, ve.n)])
        cld = p.dma("sync", cosg[:, cslot, 1, :], io["sind"][:, g * 512:(g + 1) * 512], p.s2)
        if g == 0:
            xn_ve = build_xn(0, xld[0])
        else:
            xn_ve = xn_ve_next
        if g < 7:
            xld[(g + 1) % 2] = xtf_load(g + 1)

        def flush_k(tok, g=g, cld=cld):
            ks, kv2 = proj_rope_tail(tok, cld)
            qsem = (rd, sd)[tok[0] % 2]
            kst_dma[ks] = (qsem, p.dma("sync", kspill[tok[0], :, g * 512:(g + 1) * 512],
                                       ksil[:, ks, :], qsem, waits=[(ve, kv2)]))

        pend = None
        for c in range(8):
            fw = [(ve, xn_ve), (p.s2, cld)]
            if g == 0 and c == 0:
                fw.append((gd, wk_ld))
            tok = proj_rope_mm(c, cslot, g % 2, ksil[:, c % 3, :], first_wait=fw)
            if pend is not None:
                flush_k(pend)
            pend = tok
        flush_k(pend)
        if g < 7:
            xn_ve_next = build_xn(g + 1, xld[(g + 1) % 2])
        cvs = []
        for nh in range(2):
            for m in range(4):
                vb = (pqB1, ov)[m % 2]
                for kk in range(8):
                    w = []
                    if kk == 0:
                        w = list(bank_vrd[id(vb)])
                        if g == 0 and m == 0 and nh == 0:
                            w.append((gd, wv_ld))
                    p.emit("tensor", lambda e, m=m, nh=nh, kk=kk, vb=vb, xs=g % 2: e.matmul(
                        vb[:, :], xn_fm[:, xs, kk, m * 128:(m + 1) * 128],
                        wbuf[:, 8 + kk, nh * 512:(nh + 1) * 512],
                        start=(kk == 0), stop=(kk == 7)), sem=pe, waits=w)
                mmv = pe.n
                cv = p.emit("scalar", lambda e, m=m, vb=vb: e.copy(
                    vstage[:, m, :], vb[:, :]), sem=se,
                    waits=[(pe, mmv)] + ([(p.g2, vst_dma[0]), (p.g3, vst_dma[1])] if m == 0 else []))
                bank_vrd[id(vb)] = [(se, cv)]
                cvs.append(cv)
            for pq4 in range(4):
                pr = nh * 4 + pq4
                vst_dma[pq4 % 2] = p.dma(
                    "gpsimd",
                    io["vspill"][pr, :, g * 4:(g + 1) * 4, :],
                    vstage[:, :, pq4 * 128:(pq4 + 1) * 128].rearrange("q m d -> q m d"),
                    (p.g2, p.g3)[pq4 % 2], waits=[(se, cvs[-1])])
            cvs = []

    wq_ld = p.dma("gpsimd", wbuf[:, 0:8, :], io["Wq"].rearrange("(c p) o -> p c o", p=128), gd,
                  waits=[(pe, pe.n)])
    wo_ld = p.dma("gpsimd", wbuf[:, 8:16, :], io["Wo"].rearrange("(c p) o -> p c o", p=128), gd)
    wg1_ld = p.dma("gpsimd", wg1[:, :, :], io["Wg1"].rearrange("(c p) o -> p c o", p=128), gd)
    wg2_ld = p.dma("gpsimd", wg2[:, :], io["Wg2"][:, :], gd)
    rd_spill_done = rd.n

    kbuf_dma = [None, None, None]
    vbuf_dma = [0, 0, 0]
    kbuf_rd = [0, 0, 0]
    vbuf_rd = [0, 0, 0]
    ov_guard = [[(se, se.n)]]
    vspill_done = p.g2.n

    xldq = [None, None]
    xldq[0] = xtf_load(0)
    xn_ve_next = build_xn(0, xldq[0])
    for qt in range(8):
        span = (qt + 1) * 512
        cslot = qt % 2
        p.dma("sync", cosg[:, cslot, 0, :], io["cosd"][:, qt * 512:(qt + 1) * 512], p.s2,
              waits=[(ve, ve.n)])
        cld = p.dma("sync", cosg[:, cslot, 1, :], io["sind"][:, qt * 512:(qt + 1) * 512], p.s2)
        xn_ve = xn_ve_next
        if qt < 7:
            xldq[(qt + 1) % 2] = xtf_load(qt + 1)
        rope_ve_g[0] = rope_ve_g[1] = ve.n
        rope_se_g[0] = rope_se_g[1] = se.n
        pend = None
        for c in range(8):
            fw = [(ve, xn_ve), (p.s2, cld)]
            if qt == 0 and c == 0:
                fw.append((gd, wq_ld))
            tok = proj_rope_mm(c, cslot, qt % 2, q_fm[:, c, :], first_wait=fw)
            if pend is not None:
                proj_rope_tail(pend, cld)
            pend = tok
        proj_rope_tail(pend, cld)
        q_done_ve = ve.n
        if qt < 7:
            xn_ve_next = build_xn(qt + 1, xldq[(qt + 1) % 2])
        for kk in range(8):
            w = [(ve, q_done_ve), (se, se.n)] if kk == 0 else []
            if kk == 0 and qt == 0:
                w.append((gd, wg1_ld))
            p.emit("tensor", lambda e, kk=kk, xs=qt % 2: e.matmul(
                m1[0:64, :], wg1[:, kk, :], xn_fm[:, xs, kk, :], start=(kk == 0), stop=(kk == 7)),
                sem=pe, waits=w)
        g1mm = pe.n
        g1v = p.emit("vector", lambda e: e.tensor_copy(g1sb[:, :], m1[0:64, :]), sem=ve, waits=[(pe, g1mm)])
        gate_se = [0] * 8
        m1_guard = [[(ve, g1v)]]

        def emit_gate(c, qt=qt, g1v=g1v):
            w = [(ve, g1v)] + m1_guard[0]
            if qt == 0 and c == 0:
                w.append((gd, wg2_ld))
            gm = p.emit("tensor", lambda e, c=c: e.matmul(
                m1[:, :], wg2[:, c * 128:(c + 1) * 128], g1sb[:, :], start=True, stop=True),
                sem=pe, waits=w)
            sg = p.emit("scalar", lambda e, c=c: e.activation(gate[:, c, :], m1[:, :], AF.Sigmoid),
                        sem=se, waits=[(pe, gm)])
            gate_se[c] = sg
            m1_guard[0] = [(se, sg)]

        for sl in range(2):
            kbuf_dma[sl] = p.dma("sync", kbuf[:, sl, 0:span], kspill[sl, :, 0:span], p.r2,
                                 waits=[(rd, rd_spill_done), (pe, kbuf_rd[sl])])
            vbuf_dma[sl] = p.dma("sync", vbuf[:, sl, 0:span // 128, :],
                                 io["vspill"][sl, :, 0:span // 128, :],
                                 (p.g2, p.g3)[sl % 2], waits=[(p.g2, vspill_done), (pe, vbuf_rd[sl])])

        esb_last = [[(pe, pe.n)]] * 6
        bank_rd = {}
        for b in (pqA0, pqA1, pqA2, pqB0, pqB1, pqB2):
            bank_rd[id(b)] = [(ve, ve.n), (se, se.n)]
        njt = (qt + 1) * 4
        gvs = []
        for pr in range(8):
            ksl = pr % 3
            emit_gate(pr)
            nxt = pr + 2
            if nxt < 8:
                dst = nxt % 3
                kbuf_dma[dst] = p.dma("sync", kbuf[:, dst, 0:span], kspill[nxt, :, 0:span],
                                      p.r2, waits=[(rd, rd_spill_done), (pe, kbuf_rd[dst])])
                vbuf_dma[dst] = p.dma("sync", vbuf[:, dst, 0:span // 128, :],
                                      io["vspill"][nxt, :, 0:span // 128, :],
                                      (p.g2, p.g3)[dst % 2],
                                      waits=[(p.g2, vspill_done), (pe, vbuf_rd[dst])])
            ev_q = []
            for j in range(njt + 1):
                if j < njt:
                    kj0 = j * 128
                    off = max(0, kj0 - qt * 512)
                    diag = kj0 >= qt * 512
                    r3 = j % 3
                    pa = (pqA0, pqA1, pqA2)[r3]
                    pb = (pqB0, pqB1, pqB2)[r3]
                    ea = 2 * r3
                    eb = ea + 1
                    wA = ([(p.r2, kbuf_dma[ksl])] if j == 0 else []) + bank_rd[id(pa)]
                    if j == 0:
                        wA.append((ve, q_done_ve))
                    mA = p.emit("tensor", lambda e, pa=pa, off=off, j=j, ks=ksl, pr=pr: e.matmul(
                        pa[:, off:512], kbuf[0:64, ks, j * 128:(j + 1) * 128], q_fm[0:64, pr, off:512],
                        start=True, stop=True, tile_position=(0, 0)), sem=pe, waits=wA)
                    mB = p.emit("tensor", lambda e, pb=pb, off=off, j=j, ks=ksl, pr=pr: e.matmul(
                        pb[:, off:512], kbuf[64:128, ks, j * 128:(j + 1) * 128], q_fm[64:128, pr, off:512],
                        start=True, stop=True, tile_position=(64, 0)), sem=pe, waits=bank_rd[id(pb)])
                if ev_q:
                    (jd, offd, ead, ebd, evw) = ev_q.pop(0)
                    evA = p.emit("tensor", lambda e, j=jd, off=offd, ea=ead, ksl=ksl, ovb=ov: e.matmul(
                        ovb[0:64, off:512], vbuf[:, ksl, j, 0:64], esb[:, ea, off:512],
                        start=(j == 0), stop=False, tile_position=(0, 0)), sem=pe, waits=evw)
                    evB = p.emit("tensor", lambda e, j=jd, off=offd, eb=ebd, ksl=ksl, ovb=ov, njt=njt: e.matmul(
                        ovb[64:128, off:512], vbuf[:, ksl, j, 64:128], esb[:, eb, off:512],
                        start=(j == 0), stop=(j == njt - 1), tile_position=(0, 64)), sem=pe, waits=evw)
                    esb_last[ead] = [(pe, evA)]
                    esb_last[ebd] = [(pe, evB)]
                if j < njt:
                    if diag:
                        va = p.emit("vector", lambda e, pa=pa, off=off, ea=ea: e.tensor_tensor(
                            esb[:, ea, off:off + 128], pa[:, off:off + 128], sb["tri"][:, :], ALU.mult),
                            sem=ve, waits=[(pe, mA)] + esb_last[ea])
                        if off + 128 < 512:
                            va = p.emit("vector", lambda e, pa=pa, off=off, ea=ea: e.tensor_copy(
                                esb[:, ea, off + 128:512], pa[:, off + 128:512]), sem=ve)
                        vb2 = p.emit("vector", lambda e, pb=pb, off=off, eb=eb: e.tensor_tensor(
                            esb[:, eb, off:off + 128], pb[:, off:off + 128], sb["tri"][:, :], ALU.mult),
                            sem=ve, waits=[(pe, mB)] + esb_last[eb])
                        evw = [(ve, va), (ve, vb2)]
                        if off + 128 < 512:
                            sc2 = p.emit("scalar", lambda e, pb=pb, off=off, eb=eb: e.copy(
                                esb[:, eb, off + 128:512], pb[:, off + 128:512]), sem=se, waits=[(pe, mB)])
                            evw.append((se, sc2))
                        bank_rd[id(pa)] = [(ve, va)]
                        bank_rd[id(pb)] = [(ve, vb2)] + ([(se, sc2)] if off + 128 < 512 else [])
                    else:
                        va = p.emit("vector", lambda e, pa=pa, ea=ea: e.tensor_copy(
                            esb[:, ea, :], pa[:, :]), sem=ve, waits=[(pe, mA)] + esb_last[ea])
                        sc2 = p.emit("scalar", lambda e, pb=pb, eb=eb: e.copy(
                            esb[:, eb, :], pb[:, :]), sem=se, waits=[(pe, mB)] + esb_last[eb])
                        evw = [(ve, va), (se, sc2)]
                        bank_rd[id(pa)] = [(ve, va)]
                        bank_rd[id(pb)] = [(se, sc2)]
                    if j == 0:
                        evw = evw + ov_guard[0] + [((p.g2, p.g3)[ksl % 2], vbuf_dma[ksl])]
                    ev_q.append((j, off, ea, eb, evw))
            gv = p.emit("vector", lambda e, pr=pr: e.tensor_tensor(
                attn[:, pr, :], ov[:, :], gate[:, pr, :], ALU.mult), sem=ve,
                waits=[(pe, pe.n), (se, gate_se[pr])])
            gvs.append(gv)
            ov_guard[0] = [(ve, gv)]
            kbuf_rd[ksl] = pe.n
            vbuf_rd[ksl] = pe.n
        attn_done_ve = ve.n

        for c in range(8):
            sqv = p.emit("scalar", lambda e, c=c: e.activation(sq[:, :], attn[:, c, :], AF.Square),
                         sem=se, waits=[(ve, gvs[c])] + ([] if c == 0 else [(pe, pe.n)]))
            p.emit("tensor", lambda e, c=c: e.matmul(
                m1[0:1, :], sb["ones_col_bf"][:, :], attn[:, c, :], start=(c == 0), stop=(c == 7)),
                sem=pe, waits=([(ve, gvs[0]), (se, se.n)] if c == 0 else [(ve, gvs[c])]))
            p.emit("tensor", lambda e, c=c: e.matmul(
                pqA0[0:1, :], sb["ones_col_bf"][:, :], sq[:, :], start=(c == 0), stop=(c == 7)),
                sem=pe, waits=[(se, sqv)] + (bank_rd[id(pqA0)] if c == 0 else []))
        sums_pe = pe.n
        r1 = p.emit("vector", lambda e: e.tensor_scalar_mul(srow(0), m1[0:1, :], 1.0 / D),
                    sem=ve, waits=[(pe, sums_pe), (rd, rd.n)])
        r2 = p.emit("vector", lambda e: e.tensor_scalar_mul(srow(1), pqA0[0:1, :], 1.0 / D),
                    sem=ve, waits=[(ve, r1)])
        r3 = p.emit("vector", lambda e: e.tensor_tensor(srow(2), srow(0), srow(0), ALU.mult),
                    sem=ve, waits=[(ve, r2)])
        r4 = p.emit("vector", lambda e: e.tensor_sub(srow(1), srow(1), srow(2)),
                    sem=ve, waits=[(ve, r3)])
        r5 = p.emit("scalar", lambda e: e.activation(srow(1), srow(1), AF.Sqrt, bias=sb["fc"][0:1, 0:1]),
                    sem=se, waits=[(ve, r4)])
        r6 = p.emit("vector", lambda e: e.reciprocal(srow(2), srow(1)), sem=ve, waits=[(se, r5)])
        r8 = p.emit("vector", lambda e: e.scalar_tensor_tensor(
            srow(3), srow(0), -1.0, srow(2), ALU.mult, ALU.mult),
            sem=ve, waits=[(ve, r6)])
        bca = p.emit("tensor", lambda e: e.matmul(pqB0[:, :], sb["ones_row"][:, :], srow[:, 2:3, 0, :]
                                                  if False else srow(2),
                                                  start=True, stop=True), sem=pe,
                     waits=[(ve, r8)] + bank_rd[id(pqB0)])
        bcb = p.emit("tensor", lambda e: e.matmul(pqB1[:, :], sb["ones_row"][:, :], srow(3),
                                                  start=True, stop=True), sem=pe,
                     waits=[(ve, r8)] + bank_rd[id(pqB1)])
        for c in range(8):
            t1 = p.emit("vector", lambda e, c=c: e.tensor_tensor(lnsA[:, :], attn[:, c, :], pqB0[:, :], ALU.mult),
                        sem=ve, waits=([(pe, bca)] if c == 0 else []))
            t2 = p.emit("vector", lambda e, c=c: e.tensor_tensor(lnsB[:, :], lnsA[:, :], pqB1[:, :], ALU.add),
                        sem=ve, waits=([(pe, bcb)] if c == 0 else []) + [(ve, t1)])
            p.emit("vector", lambda e, c=c: e.tensor_scalar(
                xo[:, c, :], lnsB[:, :], sb["lnw"][:, c:c + 1], sb["lnb"][:, c:c + 1],
                ALU.mult, ALU.add), sem=ve, waits=[(ve, t2)])
        xo_done = ve.n

        for ms in range(4):
            slot = ms % 2
            xl = p.dma("sync", xsub[:, slot, :], xin[(qt * 4 + ms) * 128:(qt * 4 + ms + 1) * 128, :], sd,
                       waits=[(ve, ve.n), (se, se.n)])
            for nh in range(2):
                wop = (m1, pqA1)[nh]
                for kk in range(8):
                    w = []
                    if kk == 0:
                        w = [(ve, xo_done)] if ms == 0 and nh == 0 else [(ve, ve.n)]
                        if qt == 0 and ms == 0 and nh == 0:
                            w.append((gd, wo_ld))
                        if nh == 1:
                            w += bank_rd[id(pqA1)]
                            bank_rd[id(pqA1)] = []
                    p.emit("tensor", lambda e, ms=ms, nh=nh, kk=kk, wop=wop: e.matmul(
                        wop[:, :], xo[:, kk, ms * 128:(ms + 1) * 128],
                        wbuf[:, 8 + kk, nh * 512:(nh + 1) * 512],
                        start=(kk == 0), stop=(kk == 7)), sem=pe, waits=w)
                mmo = pe.n
                p.emit("vector", lambda e, slot=slot, nh=nh, wop=wop: e.tensor_tensor(
                    x1t[:, slot, nh * 512:(nh + 1) * 512], xsub[:, slot, nh * 512:(nh + 1) * 512],
                    wop[:, :], ALU.add), sem=ve, waits=[(pe, mmo), (sd, xl), (rd, rd.n)])
            p.dma("sync", resid[(qt * 4 + ms) * 128:(qt * 4 + ms + 1) * 128, :], x1t[:, slot, :], rd,
                  waits=[(ve, ve.n)])





def build_phase2(p, io, E):
    nc = p.nc
    resid = io["resid"]
    sb = io["sb"]
    sd, gd, rd, pe, ve, se, po = p.sd, p.gd, p.rd, p.pe, p.ve, p.se, p.po

    xn_fm = E(nc.sbuf_tensor("p2_xnfm", [128, 8, 1024], BF16))
    hbufA = E(nc.sbuf_tensor("p2_h", [128, 32, 1024], BF16))
    hbufB = E(nc.sbuf_tensor("p2_g1", [128, 32, 1024], BF16))
    w12b = E(nc.sbuf_tensor("p2_w12b", [128, 2, 16, 128], BF16))
    w1b = w12b
    hb = E(nc.sbuf_tensor("p2_hb", [128, 2, 32, 128], BF16))
    xrb = E(nc.sbuf_tensor("p2_xrb", [128, 2, 1024], F32))
    xn_tm = E(nc.sbuf_tensor("p2_xntm", [128, 2, 1024], BF16))
    tsc = E(nc.sbuf_tensor("p2_t", [128, 2, 512], F32))
    x2p = E(nc.sbuf_tensor("p2_x2p", [128, 8, 128], F32))
    s1s = E(nc.sbuf_tensor("p2_s1", [128, 32], F32))
    s2s = E(nc.sbuf_tensor("p2_s2", [128, 32], F32))
    hs1s = E(nc.sbuf_tensor("p2_hs1", [128, 32], F32))
    hs2s = E(nc.sbuf_tensor("p2_hs2", [128, 32], F32))
    s3s = E(nc.sbuf_tensor("p2_s3", [128, 8], F32))
    stsum = E(nc.sbuf_tensor("p2_stsum", [128, 8], F32))
    stsq = E(nc.sbuf_tensor("p2_stsq", [128, 8], F32))
    mu2 = E(nc.sbuf_tensor("p2_mu", [128, 8], F32))
    rstd2 = E(nc.sbuf_tensor("p2_rstd", [128, 8], F32))
    tmp2 = E(nc.sbuf_tensor("p2_tmp", [128, 8], F32))
    sqd = E(nc.sbuf_tensor("p2_sqd", [128, 1024], BF16))

    pa0 = E(nc.psum_tensor("p2_pa0", [128, 512], F32))
    pa1 = E(nc.psum_tensor("p2_pa1", [128, 512], F32))
    pb0 = E(nc.psum_tensor("p2_pb0", [128, 512], F32))
    pb1 = E(nc.psum_tensor("p2_pb1", [128, 512], F32))
    tpA = E(nc.psum_tensor("p2_tpA", [128, 1024], BF16))
    tpB = E(nc.psum_tensor("p2_tpB", [128, 1024], BF16))
    po0 = E(nc.psum_tensor("p2_po0", [128, 512], F32))
    po1 = E(nc.psum_tensor("p2_po1", [128, 512], F32))

    for name, dst in (("s1_d", s1s), ("s2_d", s2s), ("hs1_d", hs1s), ("hs2_d", hs2s), ("s3_d", s3s)):
        p.dma("sync", dst[:, :], io[name], sd)
    sc_ld = sd.n

    tp_last = [(ve, ve.n), (ve, ve.n)]
    xntm_rd = [0, 0]
    bank_guard = {id(pa0): [], id(pa1): [], id(pb0): [], id(pb1): [],
                  id(po0): [], id(po1): []}
    w3mm = {}
    w3tp = [0]
    tsc_rd = [0, 0]
    x2p_rd = [None] * 8
    xrb_rd = [[], []]
    st_done_box = [0]
    xn_done_box = [0]
    APOOL = (pa0, pa1, po0)
    BPOOL = (pb0, pb1, po1)
    GPOOL = (pa0, pa1, po0, pb0, pb1, po1)
    po_guard = {id(po0): [], id(po1): []}
    wslot_guard = {id(w1b): [None, None], id(hb): [None, None]}

    def stage_mm(src_fm, wtile, wP, m, n_kk, dst_cb, mplan, tile_first=()):
        slot = m % 2
        g = wslot_guard[id(wtile)][slot]
        wld = p.dma("gpsimd", wtile[:, slot, 0:n_kk, :],
                    wP[m].rearrange("kk q mc -> q kk mc"), gd,
                    waits=([(pe, g)] if g else []))
        for nh in range(2):
            pa = mplan[nh]
            for kk in range(n_kk):
                w = []
                if kk == 0:
                    w = [(gd, wld)] + bank_guard[id(pa)] + list(tile_first)
                p.emit("tensor", lambda e, m=m, nh=nh, kk=kk, pa=pa, slot=slot: e.matmul(
                    pa[:, :], wtile[:, slot, kk, :], src_fm[:, kk, nh * 512:(nh + 1) * 512],
                    start=(kk == 0), stop=(kk == n_kk - 1)), sem=pe, waits=w)
            bank_guard[id(pa)] = dst_cb(m, nh, pa)
        wslot_guard[id(wtile)][slot] = pe.n

    def stats_emit(tt):
        r0 = tt * 1024
        vst, sst = [], []
        for si in range(8):
            slot = si % 2
            ld = p.dma("sync", xrb[:, slot, :], resid[r0 + si * 128:r0 + (si + 1) * 128, :], sd,
                       waits=list(xrb_rd[slot]))
            v = p.emit("vector", lambda e, s=slot, si=si: e.reduce_sum(stsum[:, si:si + 1], xrb[:, s, :], AX.X),
                       sem=ve, waits=[(sd, ld)])
            sv = p.emit("scalar", lambda e, s=slot, si=si: e.activation(sqd[:], xrb[:, s, :], AF.Square,
                                                                        accum_out=stsq[:, si:si + 1]),
                        sem=se, waits=[(sd, ld)])
            xrb_rd[slot] = [(ve, v), (se, sv)]
            vst.append(v)
            sst.append(sv)
        v1 = p.emit("vector", lambda e: e.tensor_scalar_mul(mu2[:], stsum[:], 1.0 / D), sem=ve,
                    waits=[(ve, vst[-1])])
        v2 = p.emit("vector", lambda e: e.tensor_scalar_mul(tmp2[:], stsq[:], 1.0 / D), sem=ve,
                    waits=[(se, sst[-1]), (ve, v1)])
        v3 = p.emit("vector", lambda e: e.tensor_mul(rstd2[:], mu2[:], mu2[:]), sem=ve, waits=[(ve, v2)])
        v4 = p.emit("vector", lambda e: e.tensor_sub(tmp2[:], tmp2[:], rstd2[:]), sem=ve, waits=[(ve, v3)])
        sq1 = p.emit("scalar", lambda e: e.activation(tmp2[:], tmp2[:], AF.Sqrt, bias=sb["fc"][:, 0:1]),
                     sem=se, waits=[(ve, v4)])
        st_done_box[0] = p.emit("vector", lambda e: e.reciprocal(rstd2[:], tmp2[:]), sem=ve, waits=[(se, sq1)])

    def norm_gen(tt):
        r0 = tt * 1024
        st_done = st_done_box[0]
        for si in range(8):
            slot = si % 2
            ld = p.dma("sync", xrb[:, slot, :], resid[r0 + si * 128:r0 + (si + 1) * 128, :], sd,
                       waits=list(xrb_rd[slot]))
            nv = p.emit("vector", lambda e, s=slot, si=si: e.tensor_scalar(
                xn_tm[:, s, :], xrb[:, s, :], mu2[:, si:si + 1], rstd2[:, si:si + 1],
                ALU.subtract, ALU.mult), sem=ve,
                waits=[(sd, ld), (ve, st_done), (pe, xntm_rd[slot])])
            xrb_rd[slot] = [(ve, nv)]
            for c in range(8):
                bk = c % 2
                bank = (tpA, tpB)[bk]
                pv = p.emit("tensor", lambda e, s=slot, c=c, bank=bank: e.transpose(
                    bank[:, 0:128], xn_tm[:, s, c * 128:(c + 1) * 128], sb["id_bf"][:, :]),
                    sem=pe, waits=[tp_last[bk], (ve, nv)])
                lv = p.emit("vector", lambda e, c=c, si=si, bank=bank: e.tensor_copy(
                    xn_fm[:, c, si * 128:(si + 1) * 128], bank[:, 0:128]),
                    sem=ve, waits=[(pe, pv)])
                tp_last[bk] = (ve, lv)
            xntm_rd[slot] = pe.n
            yield
        xn_done_box[0] = ve.n

    stats_emit(0)
    for _ in norm_gen(0):
        pass

    for tt in range(4):
        r0 = tt * 1024
        xn_done = xn_done_box[0]
        def h_epilogue(m, nh, pa):
            return []
        for m in range(32):
            slot = m % 2
            g1g = wslot_guard[id(w1b)][slot]
            wld1 = p.dma("gpsimd", w12b[:, slot, :, :], io["w12P"][m].rearrange("kk q mc -> q kk mc"), gd,
                         waits=([(pe, g1g)] if g1g else []))
            wld2 = wld1
            qa0 = (2 * m) % 3
            qa1 = (2 * m + 1) % 3
            for kk in range(8):
                w = []
                if kk == 0:
                    w = [(gd, wld1)] + bank_guard[id(APOOL[qa0])] + bank_guard[id(APOOL[qa1])]
                    if m == 0:
                        w += [(ve, xn_done), (sd, sc_ld)]
                for nh in range(2):
                    p.emit("tensor", lambda e, m=m, nh=nh, kk=kk, slot=slot,
                           pa=APOOL[(2 * m + nh) % 3]: e.matmul(
                        pa[:, :], w1b[:, slot, kk, :], xn_fm[:, kk, nh * 512:(nh + 1) * 512],
                        start=(kk == 0), stop=(kk == 7)), sem=pe, waits=(w if nh == 0 else []))
            mma = pe.n
            for kk in range(8):
                w = []
                if kk == 0:
                    w = [(gd, wld2)] + bank_guard[id(BPOOL[qa0])] + bank_guard[id(BPOOL[qa1])]
                for nh in range(2):
                    p.emit("tensor", lambda e, m=m, nh=nh, kk=kk, slot=slot,
                           pb=BPOOL[(2 * m + nh) % 3]: e.matmul(
                        pb[:, :], w12b[:, slot, 8 + kk, :], xn_fm[:, kk, nh * 512:(nh + 1) * 512],
                        start=(kk == 0), stop=(kk == 7)), sem=pe, waits=(w if nh == 0 else []))
            mmb = pe.n
            for nh in range(2):
                pa = APOOL[(2 * m + nh) % 3]
                pb = BPOOL[(2 * m + nh) % 3]
                sv = p.emit("scalar", lambda e, m=m, pa=pa, ts=nh: e.activation(
                    tsc[:, ts, :], pa[:, :], AF.Silu, scale=s1s[:, m:m + 1]),
                    sem=se, waits=[(pe, mma), (ve, ve.n)] if nh == 0 else [(pe, mma)])
                vv = p.emit("vector", lambda e, m=m, nh=nh, pb=pb, ts=nh: e.scalar_tensor_tensor(
                    hbufA[:, m, nh * 512:(nh + 1) * 512], pb[:, :], s2s[:, m:m + 1], tsc[:, ts, :],
                    ALU.mult, ALU.mult), sem=ve, waits=[(pe, mmb), (se, sv)])
                bank_guard[id(pa)] = [(se, sv)]
                bank_guard[id(pb)] = [(ve, vv)]
            wslot_guard[id(w1b)][slot] = pe.n
        h_done = ve.n
        if tt < 3:
            stats_emit(tt + 1)

        def big_stage(srcbuf, dstbuf, wPname, scl, first_ve, gen=None):
            for m in range(32):
                if gen is not None:
                    next(gen, None)
                slot = m % 2
                g = wslot_guard[id(hb)][slot]
                wld = p.dma("gpsimd", hb[:, slot, :, :], io[wPname][m].rearrange("kk q mc -> q kk mc"), gd,
                            waits=([(pe, g)] if g else []))
                for kk in range(32):
                    w = []
                    if kk == 0:
                        w = [(gd, wld)] + bank_guard[id(GPOOL[(2 * m) % 6])] + bank_guard[id(GPOOL[(2 * m + 1) % 6])]
                        if m == 0:
                            w += [(ve, first_ve), (se, se.n), (ve, ve.n)]
                    for nh in range(2):
                        p.emit("tensor", lambda e, m=m, nh=nh, kk=kk, slot=slot, srcbuf=srcbuf,
                               pa=GPOOL[(2 * m + nh) % 6]: e.matmul(
                            pa[:, :], hb[:, slot, kk, :], srcbuf[:, kk, nh * 512:(nh + 1) * 512],
                            start=(kk == 0), stop=(kk == 31)), sem=pe, waits=(w if nh == 0 else []))
                mm = pe.n
                for nh in range(2):
                    pa = GPOOL[(2 * m + nh) % 6]
                    sv = p.emit("scalar", lambda e, m=m, nh=nh, pa=pa, dstbuf=dstbuf, scl=scl: e.activation(
                        dstbuf[:, m, nh * 512:(nh + 1) * 512], pa[:, :], AF.Gelu, scale=scl[:, m:m + 1]),
                        sem=se, waits=[(pe, mm)])
                    bank_guard[id(pa)] = [(se, sv)]
                wslot_guard[id(hb)][slot] = pe.n
            return se.n

        g1_done = big_stage(hbufA, hbufB, "h1P", hs1s, h_done)
        gen = norm_gen(tt + 1) if tt < 3 else None
        g2_done = big_stage(hbufB, hbufA, "h2P", hs2s, ve.n, gen=gen)
        if gen is not None:
            for _ in gen:
                pass

        w3q = []

        def flush_w3(item):
            (m_, pa_, pb_) = item
            for nh in range(2):
                pab = (pa_, pb_)[nh]
                ts = nh
                sv = p.emit("scalar", lambda e, m=m_, pab=pab, ts=ts: e.activation(
                    tsc[:, ts, :], pab[:, :], AF.Copy, scale=s3s[:, m:m + 1]),
                    sem=se, waits=[(pe, w3mm[id(pab)]), (pe, tsc_rd[ts])])
                bank_guard[id(pab)] = [(se, sv)]
                for wq_ in range(4):
                    si = nh * 4 + wq_
                    pob = (po0, po1)[w3tp[0] % 2]
                    w3tp[0] += 1
                    tv2 = p.emit("tensor", lambda e, ts=ts, wq_=wq_, pob=pob: e.transpose(
                        pob[:, 0:128], tsc[:, ts, wq_ * 128:(wq_ + 1) * 128], sb["id_f32"][:, :]),
                        sem=pe, waits=[(se, sv)] + bank_guard[id(pob)])
                    prev = x2p_rd[si]
                    cp = p.emit("vector", lambda e, wq_=wq_, pob=pob, xs=si: e.tensor_copy(
                        x2p[:, xs, :], pob[:, 0:128]), sem=ve,
                        waits=[(pe, tv2)] + ([prev] if prev else []))
                    bank_guard[id(pob)] = [(ve, cp)]
                    qsem = (rd, p.od)[si % 2]
                    x2p_rd[si] = (qsem, p.emit(
                        "gpsimd", lambda e, si=si, m=m_, xs=si: e.dma_start(
                            out=resid[r0 + si * 128:r0 + (si + 1) * 128, m * 128:(m + 1) * 128],
                            in_=x2p[:, xs, :], accum_op=ALU.add), sem=qsem, inc=16,
                        waits=[(ve, cp), (qsem, qsem.n)]))
                tsc_rd[ts] = pe.n

        for m in range(8):
            slot = m % 2
            g = wslot_guard[id(hb)][slot]
            wld = p.dma("gpsimd", hb[:, slot, :, :], io["w3P"][m].rearrange("kk q mc -> q kk mc"), gd,
                        waits=([(pe, g)] if g else []))
            pa_ = (pa0, pb0)[m % 2]
            pb_ = (pa1, pb1)[m % 2]
            for kk in range(32):
                w = []
                if kk == 0:
                    w = [(gd, wld)] + bank_guard[id(pa_)] + bank_guard[id(pb_)]
                    if m == 0:
                        w += [(se, g2_done), (ve, ve.n)]
                for nh in range(2):
                    p.emit("tensor", lambda e, m=m, nh=nh, kk=kk, slot=slot,
                           pab=(pa_, pb_)[nh]: e.matmul(
                        pab[:, :], hb[:, slot, kk, :], hbufA[:, kk, nh * 512:(nh + 1) * 512],
                        start=(kk == 0), stop=(kk == 31)), sem=pe, waits=(w if nh == 0 else []))
            w3mm[id(pa_)] = pe.n
            w3mm[id(pb_)] = pe.n
            wslot_guard[id(hb)][slot] = pe.n
            w3q.append((m, pa_, pb_))
            if len(w3q) > 1:
                flush_w3(w3q.pop(0))
        while w3q:
            flush_w3(w3q.pop(0))



def build_phase4a(p, io, E):
    nc = p.nc
    l1in = io["l1in"]
    sb = io["sb"]
    sd, gd, rd, pe, ve, se, po = p.sd, p.gd, p.rd, p.pe, p.ve, p.se, p.po

    wbuf = E(nc.sbuf_tensor("p4_wbuf", [128, 3, 8, 1024], BF16))
    xrb = E(nc.sbuf_tensor("p4_xrb", [128, 2, 1024], F32))
    xn_tm = E(nc.sbuf_tensor("p4_xntm", [128, 2, 1024], BF16))
    xn_fm = E(nc.sbuf_tensor("p4_xnfm", [128, 8, 512], BF16))
    sqf = E(nc.sbuf_tensor("p4_sqf", [128, 2, 512], F32))
    sdt = E(nc.sbuf_tensor("p4_sd", [128, 2, 512], F32))
    khc = E(nc.sbuf_tensor("p4_khc", [128, 2, 512], F32))
    vqc = E(nc.sbuf_tensor("p4_vqc", [128, 2, 512], BF16))
    posblk = E(nc.sbuf_tensor("p4_posblk", [128, 128], F32))
    stsum = E(nc.sbuf_tensor("p4_stsum", [128, 4], F32))
    stsq = E(nc.sbuf_tensor("p4_stsq", [128, 4], F32))
    mu4 = E(nc.sbuf_tensor("p4_mu", [128, 4], F32))
    rstd4 = E(nc.sbuf_tensor("p4_rstd", [128, 4], F32))
    tmp4 = E(nc.sbuf_tensor("p4_tmp", [128, 4], F32))
    sqd = E(nc.sbuf_tensor("p4_sqd", [128, 1024], BF16))

    fuse5 = io.get("fuse5", False)
    if fuse5:
        wstage = E(nc.sbuf_tensor("p45_wst", [128, 2, 1024], F32))
        wq8t = E(nc.sbuf_tensor("p45_wq8t", [128, 2, 1024], FP8))
        wam5 = E(nc.sbuf_tensor("p45_wam", [128, 9], F32))
        wsw5 = E(nc.sbuf_tensor("p45_wsw", [128, 1], F32))
        scal5 = E(nc.sbuf_tensor("p45_scal", [1, 4], F32))
        pw = E(nc.psum_tensor("p45_pw", [128, 512], F32))
        wfr = io["wfT"].rearrange("(c q) o -> q c o", q=128)
        wld5 = [None, None]
        wam_v = [0] * 8
        qv_g = [[], []]
        b4box = [0]

    def quant_pair(base):
        for sl in range(2):
            c = base + sl
            qv = p.emit("vector", lambda e, c=c, sl=sl: e.tensor_scalar_mul(
                wq8t[:, sl, :], wstage[:, sl, :], wsw5[:, 0:1]), sem=ve,
                waits=[(p.od, wld5[sl]), (ve, b4box[0])] + qv_g[sl])
            qd = p.dma("gpsimd", io["wq8d"][:, c, :], wq8t[:, sl, :], gd, waits=[(ve, qv)])
            qv_g[sl] = [(gd, qd)]
            nxt = base + 2 + sl
            if nxt < 8:
                wld5[sl] = p.dma("sync", wstage[:, sl, :], wfr[:, nxt, :], p.od,
                                 waits=[(ve, qv)])

    def w_hook(t):
        if t == 0:
            for sl in range(2):
                wld5[sl] = p.dma("sync", wstage[:, sl, :], wfr[:, sl, :], p.od)
        elif t <= 3:
            for sl in range(2):
                c = (t - 1) * 2 + sl
                wam_v[c] = p.emit("vector", lambda e, c=c, sl=sl: e.reduce_max(
                    wam5[:, c:c + 1], wstage[:, sl, :], AX.X, apply_absolute_value=True),
                    sem=ve, waits=[(p.od, wld5[sl])])
                nxt = t * 2 + sl
                if nxt < 8:
                    wld5[sl] = p.dma("sync", wstage[:, sl, :], wfr[:, nxt, :], p.od,
                                     waits=[(ve, wam_v[c])])
        elif t == 4:
            for sl in range(2):
                c = 6 + sl
                wam_v[c] = p.emit("vector", lambda e, c=c, sl=sl: e.reduce_max(
                    wam5[:, c:c + 1], wstage[:, sl, :], AX.X, apply_absolute_value=True),
                    sem=ve, waits=[(p.od, wld5[sl])])
            wm1 = p.emit("vector", lambda e: e.reduce_max(wam5[:, 8:9], wam5[:, 0:8], AX.X),
                         sem=ve, waits=[(ve, wam_v[7])])
            wtp = p.emit("tensor", lambda e: e.transpose(pw[0:1, 0:128], wam5[:, 8:9],
                                                         sb["id_f32"][:, :]),
                         sem=pe, waits=[(ve, wm1)])
            wm2 = p.emit("vector", lambda e: e.reduce_max(scal5[0:1, 2:3], pw[0:1, 0:128], AX.X),
                         sem=ve, waits=[(pe, wtp)])
            r1 = p.emit("vector", lambda e: e.reciprocal(scal5[0:1, 3:4], scal5[0:1, 2:3]), sem=ve,
                        waits=[(ve, wm2)])
            r2 = p.emit("vector", lambda e: e.tensor_scalar_mul(scal5[0:1, 0:1], scal5[0:1, 3:4], 224.0),
                        sem=ve, waits=[(ve, r1)])
            b3 = p.emit("tensor", lambda e: e.matmul(pw[:, 0:1], sb["ones_row"][:, :],
                                                     scal5[0:1, 0:1], start=True, stop=True),
                        sem=pe, waits=[(ve, r2)])
            b4 = p.emit("vector", lambda e: e.tensor_copy(wsw5[:, 0:1], pw[:, 0:1]), sem=ve,
                        waits=[(pe, b3)])
            b4box[0] = b4
            p.dma("gpsimd", io["swd"][0:1, 0:1], scal5[0:1, 0:1], gd, waits=[(ve, r2)])
            for sl in range(2):
                wld5[sl] = p.dma("sync", wstage[:, sl, :], wfr[:, sl, :], p.od,
                                 waits=[(ve, wam_v[6 + sl])])
        elif t <= 7:
            quant_pair((t - 5) * 2)

    pk0 = E(nc.psum_tensor("p4_pk0", [128, 512], F32))
    pk1 = E(nc.psum_tensor("p4_pk1", [128, 512], F32))
    pn0 = E(nc.psum_tensor("p4_pn0", [128, 512], F32))
    pn1 = E(nc.psum_tensor("p4_pn1", [128, 512], F32))
    tpA = E(nc.psum_tensor("p4_tpA", [128, 1024], BF16))
    tpB = E(nc.psum_tensor("p4_tpB", [128, 1024], BF16))

    p.dma("sync", posblk[:, :], io["posblk_d"], sd)
    wld = p.dma("gpsimd", wbuf[:, 0, :, :], io["wkT"].rearrange("(c q) o -> q c o", q=128), gd)
    wld = p.dma("gpsimd", wbuf[:, 1, :, :], io["wvT"].rearrange("(c q) o -> q c o", q=128), gd)
    wld = p.dma("gpsimd", wbuf[:, 2, :, :], io["wqT"].rearrange("(c q) o -> q c o", q=128), gd)

    tp_last = [(ve, ve.n), (ve, ve.n)]
    pk_g = {id(pk0): [], id(pk1): []}
    pn_g = {id(pn0): [], id(pn1): []}
    sq_g = [[], []]
    sd_g = [[], []]
    kh_g = [[], []]
    vq_g = [[], []]

    for t in range(8):
        r0 = t * 512
        if fuse5:
            w_hook(t)
        vst, sst = [], []
        for si in range(4):
            slot = si % 2
            w = [(ve, vst[si - 2]), (se, sst[si - 2])] if si >= 2 else [(ve, ve.n), (se, se.n)]
            ld = p.dma("sync", xrb[:, slot, :], l1in[r0 + si * 128:r0 + (si + 1) * 128, :], sd, waits=w)
            v = p.emit("vector", lambda e, s=slot, si=si: e.reduce_sum(stsum[:, si:si + 1], xrb[:, s, :], AX.X),
                       sem=ve, waits=[(sd, ld)])
            sv = p.emit("scalar", lambda e, s=slot, si=si: e.activation(sqd[:], xrb[:, s, :], AF.Square,
                                                                        accum_out=stsq[:, si:si + 1]),
                        sem=se, waits=[(sd, ld)])
            vst.append(v)
            sst.append(sv)
        v1 = p.emit("vector", lambda e: e.tensor_scalar_mul(mu4[:], stsum[:], 1.0 / D), sem=ve,
                    waits=[(ve, vst[-1])])
        v2 = p.emit("vector", lambda e: e.tensor_scalar_mul(tmp4[:], stsq[:], 1.0 / D), sem=ve,
                    waits=[(se, sst[-1]), (ve, v1)])
        v3 = p.emit("vector", lambda e: e.tensor_mul(rstd4[:], mu4[:], mu4[:]), sem=ve, waits=[(ve, v2)])
        v4 = p.emit("vector", lambda e: e.tensor_sub(tmp4[:], tmp4[:], rstd4[:]), sem=ve, waits=[(ve, v3)])
        sq1 = p.emit("scalar", lambda e: e.activation(tmp4[:], tmp4[:], AF.Sqrt, bias=sb["fc"][:, 0:1]),
                     sem=se, waits=[(ve, v4)])
        st_done = p.emit("vector", lambda e: e.reciprocal(rstd4[:], tmp4[:]), sem=ve, waits=[(se, sq1)])
        for si in range(4):
            slot = si % 2
            ld = p.dma("sync", xrb[:, slot, :], l1in[r0 + si * 128:r0 + (si + 1) * 128, :], sd,
                       waits=[(ve, ve.n), (se, se.n)])
            nv = p.emit("vector", lambda e, s=slot, si=si: e.tensor_scalar(
                xn_tm[:, s, :], xrb[:, s, :], mu4[:, si:si + 1], rstd4[:, si:si + 1],
                ALU.subtract, ALU.mult), sem=ve, waits=[(sd, ld), (ve, st_done), (pe, pe.n)])
            for c in range(8):
                bk = c % 2
                bank = (tpA, tpB)[bk]
                pv = p.emit("tensor", lambda e, s=slot, c=c, bank=bank: e.transpose(
                    bank[:, 0:128], xn_tm[:, s, c * 128:(c + 1) * 128], sb["id_bf"][:, :]),
                    sem=pe, waits=[tp_last[bk], (ve, nv)])
                lv = p.emit("vector", lambda e, c=c, si=si, bank=bank: e.tensor_copy(
                    xn_fm[:, c, si * 128:(si + 1) * 128], bank[:, 0:128]),
                    sem=ve, waits=[(pe, pv)])
                tp_last[bk] = (ve, lv)
        xn_done = ve.n
        for wi, spname in ((0, "khsp"), (1, "vsp"), (2, "qsp")):
            for c in range(8):
                pk = (pk0, pk1)[c % 2]
                for kk in range(8):
                    w = []
                    if kk == 0:
                        w = pk_g[id(pk)] + [(ve, xn_done)]
                        if t == 0 and wi == 0 and c == 0:
                            w += [(gd, wld), (sd, sd.n)]
                    p.emit("tensor", lambda e, wi=wi, c=c, kk=kk, pk=pk: e.matmul(
                        pk[:, :], wbuf[:, wi, kk, c * 128:(c + 1) * 128], xn_fm[:, kk, :],
                        start=(kk == 0), stop=(kk == 7)), sem=pe, waits=w)
                mmp = pe.n
                if wi == 0:
                    cs = c % 2
                    pn = (pn0, pn1)[cs]
                    sv = p.emit("scalar", lambda e, cs=cs, pk=pk: e.activation(
                        sqf[:, cs, :], pk[:, :], AF.Square), sem=se,
                        waits=[(pe, mmp)] + sq_g[cs])
                    nm = p.emit("tensor", lambda e, cs=cs, pn=pn: e.matmul(
                        pn[:, :], posblk[:, :], sqf[:, cs, :], start=True, stop=True),
                        sem=pe, waits=[(se, sv)] + pn_g[id(pn)])
                    sq_g[cs] = [(pe, nm)]
                    s2 = p.emit("scalar", lambda e, cs=cs, pn=pn: e.activation(
                        sdt[:, cs, :], pn[:, :], AF.Sqrt), sem=se,
                        waits=[(pe, nm)] + sd_g[cs])
                    a1 = p.emit("vector", lambda e, cs=cs: e.tensor_scalar_add(
                        sdt[:, cs, :], sdt[:, cs, :], 1e-6), sem=ve, waits=[(se, s2)])
                    a2 = p.emit("vector", lambda e, cs=cs: e.reciprocal(sdt[:, cs, :], sdt[:, cs, :]),
                                sem=ve, waits=[(ve, a1)])
                    pn_g[id(pn)] = [(ve, a2)]
                    kv = p.emit("vector", lambda e, cs=cs, pk=pk: e.tensor_tensor(
                        khc[:, cs, :], pk[:, :], sdt[:, cs, :], ALU.mult), sem=ve,
                        waits=[(ve, a2)] + kh_g[cs])
                    sd_g[cs] = [(ve, kv)]
                    pk_g[id(pk)] = [(ve, kv)]
                    dm = p.dma("sync", io["khsp"][t, c, :, :], khc[:, cs, :], rd, waits=[(ve, kv)])
                    kh_g[cs] = [(rd, dm)]
                else:
                    cs = c % 2
                    cv = p.emit("scalar", lambda e, cs=cs, pk=pk: e.copy(vqc[:, cs, :], pk[:, :]),
                                sem=se, waits=[(pe, mmp)] + vq_g[cs])
                    pk_g[id(pk)] = [(se, cv)]
                    dm = p.dma("sync", io[spname][t, c, :, :], vqc[:, cs, :], p.r2, waits=[(se, cv)])
                    vq_g[cs] = [(p.r2, dm)]
    if fuse5:
        quant_pair(6)


def build_phase4b(p, io, E):
    nc = p.nc
    l1in, x3buf = io["l1in"], io["x3buf"]
    sb = io["sb"]
    sd, gd, rd, pe, ve, se, po = p.sd, p.gd, p.rd, p.pe, p.ve, p.se, p.po
    import os as _os
    _ser = _os.environ.get("SER4B", "")
    if _ser:
        _emit0 = p.emit
        def _emit(eng, fn, sem=None, inc=1, waits=()):
            hit = "a" in _ser or eng[0] in _ser
            if "d" in _ser and eng == "gpsimd" and inc == 16:
                hit = True
            if "P" in _ser and eng == "gpsimd" and inc == 1:
                hit = True
            if hit:
                waits = list(waits) + [(pe, pe.n), (ve, ve.n), (se, se.n), (po, po.n),
                                       (rd, rd.n), (gd, gd.n), (sd, sd.n)]
            return _emit0(eng, fn, sem=sem, inc=inc, waits=waits)
        p.emit = _emit

    guard0 = E(nc.sbuf_tensor("p4_guard0", [128, 512], F32))
    S = E(nc.sbuf_tensor("p4_S", [128, 64, 512], F32))
    selv = E(nc.sbuf_tensor("p4_selv", [128, 8, 128], BF16))
    selq = E(nc.sbuf_tensor("p4_selq", [128, 8, 128], BF16))
    negb = E(nc.sbuf_tensor("p4_negb", [128, 128], BF16))
    kh = E(nc.sbuf_tensor("p4b_kh", [128, 3, 512], F32))
    vb = E(nc.sbuf_tensor("p4b_vb", [128, 3, 512], BF16))
    qb = E(nc.sbuf_tensor("p4b_qb", [128, 3, 512], BF16))
    Pb = E(nc.sbuf_tensor("p4b_P", [128, 4, 512], BF16))
    Qb = E(nc.sbuf_tensor("p4b_Q", [128, 4, 512], BF16))
    tf = E(nc.sbuf_tensor("p4b_tf", [128, 2, 512], F32))
    sf = E(nc.sbuf_tensor("p4b_sf", [128, 2, 512], F32))
    of = E(nc.sbuf_tensor("p4b_of", [128, 2, 512], F32))
    of2 = E(nc.sbuf_tensor("p4b_of2", [128, 2, 512], F32))
    xrb = E(nc.sbuf_tensor("p4b_xrb", [128, 4, 1024], F32))
    x3t = E(nc.sbuf_tensor("p4b_x3t", [128, 4, 1024], F32))

    fuse5 = io.get("fuse5", False)
    if fuse5:
        stS = E(nc.sbuf_tensor("p45_stS", [128, 32], F32))
        stQa = E(nc.sbuf_tensor("p45_stQa", [128, 32], F32))
        stQb = E(nc.sbuf_tensor("p45_stQb", [128, 32], F32))
        stMX = E(nc.sbuf_tensor("p45_stMX", [128, 32], F32))
        stMN = E(nc.sbuf_tensor("p45_stMN", [128, 32], F32))
        mu5 = E(nc.sbuf_tensor("p45_mu", [128, 32], F32))
        rstd5 = E(nc.sbuf_tensor("p45_rstd", [128, 32], F32))
        tmpc5 = E(nc.sbuf_tensor("p45_tmp", [128, 32], F32))
        fm5 = E(nc.sbuf_tensor("p45_fm", [128, 32], F32))
        amc5 = E(nc.sbuf_tensor("p45_amc", [128, 1], F32))
        scal5b = E(nc.sbuf_tensor("p45_scalb", [1, 2], F32))
    st_g = [[], [], [], []]
    avl = [0, 0, 0, 0]

    pd0 = E(nc.psum_tensor("p4b_pd0", [128, 512], F32))
    pd1 = E(nc.psum_tensor("p4b_pd1", [128, 512], F32))
    pd2 = E(nc.psum_tensor("p4b_pd2", [128, 512], F32))
    pd3 = E(nc.psum_tensor("p4b_pd3", [128, 512], F32))
    pq0 = E(nc.psum_tensor("p4b_pq0", [128, 512], F32))
    pq1 = E(nc.psum_tensor("p4b_pq1", [128, 512], F32))
    po0 = E(nc.psum_tensor("p4b_po0", [128, 512], F32))
    po1 = E(nc.psum_tensor("p4b_po1", [128, 512], F32))

    p.dma("sync", selv[:, :, :], io["selv_d"].rearrange("i q m -> q i m"), sd)
    p.dma("sync", selq[:, :, :], io["selq_d"].rearrange("i q m -> q i m"), sd)
    sel_ld = p.dma("sync", negb[:, :], io["negblk_d"], sd)

    mz = 0
    for z in range(16):
        mz = p.emit("vector", lambda e, z=z: e.memset(S[:, z * 4:(z + 1) * 4, :], 0.0), sem=ve)

    tanh_at = [0] * 64
    Sread_po = [(po, 0)] * 64
    Sread_ve = [(ve, mz)] * 64
    Pb_rd = [0] * 4
    Qb_rd = [0] * 4
    tf_rd = [0] * 2
    sf_rd = [0] * 2
    pd_rd = [0, 0, 0, 0]
    pq_rd = [[], []]
    po_g = [[], []]
    x3_dma = [0, 0, 0, 0]
    of_rd = [0, 0]
    of2_rd = [0, 0]
    kld_prev = [0, 0]
    kh_rd = [[], [], []]
    vb_rd = [[], [], []]
    qb_rd = [[], [], []]

    for t in range(8):
        r0 = t * 512
        xld = []
        for ls in range(4):
            xld.append(p.dma("sync", xrb[:, ls, :], l1in[r0 + ls * 128:r0 + (ls + 1) * 128, :], sd,
                             waits=po_g[0] + po_g[1]))
        kld = [None, None, None]
        vld = [None, None, None]
        qld = [None, None, None]
        out_q = []

        def flush_out(item):
            (cc_, ovv_) = item
            last_tp = 0
            for ls in range(4):
                pob = (po0, po1)[ls % 2]
                tv2 = p.emit("tensor", lambda e, ls=ls, pob=pob, oc=cc_ % 2: e.transpose(
                    pob[:, 0:128], of2[:, oc, ls * 128:(ls + 1) * 128], sb["id_f32"][:, :]),
                    sem=pe, waits=[(ve, ovv_)] + po_g[ls % 2])
                last_tp = tv2
                av = p.emit("vector", lambda e, ls=ls, c=cc_, pob=pob: e.tensor_tensor(
                    x3t[:, ls, c * 128:(c + 1) * 128], xrb[:, ls, c * 128:(c + 1) * 128],
                    pob[:, 0:128], ALU.add), sem=ve,
                    waits=[(pe, tv2), (sd, xld[ls]), (rd, x3_dma[ls])] + st_g[ls])
                avl[ls] = av
                po_g[ls % 2] = [(ve, av)]
            of2_rd[cc_ % 2] = last_tp

        for c in range(8):
            cs = (t * 8 + c) % 3
            kld[cs] = p.dma("sync", kh[:, cs, :], io["khsp"][t, c, :, :], p.r2,
                            waits=kh_rd[cs])
            vld[cs] = p.dma("gpsimd", vb[:, cs, :], io["vsp"][t, c, :, :], gd,
                            waits=vb_rd[cs])
            qld[cs] = p.dma("gpsimd", qb[:, cs, :], io["qsp"][t, c, :, :], p.g2,
                            waits=qb_rd[cs])
            pqb = (pq0, pq1)[c % 2]
            qv_q = []
            mq_q = []
            for i in range(10):
                if i < 8:
                    ci = c * 8 + i
                    ps = i % 4
                    pdb = (pd0, pd1)[i % 2]
                    pv = p.emit("gpsimd", lambda e, ci=ci, cs=cs, ps=ps: e.tensor_tensor(
                        Pb[:, ps, :], S[:, ci, :], kh[:, cs, :], ALU.mult), sem=po,
                        waits=[(p.r2, kld[cs]), (se, tanh_at[ci]), (pe, Pb_rd[ps])]
                        + ([(ve, mz)] if t == 0 else []))
                    Sread_po[ci] = (po, pv)
                    m1v = p.emit("tensor", lambda e, i=i, cs=cs, pdb=pdb: e.matmul(
                        pdb[:, :], selv[:, i, :], vb[:, cs, :], start=True, stop=False),
                        sem=pe, waits=[(gd, vld[cs]), (po, pv), (sd, sel_ld), (ve, pd_rd[i % 2])])
                    m2v = p.emit("tensor", lambda e, i=i, ps=ps, pdb=pdb: e.matmul(
                        pdb[:, :], negb[:, :], Pb[:, ps, :], start=False, stop=True), sem=pe)
                    Pb_rd[ps] = m2v
                    tv = p.emit("vector", lambda e, cs=cs, pdb=pdb, ts=i % 2: e.tensor_tensor(
                        tf[:, ts, :], pdb[:, :], kh[:, cs, :], ALU.mult), sem=ve,
                        waits=[(pe, m2v), (ve, tf_rd[i % 2])])
                    pd_rd[i % 2] = tv
                    sv = p.emit("vector", lambda e, ci=ci, ts=i % 2: e.tensor_add(
                        sf[:, ts, :], S[:, ci, :], tf[:, ts, :]), sem=ve,
                        waits=[(ve, tv), (se, tanh_at[ci]), (se, sf_rd[i % 2])])
                    tf_rd[i % 2] = sv
                    th = p.emit("scalar", lambda e, ci=ci, ts=i % 2: e.activation(
                        S[:, ci, :], sf[:, ts, :], AF.Tanh), sem=se,
                        waits=[(ve, sv)] + [Sread_po[ci]])
                    sf_rd[i % 2] = th
                    tanh_at[ci] = th
                    qv_q.append((i, ci, th))
                if len(qv_q) > (1 if i < 8 else 0):
                    (iq, ciq, thq) = qv_q.pop(0)
                    psq = iq % 4
                    qv = p.emit("vector", lambda e, ci=ciq, cs=cs, ps=psq: e.tensor_tensor(
                        Qb[:, ps, :], S[:, ci, :], qb[:, cs, :], ALU.mult), sem=ve,
                        waits=[(se, thq), (p.g2, qld[cs]), (pe, Qb_rd[psq])])
                    mq_q.append((iq, psq, qv))
                if len(mq_q) > (1 if i < 9 else 0):
                    (im, psm, qvm) = mq_q.pop(0)
                    mq = p.emit("tensor", lambda e, i=im, ps=psm, pqb=pqb: e.matmul(
                        pqb[:, :], selq[:, i, :], Qb[:, ps, :], start=(i == 0), stop=(i == 7)),
                        sem=pe, waits=[(ve, qvm)] + (pq_rd[c % 2] if im == 0 else []))
                    Qb_rd[psm] = mq
            mmq = pe.n
            ssl = p.emit("scalar", lambda e, pqb=pqb, oc=c % 2: e.activation(
                of[:, oc, :], pqb[:, :], AF.Silu),
                sem=se, waits=[(pe, mmq), (ve, of_rd[c % 2])])
            ovv = p.emit("vector", lambda e, pqb=pqb, oc=c % 2: e.tensor_tensor(
                of2[:, oc, :], of[:, oc, :], pqb[:, :], ALU.mult), sem=ve,
                waits=[(se, ssl), (pe, of2_rd[c % 2])])
            of_rd[c % 2] = ovv
            pq_rd[c % 2] = [(ve, ovv), (se, ssl)]
            kh_rd[cs] = [(ve, ve.n), (po, po.n)]
            vb_rd[cs] = [(pe, pe.n)]
            qb_rd[cs] = [(ve, ve.n)]
            out_q.append((c, ovv))
            if len(out_q) > (1 if c < 7 else 0):
                flush_out(out_q.pop(0))
        while out_q:
            flush_out(out_q.pop(0))
        for ls in range(4):
            x3_dma[ls] = p.dma("sync", x3buf[r0 + ls * 128:r0 + (ls + 1) * 128, :], x3t[:, ls, :], rd,
                               waits=[(ve, ve.n)])
        if fuse5:
            for ls in range(4):
                si = t * 4 + ls
                p.emit("vector", lambda e, ls=ls, si=si: e.reduce_sum(
                    stS[:, si:si + 1], x3t[:, ls, :], AX.X), sem=ve)
                q1 = p.emit("scalar", lambda e, ls=ls, si=si: e.activation(
                    guard0[:, :], x3t[:, ls, 0:512], AF.Square, accum_out=stQa[:, si:si + 1]),
                    sem=se, waits=[(ve, avl[ls])])
                q2 = p.emit("scalar", lambda e, ls=ls, si=si: e.activation(
                    guard0[:, :], x3t[:, ls, 512:1024], AF.Square, accum_out=stQb[:, si:si + 1]),
                    sem=se)
                p.emit("vector", lambda e, ls=ls, si=si: e.reduce_max(
                    stMX[:, si:si + 1], x3t[:, ls, :], AX.X), sem=ve)
                mn = p.emit("vector", lambda e, ls=ls, si=si: e.tensor_reduce(
                    stMN[:, si:si + 1], x3t[:, ls, :], AX.X, ALU.min), sem=ve)
                st_g[ls] = [(se, q2)]
    if fuse5:
        dd = 1.0 / D
        t1v = p.emit("vector", lambda e: e.tensor_scalar_mul(mu5[:], stS[:], dd), sem=ve)
        t2v = p.emit("vector", lambda e: e.tensor_add(tmpc5[:], stQa[:], stQb[:]), sem=ve,
                     waits=[(se, se.n)])
        t3v = p.emit("vector", lambda e: e.tensor_scalar_mul(tmpc5[:], tmpc5[:], dd), sem=ve,
                     waits=[(ve, t2v)])
        t4v = p.emit("vector", lambda e: e.tensor_mul(fm5[:], mu5[:], mu5[:]), sem=ve,
                     waits=[(ve, t1v), (ve, t3v)])
        t5v = p.emit("vector", lambda e: e.tensor_sub(tmpc5[:], tmpc5[:], fm5[:]), sem=ve,
                     waits=[(ve, t4v)])
        sqr = p.emit("scalar", lambda e: e.activation(tmpc5[:], tmpc5[:], AF.Sqrt,
                                                      bias=sb["fc"][:, 0:1]),
                     sem=se, waits=[(ve, t5v)])
        t6v = p.emit("vector", lambda e: e.reciprocal(rstd5[:], tmpc5[:]), sem=ve,
                     waits=[(se, sqr)])
        d1 = p.emit("vector", lambda e: e.tensor_sub(fm5[:], stMX[:], mu5[:]), sem=ve,
                    waits=[(ve, t6v)])
        d2 = p.emit("vector", lambda e: e.tensor_sub(stQa[:], mu5[:], stMN[:]), sem=ve,
                    waits=[(ve, d1)])
        mxv = p.emit("vector", lambda e: e.tensor_tensor(fm5[:], fm5[:], stQa[:], ALU.max),
                     sem=ve, waits=[(ve, d2)])
        amv = p.emit("vector", lambda e: e.tensor_mul(fm5[:], fm5[:], rstd5[:]), sem=ve,
                     waits=[(ve, mxv)])
        acv = p.emit("vector", lambda e: e.reduce_max(amc5[:, 0:1], fm5[:, :], AX.X), sem=ve,
                     waits=[(ve, amv)])
        atp = p.emit("tensor", lambda e: e.transpose(pd2[0:1, 0:128], amc5[:, 0:1],
                                                     sb["id_f32"][:, :]),
                     sem=pe, waits=[(ve, acv), (pe, pe.n)])
        amr = p.emit("vector", lambda e: e.reduce_max(scal5b[0:1, 0:1], pd2[0:1, 0:128], AX.X),
                     sem=ve, waits=[(pe, atp)])
        p.dma("sync", io["musd"][:, 0:32], mu5[:, :], p.od, waits=[(ve, t1v)])
        p.dma("sync", io["musd"][:, 32:64], rstd5[:, :], p.od, waits=[(ve, t6v)])
        if io.get("famx"):
            dm5 = p.dma("gpsimd", io["amx_in"], scal5b[0:1, 0:1], gd, waits=[(ve, amr)])
            p.emit("gpsimd", lambda e: e.collective_compute(
                "AllReduce", ALU.max, replica_groups=[list(range(NCORES))],
                ins=[io["amx_in_t"].ap().opt()], outs=[io["amx_out_t"].ap().opt()]),
                sem=p.cc, waits=[(gd, dm5)])
        else:
            p.dma("gpsimd", io["swd"][0:1, 1:2], scal5b[0:1, 0:1], gd, waits=[(ve, amr)])
    if io.get("dump_S"):
        for ci in range(64):
            p.dma("sync", io["sdump"][ci, :, :], S[:, ci, :], rd,
                  waits=[(se, se.n), (ve, ve.n)])
    if _ser:
        p.emit = _emit0



def build_phase5(p, io, E, fused):
    nc = p.nc
    x3buf, yout = io["x3buf"], io["yout"]
    sb = io["sb"]
    sd, gd, rd, pe, ve, se, po, cc = p.sd, p.gd, p.rd, p.pe, p.ve, p.se, p.po, p.cc

    wq8 = E(nc.sbuf_tensor("p5_wq8", [128, 8, 1024], FP8))
    xq8 = E(nc.sbuf_tensor("p5_xq8", [128, 8, 512], FP8))
    xrb = E(nc.sbuf_tensor("p5_xrb", [128, 4, 1024], F32))
    xn5 = E(nc.sbuf_tensor("p5_xn5", [128, 2, 1024], F32))
    tsc = E(nc.sbuf_tensor("p5_t", [128, 2, 512], F32))
    ytile = E(nc.sbuf_tensor("p5_y", [128, 4, 1024], F32))
    bias_sb = E(nc.sbuf_tensor("p5_bias", [128, 8], F32))
    mu5 = E(nc.sbuf_tensor("p5_mu", [128, 32], F32))
    rstd5 = E(nc.sbuf_tensor("p5_rstd", [128, 32], F32))
    scal = E(nc.sbuf_tensor("p5_scal", [1, 8], F32))
    scb = E(nc.sbuf_tensor("p5_scb", [128, 2], F32))

    pa0 = E(nc.psum_tensor("p5_pa0", [128, 512], F32))
    pa1 = E(nc.psum_tensor("p5_pa1", [128, 512], F32))
    po0 = E(nc.psum_tensor("p5_po0", [128, 512], F32))
    po1 = E(nc.psum_tensor("p5_po1", [128, 512], F32))

    p.dma("sync", bias_sb[:, :], io["bias_d"], sd)
    w8ld = p.dma("sync", wq8[:, :, :], io["wq8d"], sd)
    muld = p.dma("sync", mu5[:, :], io["musd"][:, 0:32], sd)
    rsld = p.dma("sync", rstd5[:, :], io["musd"][:, 32:64], sd)
    swld = p.dma("sync", scal[0:1, 3:4], io["swd"][0:1, 0:1], sd)
    if fused:
        am3 = p.dma("sync", scal[0:1, 0:1], io["amx_out"], sd)
    else:
        am3 = p.dma("sync", scal[0:1, 0:1], io["swd"][0:1, 1:2], sd)

    c1 = p.emit("vector", lambda e: e.tensor_scalar_max(scal[:, 0:1], scal[:, 0:1], 1e-12), sem=ve,
                waits=[(sd, am3)])
    c2 = p.emit("vector", lambda e: e.reciprocal(scal[:, 2:3], scal[:, 0:1]), sem=ve, waits=[(ve, c1)])
    c3 = p.emit("vector", lambda e: e.tensor_scalar_mul(scal[:, 2:3], scal[:, 2:3], 224.0), sem=ve,
                waits=[(ve, c2)])
    c6 = p.emit("vector", lambda e: e.tensor_tensor(scal[:, 4:5], scal[:, 2:3], scal[:, 3:4], ALU.mult),
                sem=ve, waits=[(ve, c3), (sd, swld)])
    c7 = p.emit("vector", lambda e: e.reciprocal(scal[:, 4:5], scal[:, 4:5]), sem=ve, waits=[(ve, c6)])
    b1 = p.emit("tensor", lambda e: e.matmul(po0[:, 0:2], sb["ones_row"][:, :],
                                             scal[:, 2:5:2], start=True, stop=True),
                sem=pe, waits=[(ve, c7)])
    b2 = p.emit("vector", lambda e: e.tensor_copy(scb[:, :], po0[:, 0:2]), sem=ve, waits=[(pe, b1)])

    po_guard = {id(po0): [(ve, b2)], id(po1): []}
    pa_guard = {id(pa0): [], id(pa1): []}
    y_dma = [0, 0, 0, 0]
    for rg in range(8):
        r0 = rg * 512
        xld = []
        for ls in range(4):
            ld = p.dma("sync", xrb[:, ls, :], x3buf[r0 + ls * 128:r0 + (ls + 1) * 128, :], sd,
                       waits=[(ve, ve.n)])
            xld.append(ld)
            nv = p.emit("vector", lambda e, ls=ls, si=rg * 4 + ls: e.tensor_scalar(
                xn5[:, ls % 2, :], xrb[:, ls, :], mu5[:, si:si + 1], rstd5[:, si:si + 1],
                ALU.subtract, ALU.mult), sem=ve,
                waits=[(sd, ld), (sd, muld), (sd, rsld), (pe, pe.n)])
            for c in range(8):
                pob = (po0, po1)[c % 2]
                tv = p.emit("tensor", lambda e, ls=ls, c=c, pob=pob: e.transpose(
                    pob[:, 0:128], xn5[:, ls % 2, c * 128:(c + 1) * 128], sb["id_f32"][:, :]),
                    sem=pe, waits=[(ve, nv)] + po_guard[id(pob)])
                qv = p.emit("vector", lambda e, ls=ls, c=c, pob=pob: e.tensor_scalar_mul(
                    xq8[:, c, ls * 128:(ls + 1) * 128], pob[:, 0:128], scb[:, 0:1]),
                    sem=ve, waits=[(pe, tv), (ve, b2)])
                po_guard[id(pob)] = [(ve, qv)]
        q_done = ve.n
        for m in range(8):
            pa = (pa0, pa1)[m % 2]
            for kk in range(8):
                w = []
                if kk == 0:
                    w = pa_guard[id(pa)] + [(ve, q_done)]
                    if rg == 0 and m == 0:
                        w.append((sd, w8ld))
                p.emit("tensor", lambda e, m=m, kk=kk, pa=pa: e.matmul(
                    pa[:, :], wq8[:, kk, m * 128:(m + 1) * 128], xq8[:, kk, :],
                    start=(kk == 0), stop=(kk == 7)), sem=pe, waits=w)
            mm = pe.n
            ts = m % 2
            sv = p.emit("vector", lambda e, m=m, pa=pa, ts=ts: e.tensor_scalar(
                tsc[:, ts, :], pa[:, :], scb[:, 1:2], bias_sb[:, m:m + 1], ALU.mult, ALU.add),
                sem=ve, waits=[(pe, mm), (pe, pe.n), (ve, ve.n)])
            pa_guard[id(pa)] = [(ve, sv)]
            for wq_ in range(4):
                ls = wq_
                pob = (po0, po1)[wq_ % 2]
                tv2 = p.emit("tensor", lambda e, ts=ts, wq_=wq_, pob=pob: e.transpose(
                    pob[:, 0:128], tsc[:, ts, wq_ * 128:(wq_ + 1) * 128], sb["id_f32"][:, :]),
                    sem=pe, waits=[(ve, sv)] + po_guard[id(pob)])
                av = p.emit("vector", lambda e, ls=ls, m=m, pob=pob: e.tensor_tensor(
                    ytile[:, ls, m * 128:(m + 1) * 128], xrb[:, ls, m * 128:(m + 1) * 128],
                    pob[:, 0:128], ALU.add), sem=ve,
                    waits=[(pe, tv2), (p.od, y_dma[ls])])
                po_guard[id(pob)] = [(ve, av)]
        for ls in range(4):
            y_dma[ls] = p.dma("sync", yout[r0 + ls * 128:r0 + (ls + 1) * 128, :], ytile[:, ls, :],
                              p.od, waits=[(ve, ve.n)])



def build_nc(phases=("p1",), dev_out=(), l1_external=False, fused_amax=True, dump_S=False):
    nc = bass.Bass("TRN2", target_bir_lowering=False, debug=False)
    ctx = contextlib.ExitStack()
    io = {}
    if "p4a" in phases and "p4b" in phases and "p5" in phases:
        io["fuse5"] = True
        io["famx"] = fused_amax

    def dram(name, shape, dtype, kind="Internal"):
        t = nc.dram_tensor(name, shape, dtype, kind=kind)
        io[name] = t.ap()
        io[name + "_t"] = t

    has_l0 = "p1" in phases or "p2" in phases
    has_l1 = "p4a" in phases or "p4b" in phases or "p5" in phases

    for n, sh, dt in (("pshift_d", [128, 128], BF16), ("tri_d", [128, 128], F32),
                      ("id_bf_d", [128, 128], BF16), ("id_f32_d", [128, 128], F32),
                      ("ones_row_d", [1, 128], F32), ("ones_col_bf_d", [128, 1], BF16),
                      ("lnw_d", [128, 8], F32), ("lnb_d", [128, 8], F32),
                      ("fc_d", [128, 4], F32), ("sel4_d", [4, 4, 128], F32)):
        dram(n, sh, dt, kind="ExternalInput")

    if has_l0:
        dram("xin", [SEQ, D], F32, kind="ExternalInput")
        dram("xinT", [D, SEQ], F32, kind="ExternalInput")
        for n, sh in (("Wq", [D, D]), ("Wk", [D, D]), ("Wv", [D, D]), ("Wo", [D, D]),
                      ("Wg1", [D, 64]), ("Wg2", [64, D])):
            dram(n, sh, F32, kind="ExternalInput")
        dram("cosd", [128, SEQ], BF16, kind="ExternalInput")
        dram("sind", [128, SEQ], BF16, kind="ExternalInput")
        dram("resid", [SEQ, D], F32,
             kind="ExternalOutput" if "resid" in dev_out else "Internal")
        dram("kspill", [8, 128, SEQ], BF16)
        dram("vspill", [8, 128, 32, 128], BF16)
        dram("w12P", [32, 16, 128, 128], F32, kind="ExternalInput")
        dram("h1P", [32, 32, 128, 128], F32, kind="ExternalInput")
        dram("h2P", [32, 32, 128, 128], F32, kind="ExternalInput")
        dram("w3P", [8, 32, 128, 128], F32, kind="ExternalInput")
        for n in ("s1_d", "s2_d", "hs1_d", "hs2_d"):
            dram(n, [128, 32], F32, kind="ExternalInput")
        dram("s3_d", [128, 8], F32, kind="ExternalInput")

    if has_l1:
        for n in ("wkT", "wvT", "wqT", "wfT"):
            dram(n, [D, D], F32, kind="ExternalInput")
        dram("bias_d", [128, 8], F32, kind="ExternalInput")
        dram("posblk_d", [128, 128], F32, kind="ExternalInput")
        dram("selv_d", [8, 128, 128], BF16, kind="ExternalInput")
        dram("selq_d", [8, 128, 128], BF16, kind="ExternalInput")
        dram("negblk_d", [128, 128], BF16, kind="ExternalInput")
        dram("khsp", [8, 8, 128, 512], F32,
             kind="ExternalOutput" if "khsp" in dev_out else "Internal")
        dram("vsp", [8, 8, 128, 512], BF16,
             kind="ExternalOutput" if "vsp" in dev_out else "Internal")
        dram("qsp", [8, 8, 128, 512], BF16,
             kind="ExternalOutput" if "qsp" in dev_out else "Internal")
        dram("x3buf", [SEQ, D], F32,
             kind="ExternalOutput" if "x3buf" in dev_out else "Internal")
        dram("sdump", [64, 128, 512], F32,
             kind="ExternalOutput" if "sdump" in dev_out else "Internal")
        dram("amx_in", [1, 1], F32)
        dram("amx_out", [1, 1], F32)
        dram("wq8d", [128, 8, 1024], FP8)
        dram("musd", [128, 64], F32)
        dram("swd", [1, 4], F32)
        dram("yout", [SEQ, D], F32,
             kind="ExternalOutput" if "yout" in dev_out else "Internal")
        if l1_external:
            dram("xl1", [SEQ, D], F32, kind="ExternalInput")
            io["l1in"] = io["xl1"]
        else:
            dram("a2aout", [SEQ, D], F32)
            io["l1in"] = io["a2aout"]

    with ctx:
        E = ctx.enter_context
        p = Prog(nc, ctx)
        sb = {}
        sb["id_bf"] = E(nc.sbuf_tensor("c_idbf", [128, 128], BF16))
        sb["pshift"] = E(nc.sbuf_tensor("c_pshift", [128, 128], BF16))
        sb["tri"] = E(nc.sbuf_tensor("c_tri", [128, 128], F32))
        sb["ones_row"] = E(nc.sbuf_tensor("c_onesrow", [1, 128], F32))
        sb["ones_col_bf"] = E(nc.sbuf_tensor("c_onescol", [128, 1], BF16))
        sb["lnw"] = E(nc.sbuf_tensor("c_lnw", [128, 8], F32))
        sb["lnb"] = E(nc.sbuf_tensor("c_lnb", [128, 8], F32))
        sb["fc"] = E(nc.sbuf_tensor("c_fc", [128, 4], F32))
        sb["id_f32"] = E(nc.sbuf_tensor("c_idf32", [128, 128], F32))
        sb["sel4"] = E(nc.sbuf_tensor("c_sel4", [4, 4, 128], F32))
        io["sb"] = sb
        for name, d in (("id_bf", "id_bf_d"), ("pshift", "pshift_d"), ("tri", "tri_d"),
                        ("ones_row", "ones_row_d"), ("ones_col_bf", "ones_col_bf_d"),
                        ("lnw", "lnw_d"), ("lnb", "lnb_d"), ("fc", "fc_d"),
                        ("id_f32", "id_f32_d"), ("sel4", "sel4_d")):
            p.dma("sync", sb[name][tuple([slice(None)] * 2)], io[d], p.sd)

        if "p1" in phases:
            with contextlib.ExitStack() as phase_ctx:
                build_phase1(p, io, phase_ctx.enter_context)
            p.barrier()

        if "p2" in phases:
            with contextlib.ExitStack() as phase_ctx:
                build_phase2(p, io, phase_ctx.enter_context)
            p.barrier()

        if "p3" in phases:
            ccv = p.emit("gpsimd", lambda e: e.collective_compute(
                "AllToAll", ALU.bypass, replica_groups=[list(range(NCORES))],
                ins=[io["resid_t"].ap().opt()], outs=[io["a2aout_t"].ap().opt()]),
                sem=p.cc, waits=p.wait_all())
            p.barrier()

        if "p4a" in phases:
            with contextlib.ExitStack() as phase_ctx:
                build_phase4a(p, io, phase_ctx.enter_context)
            p.barrier()

        io["dump_S"] = dump_S
        if "p4b" in phases:
            with contextlib.ExitStack() as phase_ctx:
                build_phase4b(p, io, phase_ctx.enter_context)
            p.barrier()

        if "p5" in phases:
            with contextlib.ExitStack() as phase_ctx:
                build_phase5(p, io, phase_ctx.enter_context, fused_amax)
            p.barrier()

        vals = tuple(p.wait_all())
        with nc.Block() as block:
            p.realize(block)

            @block.sync
            def _(e):
                for sm, v in vals:
                    e.wait_ge(sm.h, v)
    return nc


def make_inmaps(inputs, consts=None, weights=None, l0=True, l1=True, xl1=None):
    consts = consts or host_consts()
    weights = weights or host_weights(inputs)
    base = {
        "pshift_d": consts["pshift"], "tri_d": consts["tri"], "id_bf_d": consts["id_bf"],
        "id_f32_d": consts["id_f32"], "ones_row_d": consts["ones_row"],
        "ones_col_bf_d": consts["ones_col_bf"], "lnw_d": weights["lnw"],
        "lnb_d": weights["lnb"], "fc_d": consts["fconsts"], "sel4_d": consts["sel4"],
    }
    if l0:
        base.update({
            "Wq": weights["L0_attn_Wq"], "Wk": weights["L0_attn_Wk"], "Wv": weights["L0_attn_Wv"],
            "Wo": weights["L0_attn_Wo"], "Wg1": weights["L0_attn_Wg1"], "Wg2": weights["L0_attn_Wg2"],
            "cosd": consts["cosd"], "sind": consts["sind"],
            "w12P": weights["w12P"], "h1P": weights["h1P"],
            "h2P": weights["h2P"], "w3P": weights["w3P"],
            "s1_d": weights["s1"], "s2_d": weights["s2"], "hs1_d": weights["hs1"],
            "hs2_d": weights["hs2"], "s3_d": weights["s3"],
        })
    if l1:
        base.update({
            "wkT": weights["wkT"], "wvT": weights["wvT"], "wqT": weights["wqT"],
            "wfT": weights["wfT"], "bias_d": weights["bias"],
            "posblk_d": consts["posblk"], "selv_d": consts["selv"], "selq_d": consts["selq"],
            "negblk_d": consts["negblk"],
        })
    in_maps = []
    x = np.asarray(inputs["L0_attn_x"], dtype=np.float32) if l0 else None
    for c in range(NCORES):
        m = dict(base)
        if l0:
            m["xin"] = np.ascontiguousarray(x[c])
            m["xinT"] = np.ascontiguousarray(x[c].T)
        if xl1 is not None:
            m["xl1"] = np.ascontiguousarray(xl1[c])
        in_maps.append(m)
    return in_maps


def kernel(**inputs):
    consts = host_consts()
    weights = host_weights(inputs)
    with allow_ici_all_to_all():
        nc = build_nc(phases=("p1", "p2", "p3", "p4a", "p4b", "p5"), dev_out=("yout",),
                      l1_external=False)
        in_maps = make_inmaps(inputs, consts, weights, l0=True, l1=True)
        res = run_bass_kernel_spmd(nc, in_maps, core_ids=list(range(NCORES)))
    out = np.empty((NCORES, SEQ, D), dtype=np.float32)
    for c in range(NCORES):
        out[:, c * LANES:(c + 1) * LANES, :] = res.results[c]["yout"].reshape(NCORES, LANES, D)
    return out
